# version 01
# baseline (speedup 1.0000x reference)
import numpy as np
import ml_dtypes
import concourse.bass as bass
import concourse.mybir as mybir
from concourse.bass_utils import run_bass_kernel_spmd
from contextlib import ExitStack

F32 = mybir.dt.float32
BF16 = mybir.dt.bfloat16
I32 = mybir.dt.int32
AF = mybir.ActivationFunctionType
ALU = mybir.AluOpType
AX = mybir.AxisListType
BF = ml_dtypes.bfloat16

B = 2
S = 2048
D = 2048
DT = 16
NTC = 4
NTT = 16

ENG = ['pe', 'act', 'dve', 'pool', 'sp']


class Prog:
    def __init__(self, nc):
        self.nc = nc
        self.ops = {e: [] for e in ENG}
        self.cnt = {e: 0 for e in ENG}
        self.dcnt = {e: 0 for e in ENG}
        self.pend = {e: {} for e in ENG}

    def _add(self, eng, fn, is_dma):
        w = self.pend[eng]
        self.pend[eng] = {}
        self.ops[eng].append((w, fn, is_dma))
        if is_dma:
            self.dcnt[eng] += 16
        else:
            self.cnt[eng] += 1

    def pe(self, fn): self._add('pe', fn, False)
    def act(self, fn): self._add('act', fn, False)
    def dve(self, fn): self._add('dve', fn, False)
    def pool(self, fn): self._add('pool', fn, False)
    def ld(self, fn): self._add('sp', fn, True)
    def st(self, fn): self._add('pool', fn, True)

    def bar(self):
        snap = {}
        for e in ENG:
            if self.cnt[e]:
                snap[('c', e)] = self.cnt[e]
            if self.dcnt[e]:
                snap[('d', e)] = self.dcnt[e]
        for e in ENG:
            d = self.pend[e]
            for k, v in snap.items():
                if d.get(k, 0) < v:
                    d[k] = v

    def emit(self, stack):
        nc = self.nc
        self.bar()
        csem = {e: stack.enter_context(nc.semaphore("c_" + e)) for e in ENG}
        dsem = {e: stack.enter_context(nc.semaphore("d_" + e))
                for e in ENG if self.dcnt[e]}

        def runner(e):
            def go(q):
                last = {}
                for w, fn, is_dma in self.ops[e]:
                    for k in sorted(w.keys()):
                        v = w[k]
                        if last.get(k, 0) >= v:
                            continue
                        q.wait_ge(csem[k[1]] if k[0] == 'c' else dsem[k[1]], v)
                        last[k] = v
                    instr = fn(q)
                    if is_dma:
                        instr.then_inc(dsem[e], 16)
                    else:
                        instr.then_inc(csem[e], 1)
                w = self.pend[e]
                for k in sorted(w.keys()):
                    v = w[k]
                    if last.get(k, 0) >= v:
                        continue
                    q.wait_ge(csem[k[1]] if k[0] == 'c' else dsem[k[1]], v)
                    last[k] = v
            return go

        blk = stack.enter_context(nc.Block())
        blk.tensor(runner('pe'))
        blk.scalar(runner('act'))
        blk.vector(runner('dve'))
        blk.gpsimd(runner('pool'))
        blk.sync(runner('sp'))


def host_consts():
    c = {}
    ident = np.eye(128, dtype=np.float32)
    c['c_identf'] = ident
    cb = np.zeros((128, 1024), dtype=np.float32)
    cb[:, 0:128] = ident
    cb[:, 128] = 1.0
    cb[:, 129:257] = 1.0
    p = np.arange(128)[:, None]
    f = np.arange(128)[None, :]
    cb[:, 257:385] = (p <= f).astype(np.float32)
    cb[:, 385:513] = -(f < p).astype(np.float32)
    cb[:, 513:641] = -(f > p).astype(np.float32)
    c['c_constb'] = cb.astype(BF)
    c['c_ut16'] = ((p <= f).astype(np.float32) / 16.0)
    dm = np.zeros((128, 4 * 512), dtype=np.float32)
    qf = np.arange(512)[None, :]
    for j in range(4):
        dm[:, j * 512:(j + 1) * 512] = (qf >= (j * 128 + p)).astype(np.float32)
    c['c_dmask'] = dm.astype(BF)
    hd = 64
    inv = 1.0 / 10000.0 ** (np.arange(0, hd, 2) / hd)
    fr = np.outer(np.arange(S), inv)
    c['c_cosrep'] = np.tile(np.cos(fr), (1, 8)).astype(np.float32).astype(BF)
    c['c_sinrep'] = np.tile(np.sin(fr), (1, 8)).astype(np.float32).astype(BF)
    c['c_scal'] = np.tile(np.array([[1e-5, 1e-6, 0.125, 1.0, 0.0, 0.0, 0.0,
                           0.0]], dtype=np.float32), (128, 1))
    return c


class Ctx:
    pass


def build_program(parts, first, last):
    nc = bass.Bass("TRN2", target_bir_lowering=False)
    stack = ExitStack()
    t = Ctx()
    t.nc = nc

    def dram(name, shape, dtype, kind):
        return nc.dram_tensor(name, list(shape), dtype, kind=kind).ap()

    t.inp = {}
    if first:
        t.inp['xin'] = dram('xin', (B, S, D), F32, 'ExternalInput')
    else:
        t.inp['xfm_in'] = dram('xfm_in', (B, DT, 128, S), BF16, 'ExternalInput')
    if last:
        t.inp['y'] = dram('y', (B, S, D), F32, 'ExternalOutput')
    else:
        t.inp['xfm_out'] = dram('xfm_out', (B, DT, 128, S), BF16, 'ExternalOutput')

    wnames = {
        'l0a': [('L0_attn_wq', (2048, 2048), F32), ('L0_attn_wk', (512, 2048), F32),
                ('L0_attn_wv', (512, 2048), F32), ('L0_attn_wo', (2048, 2048), F32)],
        'l0f': [('L0_ffn_Wr', (8, 2048), F32), ('L0_ffn_Wg', (8, 2048, 1024), F32),
                ('L0_ffn_Wu', (8, 2048, 1024), F32), ('L0_ffn_Wd', (8, 1024, 2048), F32),
                ('L0_ffn_Wg_s', (2048, 1024), F32), ('L0_ffn_Wu_s', (2048, 1024), F32),
                ('L0_ffn_Wd_s', (1024, 2048), F32)],
        'l1a': [('L1_attn_qkv_w', (3072, 2048), F32), ('L1_attn_g_w', (2048, 2048), F32),
                ('L1_attn_b_w', (16, 2048), F32), ('L1_attn_gk_w', (16, 2048), F32),
                ('L1_attn_conv_w', (3072, 4), F32), ('L1_attn_out_norm_w', (128,), F32),
                ('L1_attn_o_w', (2048, 2048), F32)],
        'l1f': [('L1_ffn_lut', (16,), F32), ('L1_ffn_gate_idx', (5504, 2048), I32),
                ('L1_ffn_up_idx', (5504, 2048), I32),
                ('L1_ffn_down_idx', (2048, 5504), I32)],
    }
    for pt in parts:
        for nm, shp, dt in wnames[pt]:
            t.inp[nm] = dram(nm, shp, dt, 'ExternalInput')

    t.chost = host_consts()
    for nm, arr in t.chost.items():
        t.inp[nm] = dram(nm, arr.shape, BF16 if arr.dtype == BF else F32,
                         'ExternalInput')

    internal = {}
    if first and last:
        internal['s_xfm'] = dram('s_xfm', (B, DT, 128, S), BF16, 'Internal')
    if 'l0a' in parts:
        internal['s_qt'] = dram('s_qt', (2048, 2048), BF16, 'Internal')
    if 'l1a' in parts:
        internal['s_g'] = dram('s_g', (2048, 2048), BF16, 'Internal')
        internal['s_ot'] = dram('s_ot', (2048, 2048), BF16, 'Internal')
        internal['s_qkv'] = dram('s_qkv', (B, 24, 128, S), BF16, 'Internal')
    if 'l0f' in parts:
        internal['s_acc'] = dram('s_acc', (B, DT, 128, S), BF16, 'Internal')
    if 'l1f' in parts:
        internal['s_h'] = dram('s_h', (43, 128, B * S), BF16, 'Internal')
        internal['s_wdt'] = dram('s_wdt', (16, 128, 5504), BF16, 'Internal')
        internal['s_wgt'] = dram('s_wgt', (43, 128, 2048), BF16, 'Internal')
        internal['s_wut'] = dram('s_wut', (43, 128, 2048), BF16, 'Internal')
    t.internal = internal

    if first and last:
        t.xfm_src = internal['s_xfm']
        t.xfm_dst = internal['s_xfm']
    elif first:
        t.xfm_src = t.inp['xfm_out']
        t.xfm_dst = t.inp['xfm_out']
    elif last:
        t.xfm_src = t.inp['xfm_in']
        t.xfm_dst = None
    else:
        t.xfm_src = t.inp['xfm_in']
        t.xfm_dst = t.inp['xfm_out']

    sb = lambda nm, shp, dt: stack.enter_context(nc.sbuf_tensor(nm, list(shp), dt))
    t.BG = sb('BG', (128, 69600), BF16)
    t.FS = sb('FS', (128, 5504), F32)
    t.CCA = sb('CCA', (128, 800), F32)
    t.BGL = sb('BGL', (16, 4096), F32)
    t.RW = sb('RW', (1, 2560), F32)
    t.SM = sb('SM', (128, 1024), F32)
    t.CB = sb('CB', (128, 1024), BF16)
    t.UT = sb('UT', (128, 128), F32)
    t.IDF = sb('IDF', (128, 128), F32)
    t.I3 = sb('I3', (128, 2048), I32)
    t.CSC = sb('CSC', (128, 8), F32)
    for i, v in enumerate([1e-5, 1e-6, 0.125, 1.0]):
        nc.const_aps.register(F32, v, t.CSC[:, i:i + 1])
    ps = lambda nm, shp, dt: stack.enter_context(nc.psum_tensor(nm, list(shp), dt))
    t.PS = [ps('PS%d' % i, (128, 512), F32) for i in range(4)]
    t.PR = ps('PR', (128, 512), F32)
    t.PB = ps('PB', (128, 512), F32)
    t.PT2 = [ps('PTa', (128, 512), BF16), ps('PTb', (128, 512), BF16)]

    p = Prog(nc)
    t.p = p

    t.ident_bf = t.CB[:, 0:128]
    t.ones_col = t.CB[:, 128:129]
    t.ones_row = t.CB[0:1, 129:257]
    t.ones_full = t.CB[:, 129:257]
    t.utri1 = t.CB[:, 257:385]
    t.negtril = t.CB[:, 385:513]
    t.negtriu = t.CB[:, 513:641]

    def _c(q, a, b):
        return q.dma_start(out=a, in_=b)
    p.ld(lambda q, a=t.CB[:], b=t.inp['c_constb'][:]: _c(q, a, b))
    p.ld(lambda q, a=t.UT[:], b=t.inp['c_ut16'][:]: _c(q, a, b))
    p.ld(lambda q, a=t.IDF[:], b=t.inp['c_identf'][:]: _c(q, a, b))
    p.ld(lambda q, a=t.CSC[:], b=t.inp['c_scal'][:]: _c(q, a, b))
    p.bar()

    if first:
        build_convert_in(t)
    for pt in parts:
        {'l0a': build_l0a, 'l0f': build_l0f, 'l1a': build_l1a,
         'l1f': lambda tt: build_l1f(tt, final=last)}[pt](t)
    if last and 'l1f' not in parts:
        build_convert_out(t)

    p.emit(stack)
    return nc, stack, t



def tr_copy(t, src_ap, dst_ap, pb_idx):
    p = t.p
    pp, ff = src_ap.shape[0], src_ap.shape[1]
    pso = t.PT2[pb_idx][0:ff, 0:pp]
    p.pe(lambda q, o=pso, i=src_ap, idn=t.ident_bf[0:pp, 0:pp]:
         q.transpose(o, i, idn))
    p.bar()
    p.act(lambda q, o=dst_ap, i=pso: q.copy(out=o, in_=i))


def load_wT(t, w_rows_ap, wt_off, stride=128):
    p = t.p
    p.ld(lambda q, o=t.FS[:, 0:2048], i=w_rows_ap: q.dma_start(out=o, in_=i))
    p.bar()
    cb = t.BG[:, 67000:69048]
    p.dve(lambda q, o=cb, i=t.FS[:, 0:2048]: q.tensor_copy(out=o, in_=i))
    p.bar()
    for dt in range(16):
        tr_copy(t, cb[:, dt * 128:(dt + 1) * 128],
                t.BG[:, wt_off + dt * stride: wt_off + dt * stride + 128], dt % 2)
    p.bar()


def make_xh(t, b, xh_off, sq_off):
    p = t.p
    src = t.xfm_src
    for tc in range(NTC):
        xh = []
        for dt in range(DT):
            xh.append(t.BG[:, xh_off + dt * 2048 + tc * 512:
                           xh_off + dt * 2048 + tc * 512 + 512])
        for dt in range(DT):
            p.ld(lambda q, o=xh[dt], i=src[b, dt, :, tc * 512:(tc + 1) * 512]:
                 q.dma_start(out=o, in_=i))
        p.bar()
        for dt in range(DT):
            sq = t.BG[:, sq_off + dt * 512: sq_off + (dt + 1) * 512]
            p.act(lambda q, o=sq, i=xh[dt]:
                  q.activation(out=o, in_=i, func=AF.Square))
        p.bar()
        for dt in range(DT):
            p.pe(lambda q, o=t.PR[0:1, 0:512], i=xh[dt], s=(dt == 0), e=(dt == 15):
                 q.matmul(o, t.ones_col, i, start=s, stop=e))
        for dt in range(DT):
            sq = t.BG[:, sq_off + dt * 512: sq_off + (dt + 1) * 512]
            p.pe(lambda q, o=t.PS[2][0:1, 0:512], i=sq, s=(dt == 0), e=(dt == 15):
                 q.matmul(o, t.ones_col, i, start=s, stop=e))
        p.bar()
        m = t.RW[0:1, 0:512]
        v2 = t.RW[0:1, 512:1024]
        r = t.RW[0:1, 1024:1536]
        mr = t.RW[0:1, 1536:2048]
        p.dve(lambda q, o=m: q.tensor_scalar(out=o, in0=t.PR[0:1, 0:512],
              scalar1=1.0 / D, scalar2=None, op0=ALU.mult))
        p.dve(lambda q, o=v2: q.tensor_scalar(out=o, in0=t.PS[2][0:1, 0:512],
              scalar1=1.0 / D, scalar2=None, op0=ALU.mult))
        p.bar()
        p.dve(lambda q, o=r, a=m: q.scalar_tensor_tensor(out=o, in0=a, scalar=-1.0,
              in1=a, op0=ALU.mult, op1=ALU.mult))
        p.bar()
        p.dve(lambda q, o=r, a=r, b2=v2: q.tensor_tensor(out=o, in0=a, in1=b2,
              op=ALU.add))
        p.bar()
        p.act(lambda q, o=r, i=r: q.activation(out=o, in_=i, func=AF.Sqrt,
              bias=1e-5))
        p.bar()
        p.dve(lambda q, o=r, i=r: q.reciprocal(out=o, in_=i))
        p.bar()
        p.dve(lambda q, o=mr, a=m, b2=r: q.scalar_tensor_tensor(out=o, in0=a,
              scalar=-1.0, in1=b2, op0=ALU.mult, op1=ALU.mult))
        p.bar()
        rb16 = t.BG[0:1, 69048:69560]
        mrb16 = t.BG[0:1, 66000:66512]
        p.act(lambda q, o=rb16, i=r: q.copy(out=o, in_=i))
        p.act(lambda q, o=mrb16, i=mr: q.copy(out=o, in_=i))
        p.bar()
        p.pe(lambda q, o=t.PB[:, 0:512], i=rb16:
             q.matmul(o, t.ones_row, i, start=True, stop=True))
        p.pe(lambda q, o=t.PS[3][:, 0:512], i=mrb16:
             q.matmul(o, t.ones_row, i, start=True, stop=True))
        p.bar()
        for dt in range(DT):
            p.dve(lambda q, o=xh[dt], i=xh[dt]: q.tensor_tensor(out=o, in0=i,
                  in1=t.PB[:, 0:512], op=ALU.mult))
        p.bar()
        for dt in range(DT):
            p.dve(lambda q, o=xh[dt], i=xh[dt]: q.tensor_tensor(out=o, in0=i,
                  in1=t.PS[3][:, 0:512], op=ALU.add))
        p.bar()


def build_convert_in(t):
    p = t.p
    dst = t.internal['s_xfm'] if 's_xfm' in t.internal else t.inp['xfm_out']
    t.xfm_src = dst
    if t.xfm_dst is not None:
        t.xfm_dst = t.xfm_dst
    x = t.inp['xin']
    for b in range(B):
        for tt in range(NTT):
            p.ld(lambda q, o=t.FS[:, 0:2048], i=x[b, tt * 128:(tt + 1) * 128, :]:
                 q.dma_start(out=o, in_=i))
            p.bar()
            cb = t.BG[:, 67000:69048]
            p.dve(lambda q, o=cb, i=t.FS[:, 0:2048]: q.tensor_copy(out=o, in_=i))
            p.bar()
            for dt in range(DT):
                stg = t.BG[:, 69048 + (dt % 2) * 128: 69048 + (dt % 2) * 128 + 128]
                tr_copy(t, cb[:, dt * 128:(dt + 1) * 128], stg, dt % 2)
                p.bar()
                p.st(lambda q, o=dst[b, dt, :, tt * 128:(tt + 1) * 128], i=stg:
                     q.dma_start(out=o, in_=i))
            p.bar()


def build_convert_out(t):
    p = t.p
    src = t.xfm_src
    y = t.inp['y']
    for b in range(B):
        for dt in range(DT):
            cb = t.BG[:, 0:2048]
            p.ld(lambda q, o=cb, i=src[b, dt, :, :]: q.dma_start(out=o, in_=i))
            p.bar()
            for tt in range(NTT):
                fsb = t.FS[:, 0:2048]
                p.dve(lambda q, o=fsb[:, (tt % 2) * 128:(tt % 2) * 128 + 128],
                      i=cb[:, tt * 128:(tt + 1) * 128]: q.tensor_copy(out=o, in_=i))
                p.bar()
                pso = t.PS[tt % 2][0:128, 0:128]
                p.pe(lambda q, o=pso, i=fsb[:, (tt % 2) * 128:(tt % 2) * 128 + 128]:
                     q.transpose(o, i, t.IDF[:]))
                p.bar()
                stg = t.FS[:, 2048 + (tt % 2) * 128: 2048 + (tt % 2) * 128 + 128]
                p.act(lambda q, o=stg, i=pso: q.copy(out=o, in_=i))
                p.bar()
                p.st(lambda q, o=y[b, tt * 128:(tt + 1) * 128,
                                   dt * 128:(dt + 1) * 128], i=stg:
                     q.dma_start(out=o, in_=i))
            p.bar()



def build_l0f(t):
    p = t.p
    XH = 0
    HT = 32768
    WG = 49152
    WU = 51200
    SG = 53248
    CBB = 53760
    WR = 55808
    RT1 = 55936
    SQo = 57000
    wg = t.inp['L0_ffn_Wg']
    wu = t.inp['L0_ffn_Wu']
    wd = t.inp['L0_ffn_Wd']
    acc = t.internal['s_acc']

    for b in range(B):
        make_xh(t, b, XH, SQo)
        xh = []
        for dt in range(DT):
            row = []
            for tc in range(NTC):
                row.append(t.BG[:, XH + dt * 2048 + tc * 512:
                               XH + dt * 2048 + tc * 512 + 512])
            xh.append(row)
        p.ld(lambda q, o=t.FS[0:8, 0:2048], i=t.inp['L0_ffn_Wr'][:]:
             q.dma_start(out=o, in_=i))
        p.bar()
        cb8 = t.BG[0:8, 67000:69048]
        p.dve(lambda q, o=cb8, i=t.FS[0:8, 0:2048]: q.tensor_copy(out=o, in_=i))
        p.bar()
        for dt in range(16):
            tr_copy(t, cb8[:, dt * 128:(dt + 1) * 128],
                    t.BG[:, WR + dt * 8: WR + dt * 8 + 8], dt % 2)
        p.bar()
        for tc in range(NTC):
            for dt in range(DT):
                p.pe(lambda q, o=t.PS[0][0:8, 0:512],
                     w=t.BG[:, WR + dt * 8: WR + dt * 8 + 8], i=xh[dt][tc],
                     s=(dt == 0), e=(dt == 15):
                     q.matmul(o, w, i, start=s, stop=e))
            p.bar()
            lg = t.FS[0:8, 2048 + tc * 512: 2048 + tc * 512 + 512]
            p.act(lambda q, o=lg, i=t.PS[0][0:8, 0:512]: q.copy(out=o, in_=i))
            p.bar()
        for tt in range(NTT):
            tc, j = divmod(tt, 4)
            srcl = t.FS[0:8, 2048 + tc * 512 + j * 128:
                        2048 + tc * 512 + j * 128 + 128]
            pso = t.PS[1][0:128, 0:8]
            p.pe(lambda q, o=pso, i=srcl: q.transpose(o, i, t.IDF[0:8, 0:8]))
            p.bar()
            lgt = t.SM[:, 368 + tt * 8: 368 + tt * 8 + 8]
            p.act(lambda q, o=lgt, i=pso: q.copy(out=o, in_=i))
            p.bar()
        for tt in range(NTT):
            lgt = t.SM[:, 368 + tt * 8: 368 + tt * 8 + 8]
            m1 = t.SM[:, 496:497]; m2 = t.SM[:, 497:498]
            dcol = t.SM[:, 498:499]; w1 = t.SM[:, 499:500]; w2 = t.SM[:, 500:501]
            msk1 = t.SM[:, 502:510]; mskd = t.SM[:, 510:518]; msk2 = t.SM[:, 518:526]
            p.dve(lambda q, o=m1, i=lgt: q.reduce_max(out=o, in_=i, axis=AX.X))
            p.bar()
            p.dve(lambda q, o=msk1, i=lgt, m=m1: q.tensor_scalar(out=o, in0=i,
                  scalar1=m, scalar2=None, op0=ALU.is_ge))
            p.bar()
            p.dve(lambda q, o=mskd, i=msk1, lg=lgt: q.scalar_tensor_tensor(out=o,
                  in0=i, scalar=-1e30, in1=lg, op0=ALU.mult, op1=ALU.add))
            p.bar()
            p.dve(lambda q, o=m2, i=mskd: q.reduce_max(out=o, in_=i, axis=AX.X))
            p.bar()
            p.dve(lambda q, o=msk2, i=mskd, m=m2: q.tensor_scalar(out=o, in0=i,
                  scalar1=m, scalar2=None, op0=ALU.is_ge))
            p.dve(lambda q, o=dcol, a=m2, b2=m1: q.tensor_tensor(out=o, in0=a,
                  in1=b2, op=ALU.subtract))
            p.bar()
            p.act(lambda q, o=dcol, i=dcol: q.activation(out=o, in_=i, func=AF.Exp))
            p.bar()
            p.dve(lambda q, o=w1, i=dcol: q.tensor_scalar(out=o, in0=i, scalar1=1.0,
                  scalar2=None, op0=ALU.add))
            p.bar()
            p.dve(lambda q, o=w1, i=w1: q.reciprocal(out=o, in_=i))
            p.bar()
            p.dve(lambda q, o=w2, i=w1: q.tensor_scalar(out=o, in0=i, scalar1=-1.0,
                  scalar2=1.0, op0=ALU.mult, op1=ALU.add))
            p.bar()
            comb = t.SM[:, 240 + tt * 8: 240 + tt * 8 + 8]
            p.dve(lambda q, o=comb, i=msk2, w=w2: q.tensor_scalar(out=o, in0=i,
                  scalar1=w, scalar2=None, op0=ALU.mult))
            p.bar()
            p.dve(lambda q, o=comb, i=msk1, w=w1: q.scalar_tensor_tensor(out=o,
                  in0=i, scalar=w, in1=o, op0=ALU.mult, op1=ALU.add))
            p.bar()
        for e in range(9):
            if e < 8:
                for tc in range(NTC):
                    for j in range(4):
                        tt = tc * 4 + j
                        tmp = t.BG[:, 69048:69176]
                        p.dve(lambda q, o=tmp,
                              s=t.SM[:, 240 + tt * 8 + e: 240 + tt * 8 + e + 1]:
                              q.tensor_scalar(out=o, in0=t.ident_bf, scalar1=s,
                                              scalar2=None, op0=ALU.mult))
                        p.bar()
                        p.pe(lambda q, o=t.PB[:, j * 128:(j + 1) * 128], i=tmp:
                             q.matmul(o, t.ones_full, i, start=True, stop=True))
                        p.bar()
                    cbb = t.BG[:, CBB + tc * 512: CBB + (tc + 1) * 512]
                    p.act(lambda q, o=cbb, i=t.PB[:, 0:512]: q.copy(out=o, in_=i))
                    p.bar()
            for it in range(8):
                for half in range(2):
                    if e < 8:
                        srcw = (wg if half == 0 else wu)[e][:, it * 128:(it + 1) * 128]
                    else:
                        srcw = (t.inp['L0_ffn_Wg_s'] if half == 0 else
                                t.inp['L0_ffn_Wu_s'])[:, it * 128:(it + 1) * 128]
                    woff = WG if half == 0 else WU
                    for qtr in range(4):
                        sl = srcw[qtr * 512:(qtr + 1) * 512, :]
                        dst3 = t.FS[:, qtr * 512:(qtr + 1) * 512].rearrange(
                            "p (dt i) -> p dt i", dt=4)
                        p.ld(lambda q, o=dst3,
                             i=sl.rearrange("(dt p) i -> p dt i", p=128):
                             q.dma_start(out=o, in_=i))
                    p.bar()
                    p.dve(lambda q, o=t.BG[:, woff:woff + 2048], i=t.FS[:, 0:2048]:
                          q.tensor_copy(out=o, in_=i))
                    p.bar()
                for tc in range(NTC):
                    for dt in range(DT):
                        p.pe(lambda q, o=t.PS[0][:, 0:512],
                             w=t.BG[:, WG + dt * 128: WG + dt * 128 + 128],
                             i=xh[dt][tc], s=(dt == 0), e2=(dt == 15):
                             q.matmul(o, w, i, start=s, stop=e2))
                    for dt in range(DT):
                        p.pe(lambda q, o=t.PS[1][:, 0:512],
                             w=t.BG[:, WU + dt * 128: WU + dt * 128 + 128],
                             i=xh[dt][tc], s=(dt == 0), e2=(dt == 15):
                             q.matmul(o, w, i, start=s, stop=e2))
                    p.bar()
                    sg = t.BG[:, SG:SG + 512]
                    p.act(lambda q, o=sg: q.activation(out=o, in_=t.PS[0][:, 0:512],
                          func=AF.Silu))
                    p.bar()
                    ht = t.BG[:, HT + it * 2048 + tc * 512:
                              HT + it * 2048 + tc * 512 + 512]
                    p.dve(lambda q, o=ht, a=sg: q.tensor_tensor(out=o, in0=a,
                          in1=t.PS[1][:, 0:512], op=ALU.mult))
                    p.bar()
                    if e < 8:
                        cbb = t.BG[:, CBB + tc * 512: CBB + (tc + 1) * 512]
                        p.dve(lambda q, o=ht, c=cbb: q.tensor_tensor(out=o, in0=o,
                              in1=c, op=ALU.mult))
                        p.bar()
            wda = wd[e] if e < 8 else t.inp['L0_ffn_Wd_s']
            for ot in range(16):
                srcw = wda[:, ot * 128:(ot + 1) * 128]
                for qtr in range(2):
                    sl = srcw[qtr * 512:(qtr + 1) * 512, :]
                    dst3 = t.FS[:, qtr * 512:(qtr + 1) * 512].rearrange(
                        "p (it o) -> p it o", it=4)
                    p.ld(lambda q, o=dst3,
                         i=sl.rearrange("(it p) o -> p it o", p=128):
                         q.dma_start(out=o, in_=i))
                p.bar()
                p.dve(lambda q, o=t.BG[:, WG:WG + 1024], i=t.FS[:, 0:1024]:
                      q.tensor_copy(out=o, in_=i))
                p.bar()
                for tc in range(NTC):
                    for it in range(8):
                        ht = t.BG[:, HT + it * 2048 + tc * 512:
                                  HT + it * 2048 + tc * 512 + 512]
                        p.pe(lambda q, o=t.PS[2][:, 0:512],
                             w=t.BG[:, WG + it * 128: WG + it * 128 + 128], h=ht,
                             s=(it == 0), e2=(it == 7):
                             q.matmul(o, w, h, start=s, stop=e2))
                    p.bar()
                    stg = t.BG[:, RT1:RT1 + 512]
                    if e == 0:
                        p.act(lambda q, o=stg, i=t.PS[2][:, 0:512]:
                              q.copy(out=o, in_=i))
                        p.bar()
                    else:
                        p.ld(lambda q, o=stg,
                             i=acc[b, ot, :, tc * 512:(tc + 1) * 512]:
                             q.dma_start(out=o, in_=i))
                        p.bar()
                        p.dve(lambda q, o=stg: q.tensor_tensor(out=o, in0=o,
                              in1=t.PS[2][:, 0:512], op=ALU.add))
                        p.bar()
                    if e < 8:
                        p.st(lambda q, o=acc[b, ot, :, tc * 512:(tc + 1) * 512],
                             i=stg: q.dma_start(out=o, in_=i))
                        p.bar()
                    else:
                        xo = t.BG[:, RT1 + 512:RT1 + 1024]
                        p.ld(lambda q, o=xo,
                             i=t.xfm_src[b, ot, :, tc * 512:(tc + 1) * 512]:
                             q.dma_start(out=o, in_=i))
                        p.bar()
                        p.dve(lambda q, o=stg, i2=xo: q.tensor_tensor(out=o, in0=o,
                              in1=i2, op=ALU.add))
                        p.bar()
                        p.st(lambda q, o=t.xfm_dst[b, ot, :, tc * 512:(tc + 1) * 512],
                             i=stg: q.dma_start(out=o, in_=i))
                        p.bar()


def build_l0a(t):
    p = t.p
    XH = 0
    OTo = 0
    KTo = 32768
    Vo = 40960
    QSo = 49152
    QBo = 51200
    PTBo = 51712
    OVSo = 52736
    WSo = 53248
    COSo = 40960
    SINo = 45056
    qt = t.internal['s_qt']
    wq = t.inp['L0_attn_wq']; wk = t.inp['L0_attn_wk']
    wv = t.inp['L0_attn_wv']; wo = t.inp['L0_attn_wo']

    for b in range(B):
        make_xh(t, b, XH, 32768)
        xh = []
        for dt in range(DT):
            xh.append(t.BG[:, XH + dt * 2048: XH + (dt + 1) * 2048])
        p.ld(lambda q, o=t.BG[:, COSo:COSo + 4096].rearrange(
             "p (tt c) -> p tt c", tt=16),
             i=t.inp['c_cosrep'].rearrange("(tt p) c -> p tt c", p=128):
             q.dma_start(out=o, in_=i))
        p.ld(lambda q, o=t.BG[:, SINo:SINo + 4096].rearrange(
             "p (tt c) -> p tt c", tt=16),
             i=t.inp['c_sinrep'].rearrange("(tt p) c -> p tt c", p=128):
             q.dma_start(out=o, in_=i))
        p.bar()
        for wap, nocs, kind in ((wq, 4, 'q'), (wk, 1, 'k'), (wv, 1, 'v')):
            for oc in range(nocs):
                for j in range(4):
                    load_wT(t, wap[oc * 512 + j * 128: oc * 512 + (j + 1) * 128, :],
                            WSo + j * 128, stride=512)
                for tt in range(NTT):
                    for dt in range(DT):
                        p.pe(lambda q, o=t.PS[0][:, 0:512],
                             l=xh[dt][:, tt * 128:(tt + 1) * 128],
                             r=t.BG[:, WSo + dt * 512: WSo + dt * 512 + 512],
                             s=(dt == 0), e2=(dt == 15):
                             q.matmul(o, l, r, start=s, stop=e2))
                    p.bar()
                    qb = t.BG[:, QBo:QBo + 512]
                    if kind in ('q', 'k'):
                        cosn = t.BG[:, COSo + tt * 256: COSo + tt * 256 + 256]
                        sinn = t.BG[:, SINo + tt * 256: SINo + tt * 256 + 256]
                        xr = t.PS[0][:, 0:512:2]
                        xi = t.PS[0][:, 1:512:2]
                        t1 = t.FS[:, 0:256]; t2 = t.FS[:, 256:512]
                        t3 = t.FS[:, 512:768]; t4 = t.FS[:, 768:1024]
                        p.dve(lambda q, o=t1, a=cosn, b2=xr: q.tensor_tensor(
                              out=o, in0=a, in1=b2, op=ALU.mult))
                        p.dve(lambda q, o=t2, a=sinn, b2=xi: q.tensor_tensor(
                              out=o, in0=a, in1=b2, op=ALU.mult))
                        p.dve(lambda q, o=t3, a=sinn, b2=xr: q.tensor_tensor(
                              out=o, in0=a, in1=b2, op=ALU.mult))
                        p.dve(lambda q, o=t4, a=cosn, b2=xi: q.tensor_tensor(
                              out=o, in0=a, in1=b2, op=ALU.mult))
                        p.bar()
                        p.dve(lambda q, o=qb[:, 0:512:2], a=t1, b2=t2:
                              q.tensor_tensor(out=o, in0=a, in1=b2,
                                              op=ALU.subtract))
                        p.dve(lambda q, o=qb[:, 1:512:2], a=t3, b2=t4:
                              q.tensor_tensor(out=o, in0=a, in1=b2, op=ALU.add))
                        p.bar()
                        if kind == 'q':
                            for j in range(4):
                                stg = t.BG[:, 66000 + (j % 2) * 128:
                                           66000 + (j % 2) * 128 + 128]
                                tr_copy(t, qb[:, j * 128:(j + 1) * 128], stg, j % 2)
                                p.bar()
                                p.st(lambda q, o=qt[oc * 512 + j * 128:
                                                    oc * 512 + (j + 1) * 128,
                                                    tt * 128:(tt + 1) * 128],
                                     i=stg: q.dma_start(out=o, in_=i))
                            p.bar()
                        else:
                            for j in range(4):
                                dst = t.BG[:, KTo + j * 2048 + tt * 128:
                                           KTo + j * 2048 + tt * 128 + 128]
                                tr_copy(t, qb[:, j * 128:(j + 1) * 128], dst, j % 2)
                            p.bar()
                    else:
                        p.act(lambda q, o=t.BG[:, Vo + tt * 512: Vo + tt * 512 + 512],
                              i=t.PS[0][:, 0:512]: q.copy(out=o, in_=i))
                        p.bar()
        p.ld(lambda q, o=t.BG[:, WSo:WSo + 2048], i=t.inp['c_dmask'][:]:
             q.dma_start(out=o, in_=i))
        p.bar()
        for h in range(32):
            kv = h // 4
            ktile = kv // 2
            base = 64 * (kv % 2)
            qs = t.BG[base:base + 64, QSo:QSo + 2048]
            p.ld(lambda q, o=qs, i=qt[h * 64:h * 64 + 64, :]:
                 q.dma_start(out=o, in_=i))
            p.bar()
            for qc in range(NTC):
                nkt = 4 * qc + 4
                for kt in range(nkt):
                    lk = t.BG[base:base + 64,
                              KTo + ktile * 2048 + kt * 128:
                              KTo + ktile * 2048 + kt * 128 + 128]
                    rq = t.BG[base:base + 64, QSo + qc * 512: QSo + qc * 512 + 512]
                    p.pe(lambda q, o=t.PS[1][:, 0:512], l=lk, r=rq:
                         q.matmul(o, l, r, start=True, stop=True))
                    p.bar()
                    ptb = t.BG[:, PTBo + (kt % 2) * 512: PTBo + (kt % 2) * 512 + 512]
                    diag = kt - 4 * qc
                    if diag >= 0:
                        ef = t.FS[:, 1024:1536]
                        p.act(lambda q, o=ef: q.activation(
                              out=o, in_=t.PS[1][:, 0:512], func=AF.Exp,
                              scale=0.125))
                        p.bar()
                        p.dve(lambda q, o=ptb, a=ef,
                              m=t.BG[:, WSo + diag * 512: WSo + diag * 512 + 512]:
                              q.tensor_tensor(out=o, in0=a, in1=m, op=ALU.mult))
                    else:
                        p.act(lambda q, o=ptb: q.activation(
                              out=o, in_=t.PS[1][:, 0:512], func=AF.Exp,
                              scale=0.125))
                    p.bar()
                    p.pe(lambda q, o=t.PR[0:1, 0:512], i=ptb, s=(kt == 0),
                         e2=(kt == nkt - 1):
                         q.matmul(o, t.ones_col, i, start=s, stop=e2))
                    p.pe(lambda q, o=t.PS[2][0:64, 0:512],
                         l=t.BG[:, Vo + kt * 512 + kv * 64:
                                Vo + kt * 512 + kv * 64 + 64], i=ptb,
                         s=(kt == 0), e2=(kt == nkt - 1):
                         q.matmul(o, l, i, start=s, stop=e2))
                    p.bar()
                rz = t.RW[0:1, 2048:2560]
                p.dve(lambda q, o=rz: q.reciprocal(out=o, in_=t.PR[0:1, 0:512]))
                p.bar()
                rzb = t.BG[0:1, 66000:66512]
                p.act(lambda q, o=rzb, i=rz: q.copy(out=o, in_=i))
                p.bar()
                p.pe(lambda q, o=t.PB[0:64, 0:512], i=rzb:
                     q.matmul(o, t.CB[0:1, 129:193], i, start=True, stop=True))
                ovs = t.BG[0:64, OVSo:OVSo + 512]
                p.act(lambda q, o=ovs, i=t.PS[2][0:64, 0:512]: q.copy(out=o, in_=i))
                p.bar()
                obase = 64 * (h % 2)
                ht2 = h // 2
                odst = t.BG[obase:obase + 64,
                            OTo + ht2 * 2048 + qc * 512:
                            OTo + ht2 * 2048 + qc * 512 + 512]
                p.dve(lambda q, o=odst, a=ovs: q.tensor_tensor(
                      out=o, in0=a, in1=t.PB[0:64, 0:512], op=ALU.mult))
                p.bar()
        for ot in range(16):
            load_wT(t, wo[ot * 128:(ot + 1) * 128, :], WSo, stride=128)
            for tc in range(NTC):
                for hd in range(16):
                    p.pe(lambda q, o=t.PS[3][:, 0:512],
                         l=t.BG[:, WSo + hd * 128: WSo + hd * 128 + 128],
                         r=t.BG[:, OTo + hd * 2048 + tc * 512:
                                OTo + hd * 2048 + tc * 512 + 512],
                         s=(hd == 0), e2=(hd == 15):
                         q.matmul(o, l, r, start=s, stop=e2))
                p.bar()
                xo = t.BG[:, 66000:66512]
                p.ld(lambda q, o=xo, i=t.xfm_src[b, ot, :, tc * 512:(tc + 1) * 512]:
                     q.dma_start(out=o, in_=i))
                p.bar()
                stg = t.BG[:, 66512:67024]
                p.dve(lambda q, o=stg, a=xo: q.tensor_tensor(
                      out=o, in0=a, in1=t.PS[3][:, 0:512], op=ALU.add))
                p.bar()
                p.st(lambda q, o=t.xfm_dst[b, ot, :, tc * 512:(tc + 1) * 512],
                     i=stg: q.dma_start(out=o, in_=i))
                p.bar()


L1A_PH = {'p1', 'p2', 'p3', 'p4', 'p5'}
L1A_NH = 16
L1A_NC = 16
L1A_SEQ = True
L1A_GS = True
L1A_TR = True


def build_l1a(t):
    p = t.p
    qkvw = t.inp['L1_attn_qkv_w']
    gw = t.inp['L1_attn_g_w']
    ow = t.inp['L1_attn_o_w']
    sqkv = t.internal['s_qkv']
    sg = t.internal['s_g']
    sot = t.internal['s_ot']
    p.ld(lambda q, o=t.SM[0:1, 144:272],
         i=t.inp['L1_attn_out_norm_w'].rearrange("(a b) -> a b", a=1):
         q.dma_start(out=o, in_=i))
    p.bar()
    onr = t.BG[0:1, 66000:66128]
    p.act(lambda q, o=onr, i=t.SM[0:1, 144:272]: q.copy(out=o, in_=i))
    p.bar()
    p.pe(lambda q, o=t.PB[:, 0:128], i=onr:
         q.matmul(o, t.ones_row, i, start=True, stop=True))
    p.bar()
    p.act(lambda q, o=t.SM[:, 0:128], i=t.PB[:, 0:128]: q.copy(out=o, in_=i))
    p.bar()
    for ct in range(24):
        p.ld(lambda q, o=t.SM[:, 144 + ct * 4: 144 + (ct + 1) * 4],
             i=t.inp['L1_attn_conv_w'][ct * 128:(ct + 1) * 128, :]:
             q.dma_start(out=o, in_=i))
    p.bar()

    for b in range(B):
        if 'p1' not in L1A_PH:
            break
        make_xh(t, b, 0, 32768)
        xh = []
        for dt in range(DT):
            row = []
            for tc in range(NTC):
                row.append(t.BG[:, dt * 2048 + tc * 512:
                               dt * 2048 + tc * 512 + 512])
            xh.append(row)
        for wnm, outrow in (('L1_attn_b_w', 0), ('L1_attn_gk_w', 2048)):
            p.ld(lambda q, o=t.FS[0:16, 0:2048], i=t.inp[wnm][:]:
                 q.dma_start(out=o, in_=i))
            p.bar()
            cb16 = t.BG[0:16, 67000:69048]
            p.dve(lambda q, o=cb16, i=t.FS[0:16, 0:2048]: q.tensor_copy(out=o, in_=i))
            p.bar()
            for dt in range(16):
                tr_copy(t, cb16[:, dt * 128:(dt + 1) * 128],
                        t.BG[:, 33024 + dt * 16: 33024 + dt * 16 + 16], dt % 2)
            p.bar()
            for tc in range(NTC):
                for dt in range(DT):
                    p.pe(lambda q, o=t.PS[0][0:16, 0:512],
                         l=t.BG[:, 33024 + dt * 16: 33024 + dt * 16 + 16],
                         r=xh[dt][tc], s=(dt == 0), e2=(dt == 15):
                         q.matmul(o, l, r, start=s, stop=e2))
                p.bar()
                dst = t.BGL[0:16, outrow + tc * 512: outrow + (tc + 1) * 512]
                p.act(lambda q, o=dst: q.activation(
                      out=o, in_=t.PS[0][0:16, 0:512], func=AF.Sigmoid))
                p.bar()
                if outrow == 2048:
                    p.act(lambda q, o=dst, i=dst: q.activation(out=o, in_=i,
                          func=AF.Ln))
                    p.bar()
        for oc in range(4):
            for j in range(4):
                load_wT(t, gw[oc * 512 + j * 128: oc * 512 + (j + 1) * 128, :],
                        33280 + j * 128, stride=512)
            for tt in range(NTT):
                for dt in range(DT):
                    p.pe(lambda q, o=t.PS[0][:, 0:512],
                         l=xh[dt][0][:, 0:512].tensor[
                             :, dt * 2048 + tt * 128: dt * 2048 + tt * 128 + 128]
                         if False else t.BG[:, dt * 2048 + tt * 128:
                                            dt * 2048 + tt * 128 + 128],
                         r=t.BG[:, 33280 + dt * 512: 33280 + dt * 512 + 512],
                         s=(dt == 0), e2=(dt == 15):
                         q.matmul(o, l, r, start=s, stop=e2))
                p.bar()
                gb = t.BG[:, 41472:41984]
                p.act(lambda q, o=gb: q.activation(out=o, in_=t.PS[0][:, 0:512],
                      func=AF.Silu))
                p.bar()
                p.st(lambda q, o=sg[tt * 128:(tt + 1) * 128,
                                    oc * 512:(oc + 1) * 512], i=gb:
                     q.dma_start(out=o, in_=i))
                p.bar()
        for ct in range(24):
            load_wT(t, qkvw[ct * 128:(ct + 1) * 128, :], 33280, stride=128)
            for tc in range(NTC):
                for dt in range(DT):
                    p.pe(lambda q, o=t.PS[0][:, 0:512],
                         l=t.BG[:, 33280 + dt * 128: 33280 + dt * 128 + 128],
                         r=xh[dt][tc], s=(dt == 0), e2=(dt == 15):
                         q.matmul(o, l, r, start=s, stop=e2))
                p.bar()
                stg = t.BG[:, 41472:41984]
                p.act(lambda q, o=stg, i=t.PS[0][:, 0:512]: q.copy(out=o, in_=i))
                p.bar()
                p.st(lambda q, o=sqkv[b, ct, :, tc * 512:(tc + 1) * 512], i=stg:
                     q.dma_start(out=o, in_=i))
                p.bar()
        for ct in (range(24) if 'p2' in L1A_PH else []):
            kb = t.BG[:, 0:2048]
            p.ld(lambda q, o=kb, i=sqkv[b, ct, :, :]: q.dma_start(out=o, in_=i))
            p.bar()
            TA = t.FS[:, 0:2048]
            TB = t.FS[:, 2048:4096]
            w = lambda i: t.SM[:, 144 + ct * 4 + i: 144 + ct * 4 + i + 1]
            p.dve(lambda q, o=TA, i=kb, s=w(3): q.tensor_scalar(
                  out=o, in0=i, scalar1=s, scalar2=None, op0=ALU.mult))
            p.bar()
            p.dve(lambda q, o=TB[:, 1:2048], i=kb[:, 0:2047], s=w(2),
                  a=TA[:, 1:2048]: q.scalar_tensor_tensor(
                  out=o, in0=i, scalar=s, in1=a, op0=ALU.mult, op1=ALU.add))
            p.dve(lambda q, o=TB[:, 0:1], i=TA[:, 0:1]: q.tensor_copy(out=o, in_=i))
            p.bar()
            p.dve(lambda q, o=TA[:, 2:2048], i=kb[:, 0:2046], s=w(1),
                  a=TB[:, 2:2048]: q.scalar_tensor_tensor(
                  out=o, in0=i, scalar=s, in1=a, op0=ALU.mult, op1=ALU.add))
            p.dve(lambda q, o=TA[:, 0:2], i=TB[:, 0:2]: q.tensor_copy(out=o, in_=i))
            p.bar()
            p.dve(lambda q, o=TB[:, 3:2048], i=kb[:, 0:2045], s=w(0),
                  a=TA[:, 3:2048]: q.scalar_tensor_tensor(
                  out=o, in0=i, scalar=s, in1=a, op0=ALU.mult, op1=ALU.add))
            p.dve(lambda q, o=TB[:, 0:3], i=TA[:, 0:3]: q.tensor_copy(out=o, in_=i))
            p.bar()
            p.act(lambda q, o=kb, i=TB: q.activation(out=o, in_=i, func=AF.Silu))
            p.bar()
            if ct < 20:
                sqb = t.BG[:, 2048:4096]
                p.act(lambda q, o=sqb, i=kb: q.activation(out=o, in_=i,
                      func=AF.Square))
                p.bar()
                for tc in range(NTC):
                    p.pe(lambda q, o=t.PR[0:1, 0:512],
                         i=sqb[:, tc * 512:(tc + 1) * 512]:
                         q.matmul(o, t.ones_col, i, start=True, stop=True))
                    p.bar()
                    rr = t.RW[0:1, 2048:2560]
                    p.act(lambda q, o=rr: q.activation(
                          out=o, in_=t.PR[0:1, 0:512], func=AF.Sqrt, bias=1e-6))
                    p.bar()
                    p.dve(lambda q, o=rr, i=rr: q.reciprocal(out=o, in_=i))
                    p.bar()
                    if ct < 16:
                        p.dve(lambda q, o=rr, i=rr: q.tensor_scalar(
                              out=o, in0=i, scalar1=0.125, scalar2=None,
                              op0=ALU.mult))
                        p.bar()
                    rrb = t.BG[0:1, 66000:66512]
                    p.act(lambda q, o=rrb, i=rr: q.copy(out=o, in_=i))
                    p.bar()
                    p.pe(lambda q, o=t.PB[:, 0:512], i=rrb:
                         q.matmul(o, t.ones_row, i, start=True, stop=True))
                    p.bar()
                    p.dve(lambda q, o=kb[:, tc * 512:(tc + 1) * 512]:
                          q.tensor_tensor(out=o, in0=o, in1=t.PB[:, 0:512],
                                          op=ALU.mult))
                    p.bar()
            p.st(lambda q, o=sqkv[b, ct, :, :], i=kb: q.dma_start(out=o, in_=i))
            p.bar()
        for c in (range(16) if 'p3' in L1A_PH else []):
            for srow, dcol in ((2048, 0), (0, 512)):
                src = t.BGL[0:16, srow + c * 128: srow + (c + 1) * 128]
                pso = t.PS[1][0:128, 0:16]
                p.pe(lambda q, o=pso, i=src: q.transpose(o, i, t.IDF[0:16, 0:16]))
                p.bar()
                if srow == 2048:
                    gt = t.FS[:, 0:16]
                    p.act(lambda q, o=gt, i=pso: q.copy(out=o, in_=i))
                    p.bar()
                    p.pe(lambda q, o=t.PS[2][0:128, 0:16], i=gt:
                         q.matmul(o, t.UT[:], i, start=True, stop=True))
                    p.bar()
                    cc = t.CCA[:, c * 16:(c + 1) * 16]
                    p.act(lambda q, o=cc: q.activation(
                          out=o, in_=t.PS[2][0:128, 0:16], func=AF.Exp))
                    p.bar()
                    p.dve(lambda q, o=t.CCA[:, 256 + c * 16: 256 + (c + 1) * 16],
                          i=cc: q.reciprocal(out=o, in_=i))
                    p.bar()
                else:
                    p.act(lambda q, o=t.CCA[:, 512 + c * 16: 512 + (c + 1) * 16],
                          i=pso: q.copy(out=o, in_=i))
                    p.bar()
        KFM, VFM, KTOK, VTOK, QF, GSIL, CH5 = 0, 2048, 4096, 6144, 8192, 10240, 12288
        DB = 22528
        Pb = [DB, DB + 128]; PTb = [DB + 256, DB + 384]; RTb = [DB + 512, DB + 640]
        DLo, SBo, OFo = DB + 768, DB + 896, DB + 1024
        TI = DB + 1152
        KAo, VBo, STG = DB + 1536, DB + 1664, DB + 1792
        for h in (range(L1A_NH) if 'p4' in L1A_PH else []):
            kvh = h // 4
            if h % 4 == 0:
                p.ld(lambda q, o=t.BG[:, KFM:KFM + 2048],
                     i=sqkv[b, 16 + kvh, :, :]: q.dma_start(out=o, in_=i))
                p.ld(lambda q, o=t.BG[:, VFM:VFM + 2048],
                     i=sqkv[b, 20 + kvh, :, :]: q.dma_start(out=o, in_=i))
                p.bar()
                for tt in (range(16) if L1A_TR else []):
                    tr_copy(t, t.BG[:, KFM + tt * 128: KFM + tt * 128 + 128],
                            t.BG[:, KTOK + tt * 128: KTOK + tt * 128 + 128], 0)
                    tr_copy(t, t.BG[:, VFM + tt * 128: VFM + tt * 128 + 128],
                            t.BG[:, VTOK + tt * 128: VTOK + tt * 128 + 128], 1)
                p.bar()
            p.ld(lambda q, o=t.BG[:, QF:QF + 2048], i=sqkv[b, h, :, :]:
                 q.dma_start(out=o, in_=i))
            for cq in (range(4) if L1A_GS else []):
                p.ld(lambda q, o=t.BG[:, GSIL + cq * 512: GSIL + (cq + 1) * 512]
                     .rearrange("p (c v) -> p c v", c=4),
                     i=sg[cq * 512:(cq + 1) * 512, h * 128:(h + 1) * 128]
                     .rearrange("(c p) v -> p c v", p=128):
                     q.dma_start(out=o, in_=i))
            p.bar()
            S32 = [t.FS[:, 4096:4224], t.FS[:, 4224:4352]]
            sbf = t.BG[:, SBo:SBo + 128]
            p.dve(lambda q, o=S32[0]: q.memzero(o))
            p.dve(lambda q, o=sbf: q.memzero(o))
            p.bar()
            cco = lambda c: t.CCA[:, c * 16 + h: c * 16 + h + 1]
            ccio = lambda c: t.CCA[:, 256 + c * 16 + h: 256 + c * 16 + h + 1]
            bto = lambda c: t.CCA[:, 512 + c * 16 + h: 512 + c * 16 + h + 1]
            for c in range(L1A_NC):
                kf = t.BG[:, KFM + c * 128: KFM + c * 128 + 128]
                qf = t.BG[:, QF + c * 128: QF + c * 128 + 128]
                ktk = t.BG[:, KTOK + c * 128: KTOK + c * 128 + 128]
                vtk = t.BG[:, VTOK + c * 128: VTOK + c * 128 + 128]
                s1 = t.CCA[:, 768:769]
                p.dve(lambda q, o=s1, a=bto(c), b2=cco(c): q.tensor_tensor(
                      out=o, in0=a, in1=b2, op=ALU.mult))
                p.bar()
                for bi, colap in ((0, cco(c)), (1, s1), (2, ccio(c))):
                    ti = t.BG[:, TI + bi * 128: TI + (bi + 1) * 128]
                    p.dve(lambda q, o=ti, s=colap: q.tensor_scalar(
                          out=o, in0=t.ident_bf, scalar1=s, scalar2=None,
                          op0=ALU.mult))
                    p.bar()
                    p.pe(lambda q, o=t.PB[:, bi * 128:(bi + 1) * 128], i=ti:
                         q.matmul(o, t.ones_full, i, start=True, stop=True))
                    p.bar()
                ccB = t.PB[:, 0:128]; s1B = t.PB[:, 128:256]; cciB = t.PB[:, 256:384]
                p.pe(lambda q, o=t.PS[0][:, 0:128], l=kf, r=kf:
                     q.matmul(o, l, r, start=True, stop=True))
                p.bar()
                v1 = t.FS[:, 4352:4480]
                p.dve(lambda q, o=v1, s=s1: q.tensor_scalar(
                      out=o, in0=t.PS[0][:, 0:128], scalar1=s, scalar2=None,
                      op0=ALU.mult))
                v3 = t.FS[:, 4480:4608]
                p.dve(lambda q, o=v3, s=ccio(c): q.tensor_scalar(
                      out=o, in0=t.PS[0][:, 0:128], scalar1=s, scalar2=None,
                      op0=ALU.mult))
                p.bar()
                p.dve(lambda q, o=v1: q.tensor_tensor(out=o, in0=v1, in1=cciB,
                      op=ALU.mult))
                p.dve(lambda q, o=v3: q.tensor_tensor(out=o, in0=v3, in1=s1B,
                      op=ALU.mult))
                p.bar()
                pn = t.BG[:, Pb[0]:Pb[0] + 128]
                pt = t.BG[:, PTb[0]:PTb[0] + 128]
                p.dve(lambda q, o=pn, a=v1: q.tensor_tensor(out=o, in0=a,
                      in1=t.negtril, op=ALU.mult))
                p.dve(lambda q, o=pt, a=v3: q.tensor_tensor(out=o, in0=a,
                      in1=t.negtriu, op=ALU.mult))
                p.bar()
                rt = t.BG[:, RTb[0]:RTb[0] + 128]
                p.dve(lambda q, o=rt, a=pt: q.tensor_tensor(out=o, in0=a,
                      in1=t.ident_bf, op=ALU.add))
                p.bar()
                cur = 0
                for li in range(6):
                    rtc = t.BG[:, RTb[cur % 2]:RTb[cur % 2] + 128]
                    rtn = t.BG[:, RTb[(cur + 1) % 2]:RTb[(cur + 1) % 2] + 128]
                    pc = t.BG[:, Pb[cur % 2]:Pb[cur % 2] + 128]
                    ptc = t.BG[:, PTb[cur % 2]:PTb[cur % 2] + 128]
                    pnx = t.BG[:, Pb[(cur + 1) % 2]:Pb[(cur + 1) % 2] + 128]
                    ptx = t.BG[:, PTb[(cur + 1) % 2]:PTb[(cur + 1) % 2] + 128]
                    p.pe(lambda q, o=t.PS[2][:, 0:128], l=ptc, r=pc:
                         q.matmul(o, l, r, start=True, stop=True))
                    p.pe(lambda q, o=t.PS[3][:, 0:128], l=pc, r=ptc:
                         q.matmul(o, l, r, start=True, stop=True))
                    p.bar()
                    p.act(lambda q, o=pnx, i=t.PS[2][:, 0:128]: q.copy(out=o, in_=i))
                    p.act(lambda q, o=ptx, i=t.PS[3][:, 0:128]: q.copy(out=o, in_=i))
                    p.bar()
                    p.pe(lambda q, o=t.PS[1][:, 0:128], l=pnx, r=rtc:
                         q.matmul(o, l, r, start=True, stop=True))
                    p.bar()
                    p.dve(lambda q, o=rtn, a=rtc: q.tensor_tensor(
                          out=o, in0=a, in1=t.PS[1][:, 0:128], op=ALU.add))
                    p.bar()
                    cur += 1
                rtf = t.BG[:, RTb[cur % 2]:RTb[cur % 2] + 128]
                ch5 = lambda s: t.BG[:, CH5 + c * 640 + s * 128:
                                     CH5 + c * 640 + (s + 1) * 128]
                ka = t.BG[:, KAo:KAo + 128]
                p.dve(lambda q, o=ka, i=ktk, s=s1: q.tensor_scalar(
                      out=o, in0=i, scalar1=s, scalar2=None, op0=ALU.mult))
                p.bar()
                p.pe(lambda q, o=t.PS[1][:, 0:128], l=ka, r=rtf:
                     q.matmul(o, l, r, start=True, stop=True))
                p.bar()
                p.act(lambda q, o=ch5(0), i=t.PS[1][:, 0:128]: q.copy(out=o, in_=i))
                vb = t.BG[:, VBo:VBo + 128]
                p.dve(lambda q, o=vb, i=vtk, s=bto(c): q.tensor_scalar(
                      out=o, in0=i, scalar1=s, scalar2=None, op0=ALU.mult))
                p.bar()
                p.pe(lambda q, o=t.PS[2][:, 0:128], l=rtf, r=vb:
                     q.matmul(o, l, r, start=True, stop=True))
                p.pe(lambda q, o=t.PS[3][:, 0:128], l=kf, r=qf:
                     q.matmul(o, l, r, start=True, stop=True))
                p.bar()
                p.act(lambda q, o=ch5(1), i=t.PS[2][:, 0:128]: q.copy(out=o, in_=i))
                v5 = t.FS[:, 4352:4480]
                p.dve(lambda q, o=v5, s=ccio(c): q.tensor_scalar(
                      out=o, in0=t.PS[3][:, 0:128], scalar1=s, scalar2=None,
                      op0=ALU.mult))
                p.bar()
                p.dve(lambda q, o=v5: q.tensor_tensor(out=o, in0=v5, in1=ccB,
                      op=ALU.mult))
                p.bar()
                p.dve(lambda q, o=ch5(2), a=v5: q.tensor_tensor(out=o, in0=a,
                      in1=t.utri1, op=ALU.mult))
                ccr = t.CCA[:, 769:770]
                p.dve(lambda q, o=ccr, a=ccio(c): q.tensor_tensor(
                      out=o, in0=a, in1=ccB[:, 127:128], op=ALU.mult))
                p.bar()
                p.dve(lambda q, o=ch5(3), i=ktk, s=ccr: q.tensor_scalar(
                      out=o, in0=i, scalar1=s, scalar2=None, op0=ALU.mult))
                p.dve(lambda q, o=ch5(4), i=qf: q.tensor_tensor(
                      out=o, in0=i, in1=ccB, op=ALU.mult))
                p.act(lambda q, o=t.CCA[:, 784 + c: 785 + c], i=ccB[:, 127:128]:
                      q.copy(out=o, in_=i))
                p.bar()
            for c in (range(L1A_NC) if L1A_SEQ else []):
                ch5 = lambda s: t.BG[:, CH5 + c * 640 + s * 128:
                                     CH5 + c * 640 + (s + 1) * 128]
                p.pe(lambda q, o=t.PS[0][:, 0:128], l=ch5(0), r=sbf:
                     q.matmul(o, l, r, start=True, stop=True))
                p.bar()
                dl = t.BG[:, DLo:DLo + 128]
                p.dve(lambda q, o=dl, a=ch5(1): q.tensor_tensor(
                      out=o, in0=a, in1=t.PS[0][:, 0:128], op=ALU.subtract))
                p.bar()
                p.pe(lambda q, o=t.PS[1][:, 0:128], l=ch5(4), r=sbf:
                     q.matmul(o, l, r, start=True, stop=False))
                p.pe(lambda q, o=t.PS[1][:, 0:128], l=ch5(2), r=dl:
                     q.matmul(o, l, r, start=False, stop=True))
                p.pe(lambda q, o=t.PS[2][:, 0:128], l=ch5(3), r=dl:
                     q.matmul(o, l, r, start=True, stop=True))
                p.bar()
                snew = S32[(c + 1) % 2]
                p.dve(lambda q, o=snew, i=S32[c % 2], s=t.CCA[:, 784 + c: 785 + c]:
                      q.scalar_tensor_tensor(out=o, in0=i, scalar=s,
                      in1=t.PS[2][:, 0:128], op0=ALU.mult, op1=ALU.add))
                sq2 = t.FS[:, 4352:4480]
                p.act(lambda q, o=sq2: q.activation(out=o, in_=t.PS[1][:, 0:128],
                      func=AF.Square))
                p.bar()
                p.act(lambda q, o=sbf, i=snew: q.copy(out=o, in_=i))
                rs = t.CCA[:, 770:771]
                p.dve(lambda q, o=rs, i=sq2: q.reduce_sum(out=o, in_=i, axis=AX.X))
                p.bar()
                p.dve(lambda q, o=rs, i=rs: q.tensor_scalar(
                      out=o, in0=i, scalar1=1.0 / 128, scalar2=None, op0=ALU.mult))
                p.bar()
                p.act(lambda q, o=rs, i=rs: q.activation(out=o, in_=i,
                      func=AF.Sqrt, bias=1e-5))
                p.bar()
                p.dve(lambda q, o=rs, i=rs: q.reciprocal(out=o, in_=i))
                p.bar()
                o1 = t.FS[:, 4480:4608]
                p.dve(lambda q, o=o1, s=rs: q.tensor_scalar(
                      out=o, in0=t.PS[1][:, 0:128], scalar1=s, scalar2=None,
                      op0=ALU.mult))
                p.bar()
                p.dve(lambda q, o=o1: q.tensor_tensor(out=o, in0=o1,
                      in1=t.SM[:, 0:128], op=ALU.mult))
                p.bar()
                of = t.BG[:, OFo:OFo + 128]
                p.dve(lambda q, o=of, a=o1,
                      g=t.BG[:, GSIL + c * 128: GSIL + c * 128 + 128]:
                      q.tensor_tensor(out=o, in0=a, in1=g, op=ALU.mult))
                p.bar()
                stg = t.BG[:, STG:STG + 128]
                tr_copy(t, of, stg, 0)
                p.bar()
                p.st(lambda q, o=sot[h * 128:(h + 1) * 128, c * 128:(c + 1) * 128],
                     i=stg: q.dma_start(out=o, in_=i))
                p.bar()
        if 'p5' not in L1A_PH:
            continue
        for hd in range(16):
            p.ld(lambda q, o=t.BG[:, hd * 2048:(hd + 1) * 2048],
                 i=sot[hd * 128:(hd + 1) * 128, :]: q.dma_start(out=o, in_=i))
        p.bar()
        for ot in range(16):
            load_wT(t, ow[ot * 128:(ot + 1) * 128, :], 33280, stride=128)
            for tc in range(NTC):
                for hd in range(16):
                    p.pe(lambda q, o=t.PS[3][:, 0:512],
                         l=t.BG[:, 33280 + hd * 128: 33280 + hd * 128 + 128],
                         r=t.BG[:, hd * 2048 + tc * 512: hd * 2048 + tc * 512 + 512],
                         s=(hd == 0), e2=(hd == 15):
                         q.matmul(o, l, r, start=s, stop=e2))
                p.bar()
                xo = t.BG[:, 66000:66512]
                p.ld(lambda q, o=xo, i=t.xfm_src[b, ot, :, tc * 512:(tc + 1) * 512]:
                     q.dma_start(out=o, in_=i))
                p.bar()
                stg = t.BG[:, 66512:67024]
                p.dve(lambda q, o=stg, a=xo: q.tensor_tensor(
                      out=o, in0=a, in1=t.PS[3][:, 0:512], op=ALU.add))
                p.bar()
                p.st(lambda q, o=t.xfm_dst[b, ot, :, tc * 512:(tc + 1) * 512],
                     i=stg: q.dma_start(out=o, in_=i))
                p.bar()


def gather_lut(t, idx_fp_ap, out_ap, scr):
    p = t.p
    F = idx_fp_ap.shape[1]
    acc1 = t.BG[:, scr:scr + F]
    tmp0 = t.BG[:, scr + F:scr + 2 * F]
    tmp1 = t.BG[:, scr + 2 * F:scr + 3 * F]
    lut = lambda i: t.SM[:, 128 + i:128 + i + 1]
    p.dve(lambda q, o=out_ap, i=idx_fp_ap, s=lut(0): q.tensor_scalar(
          out=o, in0=i, scalar1=0.0, scalar2=s, op0=ALU.is_equal, op1=ALU.mult))
    p.pool(lambda q, o=acc1, i=idx_fp_ap, s=lut(1): q.tensor_scalar(
           out=o, in0=i, scalar1=1.0, scalar2=s, op0=ALU.is_equal, op1=ALU.mult))
    p.bar()
    for k in range(1, 8):
        p.dve(lambda q, o=tmp0, i=idx_fp_ap, v=float(2 * k), s=lut(2 * k):
              q.tensor_scalar(out=o, in0=i, scalar1=v, scalar2=s,
                              op0=ALU.is_equal, op1=ALU.mult))
        p.pool(lambda q, o=tmp1, i=idx_fp_ap, v=float(2 * k + 1), s=lut(2 * k + 1):
               q.tensor_scalar(out=o, in0=i, scalar1=v, scalar2=s,
                               op0=ALU.is_equal, op1=ALU.mult))
        p.bar()
        p.dve(lambda q, o=out_ap, a=tmp0: q.tensor_tensor(out=o, in0=o, in1=a,
              op=ALU.add))
        p.pool(lambda q, o=acc1, a=tmp1: q.tensor_tensor(out=o, in0=o, in1=a,
               op=ALU.add))
        p.bar()
    p.dve(lambda q, o=out_ap, a=acc1: q.tensor_tensor(out=o, in0=o, in1=a,
          op=ALU.add))
    p.bar()


def load_idx_fp(t, idx_rows_ap, F):
    p = t.p
    off = 0
    while off < F:
        w = min(2048, F - off)
        p.ld(lambda q, o=t.I3[:, 0:w], i=idx_rows_ap[:, off:off + w]:
             q.dma_start(out=o, in_=i))
        p.bar()
        p.dve(lambda q, o=t.FS[:, off:off + w], i=t.I3[:, 0:w]:
              q.tensor_copy(out=o, in_=i))
        p.bar()
        off += w


def build_l1f(t, final):
    p = t.p
    gi = t.inp['L1_ffn_gate_idx']
    ui = t.inp['L1_ffn_up_idx']
    di = t.inp['L1_ffn_down_idx']
    sh = t.internal['s_h']
    swg = t.internal['s_wgt']
    swu = t.internal['s_wut']
    swd = t.internal['s_wdt']
    p.ld(lambda q, o=t.SM[0:1, 144:160],
         i=t.inp['L1_ffn_lut'].rearrange("(a b) -> a b", a=1):
         q.dma_start(out=o, in_=i))
    p.bar()
    lrow = t.BG[0:1, 66000:66016]
    p.act(lambda q, o=lrow, i=t.SM[0:1, 144:160]: q.copy(out=o, in_=i))
    p.bar()
    p.pe(lambda q, o=t.PB[:, 0:16], i=lrow:
         q.matmul(o, t.ones_row, i, start=True, stop=True))
    p.bar()
    p.act(lambda q, o=t.SM[:, 128:144], i=t.PB[:, 0:16]: q.copy(out=o, in_=i))
    p.bar()
    for it in range(43):
        for idx_ap, dst in ((gi, swg), (ui, swu)):
            load_idx_fp(t, idx_ap[it * 128:(it + 1) * 128, :], 2048)
            gw = t.BG[:, 0:2048]
            gather_lut(t, t.FS[:, 0:2048], gw, 8192)
            for dt in range(16):
                tr_copy(t, gw[:, dt * 128:(dt + 1) * 128],
                        t.BG[:, 2048 + dt * 128: 2048 + dt * 128 + 128], dt % 2)
            p.bar()
            p.st(lambda q, o=dst[it], i=t.BG[:, 2048:4096]:
                 q.dma_start(out=o, in_=i))
            p.bar()
    for b in range(B):
        make_xh(t, b, 0, 32768)
        xh = []
        for dt in range(DT):
            row = []
            for tc in range(NTC):
                row.append(t.BG[:, dt * 2048 + tc * 512:
                               dt * 2048 + tc * 512 + 512])
            xh.append(row)
        for it in range(43):
            p.ld(lambda q, o=t.BG[:, 32768:34816], i=swg[it]:
                 q.dma_start(out=o, in_=i))
            p.ld(lambda q, o=t.BG[:, 34816:36864], i=swu[it]:
                 q.dma_start(out=o, in_=i))
            p.bar()
            for tc in range(NTC):
                for dt in range(DT):
                    p.pe(lambda q, o=t.PS[0][:, 0:512],
                         l=t.BG[:, 32768 + dt * 128: 32768 + dt * 128 + 128],
                         r=xh[dt][tc], s=(dt == 0), e2=(dt == 15):
                         q.matmul(o, l, r, start=s, stop=e2))
                for dt in range(DT):
                    p.pe(lambda q, o=t.PS[1][:, 0:512],
                         l=t.BG[:, 34816 + dt * 128: 34816 + dt * 128 + 128],
                         r=xh[dt][tc], s=(dt == 0), e2=(dt == 15):
                         q.matmul(o, l, r, start=s, stop=e2))
                p.bar()
                sg = t.BG[:, 36864:37376]
                p.act(lambda q, o=sg: q.activation(out=o, in_=t.PS[0][:, 0:512],
                      func=AF.Silu))
                p.bar()
                htile = t.BG[:, 37376:37888]
                p.dve(lambda q, o=htile, a=sg: q.tensor_tensor(out=o, in0=a,
                      in1=t.PS[1][:, 0:512], op=ALU.mult))
                p.bar()
                p.st(lambda q, o=sh[it, :, b * 2048 + tc * 512:
                                    b * 2048 + (tc + 1) * 512], i=htile:
                     q.dma_start(out=o, in_=i))
                p.bar()
    for ot in range(16):
        load_idx_fp(t, di[ot * 128:(ot + 1) * 128, :], 5504)
        dw = t.BG[:, 0:5504]
        gather_lut(t, t.FS[:, 0:5504], dw, 16384)
        for it in range(43):
            tr_copy(t, dw[:, it * 128:(it + 1) * 128],
                    t.BG[:, 5504 + it * 128: 5504 + it * 128 + 128], it % 2)
        p.bar()
        p.st(lambda q, o=swd[ot], i=t.BG[:, 5504:11008]:
             q.dma_start(out=o, in_=i))
        p.bar()
    y = t.inp['y'] if final else None
    for b in range(B):
        for tc in range(NTC):
            for it in range(43):
                p.ld(lambda q, o=t.BG[:, 16384 + it * 512: 16384 + it * 512 + 512],
                     i=sh[it, :, b * 2048 + tc * 512: b * 2048 + (tc + 1) * 512]:
                     q.dma_start(out=o, in_=i))
            p.bar()
            for ot in range(16):
                p.ld(lambda q, o=t.BG[:, 5504:11008], i=swd[ot]:
                     q.dma_start(out=o, in_=i))
                p.bar()
                for it in range(43):
                    p.pe(lambda q, o=t.PS[2][:, 0:512],
                         l=t.BG[:, 5504 + it * 128: 5504 + it * 128 + 128],
                         r=t.BG[:, 16384 + it * 512: 16384 + it * 512 + 512],
                         s=(it == 0), e2=(it == 42):
                         q.matmul(o, l, r, start=s, stop=e2))
                p.bar()
                xo = t.BG[:, 66016:66528]
                p.ld(lambda q, o=xo,
                     i=t.xfm_src[b, ot, :, tc * 512:(tc + 1) * 512]:
                     q.dma_start(out=o, in_=i))
                p.bar()
                if final:
                    ofm = t.FS[:, 0:512]
                    p.dve(lambda q, o=ofm, a=xo: q.tensor_tensor(out=o, in0=a,
                          in1=t.PS[2][:, 0:512], op=ALU.add))
                    p.bar()
                    for j in range(4):
                        pso = t.PS[3][0:128, 0:128]
                        p.pe(lambda q, o=pso, i=ofm[:, j * 128:(j + 1) * 128]:
                             q.transpose(o, i, t.IDF[:]))
                        p.bar()
                        stg = t.FS[:, 512 + (j % 2) * 128: 512 + (j % 2) * 128 + 128]
                        p.act(lambda q, o=stg, i=pso: q.copy(out=o, in_=i))
                        p.bar()
                        p.st(lambda q, o=y[b, tc * 512 + j * 128:
                                           tc * 512 + (j + 1) * 128,
                                           ot * 128:(ot + 1) * 128], i=stg:
                             q.dma_start(out=o, in_=i))
                        p.bar()
                else:
                    stg = t.BG[:, 66528:67040]
                    p.dve(lambda q, o=stg, a=xo: q.tensor_tensor(out=o, in0=a,
                          in1=t.PS[2][:, 0:512], op=ALU.add))
                    p.bar()
                    p.st(lambda q, o=t.xfm_dst[b, ot, :, tc * 512:(tc + 1) * 512],
                         i=stg: q.dma_start(out=o, in_=i))
                    p.bar()



LAUNCHES = [
    (['l0a'], True, False),
    (['l0f'], False, False),
    (['l1a'], False, False),
    (['l1f'], False, True),
]


def kernel(**inputs):
    ch = host_consts()
    state = None
    out = None
    for parts, first, last in LAUNCHES:
        nc, stack, t = build_program(parts, first, last)
        in_maps = []
        for c in range(8):
            m = {}
            for nm in t.inp.keys():
                if nm in ('y', 'xfm_out'):
                    continue
                if nm == 'xin':
                    m[nm] = np.ascontiguousarray(inputs['L0_attn_x'][2 * c:2 * c + 2])
                elif nm == 'xfm_in':
                    m[nm] = state[c]
                elif nm in ch:
                    m[nm] = ch[nm]
                else:
                    m[nm] = np.ascontiguousarray(inputs[nm])
            in_maps.append(m)
        res = run_bass_kernel_spmd(nc, in_maps, core_ids=list(range(8)))
        if last:
            out = np.concatenate([res.results[c]['y'] for c in range(8)], axis=0)
        else:
            state = [res.results[c]['xfm_out'] for c in range(8)]
        stack.close()
    return out


# version 02
# speedup vs baseline: 1.0030x; 1.0030x over previous
import numpy as np
import ml_dtypes
import concourse.bass as bass
import concourse.mybir as mybir
from concourse.bass_utils import run_bass_kernel_spmd
from contextlib import ExitStack

F32 = mybir.dt.float32
BF16 = mybir.dt.bfloat16
I32 = mybir.dt.int32
AF = mybir.ActivationFunctionType
ALU = mybir.AluOpType
AX = mybir.AxisListType
BF = ml_dtypes.bfloat16

B = 2
S = 2048
D = 2048
DT = 16
NTC = 4
NTT = 16

ENG = ['pe', 'act', 'dve', 'pool', 'sp']


class Prog:
    def __init__(self, nc):
        self.nc = nc
        self.ops = {e: [] for e in ENG}
        self.cnt = {e: 0 for e in ENG}
        self.dcnt = {e: 0 for e in ENG}
        self.pend = {e: {} for e in ENG}

    def _add(self, eng, fn, is_dma):
        w = self.pend[eng]
        self.pend[eng] = {}
        self.ops[eng].append((w, fn, is_dma))
        if is_dma:
            self.dcnt[eng] += 16
        else:
            self.cnt[eng] += 1

    def pe(self, fn): self._add('pe', fn, False)
    def act(self, fn): self._add('act', fn, False)
    def dve(self, fn): self._add('dve', fn, False)
    def pool(self, fn): self._add('pool', fn, False)
    def ld(self, fn): self._add('sp', fn, True)
    def st(self, fn): self._add('pool', fn, True)

    def bar(self):
        snap = {}
        for e in ENG:
            if self.cnt[e]:
                snap[('c', e)] = self.cnt[e]
            if self.dcnt[e]:
                snap[('d', e)] = self.dcnt[e]
        for e in ENG:
            d = self.pend[e]
            for k, v in snap.items():
                if d.get(k, 0) < v:
                    d[k] = v

    def emit(self, stack):
        nc = self.nc
        self.bar()
        csem = {e: stack.enter_context(nc.semaphore("c_" + e)) for e in ENG}
        dsem = {e: stack.enter_context(nc.semaphore("d_" + e))
                for e in ENG if self.dcnt[e]}

        def runner(e):
            def go(q):
                last = {}
                for w, fn, is_dma in self.ops[e]:
                    for k in sorted(w.keys()):
                        v = w[k]
                        if last.get(k, 0) >= v:
                            continue
                        q.wait_ge(csem[k[1]] if k[0] == 'c' else dsem[k[1]], v)
                        last[k] = v
                    instr = fn(q)
                    if is_dma:
                        instr.then_inc(dsem[e], 16)
                    else:
                        instr.then_inc(csem[e], 1)
                w = self.pend[e]
                for k in sorted(w.keys()):
                    v = w[k]
                    if last.get(k, 0) >= v:
                        continue
                    q.wait_ge(csem[k[1]] if k[0] == 'c' else dsem[k[1]], v)
                    last[k] = v
            return go

        blk = stack.enter_context(nc.Block())
        blk.tensor(runner('pe'))
        blk.scalar(runner('act'))
        blk.vector(runner('dve'))
        blk.gpsimd(runner('pool'))
        blk.sync(runner('sp'))


def host_consts():
    c = {}
    ident = np.eye(128, dtype=np.float32)
    c['c_identf'] = ident
    cb = np.zeros((128, 1024), dtype=np.float32)
    cb[:, 0:128] = ident
    cb[:, 128] = 1.0
    cb[:, 129:257] = 1.0
    p = np.arange(128)[:, None]
    f = np.arange(128)[None, :]
    cb[:, 257:385] = (p <= f).astype(np.float32)
    cb[:, 385:513] = -(f < p).astype(np.float32)
    cb[:, 513:641] = -(f > p).astype(np.float32)
    c['c_constb'] = cb.astype(BF)
    c['c_ut16'] = ((p <= f).astype(np.float32) / 16.0)
    dm = np.zeros((128, 4 * 512), dtype=np.float32)
    qf = np.arange(512)[None, :]
    for j in range(4):
        dm[:, j * 512:(j + 1) * 512] = (qf >= (j * 128 + p)).astype(np.float32)
    c['c_dmask'] = dm.astype(BF)
    hd = 64
    inv = 1.0 / 10000.0 ** (np.arange(0, hd, 2) / hd)
    fr = np.outer(np.arange(S), inv)
    c['c_cosrep'] = np.tile(np.cos(fr), (1, 8)).astype(np.float32).astype(BF)
    c['c_sinrep'] = np.tile(np.sin(fr), (1, 8)).astype(np.float32).astype(BF)
    c['c_scal'] = np.tile(np.array([[1e-5, 1e-6, 0.125, 1.0, 0.0, 0.0, 0.0,
                           0.0]], dtype=np.float32), (128, 1))
    return c


class Ctx:
    pass


def build_program(parts, first, last):
    nc = bass.Bass("TRN2", target_bir_lowering=False)
    stack = ExitStack()
    t = Ctx()
    t.nc = nc

    def dram(name, shape, dtype, kind):
        return nc.dram_tensor(name, list(shape), dtype, kind=kind).ap()

    t.inp = {}
    if first:
        t.inp['xin'] = dram('xin', (B, S, D), F32, 'ExternalInput')
    else:
        t.inp['xfm_in'] = dram('xfm_in', (B, DT, 128, S), BF16, 'ExternalInput')
    if last:
        t.inp['y'] = dram('y', (B, S, D), F32, 'ExternalOutput')
    else:
        t.inp['xfm_out'] = dram('xfm_out', (B, DT, 128, S), BF16, 'ExternalOutput')

    wnames = {
        'l0a': [('L0_attn_wq', (2048, 2048), F32), ('L0_attn_wk', (512, 2048), F32),
                ('L0_attn_wv', (512, 2048), F32), ('L0_attn_wo', (2048, 2048), F32)],
        'l0f': [('L0_ffn_Wr', (8, 2048), F32), ('L0_ffn_Wg', (8, 2048, 1024), F32),
                ('L0_ffn_Wu', (8, 2048, 1024), F32), ('L0_ffn_Wd', (8, 1024, 2048), F32),
                ('L0_ffn_Wg_s', (2048, 1024), F32), ('L0_ffn_Wu_s', (2048, 1024), F32),
                ('L0_ffn_Wd_s', (1024, 2048), F32)],
        'l1a': [('L1_attn_qkv_w', (3072, 2048), F32), ('L1_attn_g_w', (2048, 2048), F32),
                ('L1_attn_b_w', (16, 2048), F32), ('L1_attn_gk_w', (16, 2048), F32),
                ('L1_attn_conv_w', (3072, 4), F32), ('L1_attn_out_norm_w', (128,), F32),
                ('L1_attn_o_w', (2048, 2048), F32)],
        'l1f': [('L1_ffn_lut', (16,), F32), ('L1_ffn_gate_idx', (5504, 2048), I32),
                ('L1_ffn_up_idx', (5504, 2048), I32),
                ('L1_ffn_down_idx', (2048, 5504), I32)],
    }
    for pt in parts:
        for nm, shp, dt in wnames[pt]:
            t.inp[nm] = dram(nm, shp, dt, 'ExternalInput')

    t.chost = host_consts()
    for nm, arr in t.chost.items():
        t.inp[nm] = dram(nm, arr.shape, BF16 if arr.dtype == BF else F32,
                         'ExternalInput')

    internal = {}
    if first and last:
        internal['s_xfm'] = dram('s_xfm', (B, DT, 128, S), BF16, 'Internal')
    if 'l0a' in parts:
        internal['s_qt'] = dram('s_qt', (2048, 2048), BF16, 'Internal')
    if 'l1a' in parts:
        internal['s_g'] = dram('s_g', (2048, 2048), BF16, 'Internal')
        internal['s_ot'] = dram('s_ot', (2048, 2048), BF16, 'Internal')
        internal['s_qkv'] = dram('s_qkv', (B, 24, 128, S), BF16, 'Internal')
    if 'l0f' in parts:
        internal['s_acc'] = dram('s_acc', (B, DT, 128, S), BF16, 'Internal')
    if 'l1f' in parts:
        internal['s_h'] = dram('s_h', (43, 128, B * S), BF16, 'Internal')
        internal['s_wdt'] = dram('s_wdt', (16, 128, 5504), BF16, 'Internal')
        internal['s_wgt'] = dram('s_wgt', (43, 128, 2048), BF16, 'Internal')
        internal['s_wut'] = dram('s_wut', (43, 128, 2048), BF16, 'Internal')
    t.internal = internal

    if first and last:
        t.xfm_src = internal['s_xfm']
        t.xfm_dst = internal['s_xfm']
    elif first:
        t.xfm_src = t.inp['xfm_out']
        t.xfm_dst = t.inp['xfm_out']
    elif last:
        t.xfm_src = t.inp['xfm_in']
        t.xfm_dst = None
    else:
        t.xfm_src = t.inp['xfm_in']
        t.xfm_dst = t.inp['xfm_out']

    sb = lambda nm, shp, dt: stack.enter_context(nc.sbuf_tensor(nm, list(shp), dt))
    t.BG = sb('BG', (128, 69600), BF16)
    t.FS = sb('FS', (128, 5504), F32)
    t.CCA = sb('CCA', (128, 800), F32)
    t.BGL = sb('BGL', (16, 4096), F32)
    t.RW = sb('RW', (1, 2560), F32)
    t.SM = sb('SM', (128, 1024), F32)
    t.CB = sb('CB', (128, 1024), BF16)
    t.UT = sb('UT', (128, 128), F32)
    t.IDF = sb('IDF', (128, 128), F32)
    t.I3 = sb('I3', (128, 2048), I32)
    t.CSC = sb('CSC', (128, 8), F32)
    for i, v in enumerate([1e-5, 1e-6, 0.125, 1.0]):
        nc.const_aps.register(F32, v, t.CSC[:, i:i + 1])
    ps = lambda nm, shp, dt: stack.enter_context(nc.psum_tensor(nm, list(shp), dt))
    t.PS = [ps('PS%d' % i, (128, 512), F32) for i in range(4)]
    t.PR = ps('PR', (128, 512), F32)
    t.PB = ps('PB', (128, 512), F32)
    t.PT2 = [ps('PTa', (128, 512), BF16), ps('PTb', (128, 512), BF16)]

    p = Prog(nc)
    t.p = p

    t.ident_bf = t.CB[:, 0:128]
    t.ones_col = t.CB[:, 128:129]
    t.ones_row = t.CB[0:1, 129:257]
    t.ones_full = t.CB[:, 129:257]
    t.utri1 = t.CB[:, 257:385]
    t.negtril = t.CB[:, 385:513]
    t.negtriu = t.CB[:, 513:641]

    def _c(q, a, b):
        return q.dma_start(out=a, in_=b)
    p.ld(lambda q, a=t.CB[:], b=t.inp['c_constb'][:]: _c(q, a, b))
    p.ld(lambda q, a=t.UT[:], b=t.inp['c_ut16'][:]: _c(q, a, b))
    p.ld(lambda q, a=t.IDF[:], b=t.inp['c_identf'][:]: _c(q, a, b))
    p.ld(lambda q, a=t.CSC[:], b=t.inp['c_scal'][:]: _c(q, a, b))
    p.bar()

    if first:
        build_convert_in(t)
    for pt in parts:
        {'l0a': build_l0a, 'l0f': build_l0f, 'l1a': build_l1a,
         'l1f': lambda tt: build_l1f(tt, final=last)}[pt](t)
    if last and 'l1f' not in parts:
        build_convert_out(t)

    p.emit(stack)
    return nc, stack, t



def tr_copy(t, src_ap, dst_ap, pb_idx):
    p = t.p
    pp, ff = src_ap.shape[0], src_ap.shape[1]
    pso = t.PT2[pb_idx][0:ff, 0:pp]
    p.pe(lambda q, o=pso, i=src_ap, idn=t.ident_bf[0:pp, 0:pp]:
         q.transpose(o, i, idn))
    p.bar()
    p.act(lambda q, o=dst_ap, i=pso: q.copy(out=o, in_=i))


def load_wT(t, w_rows_ap, wt_off, stride=128):
    p = t.p
    p.ld(lambda q, o=t.FS[:, 0:2048], i=w_rows_ap: q.dma_start(out=o, in_=i))
    p.bar()
    cb = t.BG[:, 67000:69048]
    p.dve(lambda q, o=cb, i=t.FS[:, 0:2048]: q.tensor_copy(out=o, in_=i))
    p.bar()
    for dt in range(16):
        tr_copy(t, cb[:, dt * 128:(dt + 1) * 128],
                t.BG[:, wt_off + dt * stride: wt_off + dt * stride + 128], dt % 2)
    p.bar()


def make_xh(t, b, xh_off, sq_off):
    p = t.p
    src = t.xfm_src
    for tc in range(NTC):
        xh = []
        for dt in range(DT):
            xh.append(t.BG[:, xh_off + dt * 2048 + tc * 512:
                           xh_off + dt * 2048 + tc * 512 + 512])
        for dt in range(DT):
            p.ld(lambda q, o=xh[dt], i=src[b, dt, :, tc * 512:(tc + 1) * 512]:
                 q.dma_start(out=o, in_=i))
        p.bar()
        for dt in range(DT):
            sq = t.BG[:, sq_off + dt * 512: sq_off + (dt + 1) * 512]
            p.act(lambda q, o=sq, i=xh[dt]:
                  q.activation(out=o, in_=i, func=AF.Square))
        p.bar()
        for dt in range(DT):
            p.pe(lambda q, o=t.PR[0:1, 0:512], i=xh[dt], s=(dt == 0), e=(dt == 15):
                 q.matmul(o, t.ones_col, i, start=s, stop=e))
        for dt in range(DT):
            sq = t.BG[:, sq_off + dt * 512: sq_off + (dt + 1) * 512]
            p.pe(lambda q, o=t.PS[2][0:1, 0:512], i=sq, s=(dt == 0), e=(dt == 15):
                 q.matmul(o, t.ones_col, i, start=s, stop=e))
        p.bar()
        m = t.RW[0:1, 0:512]
        v2 = t.RW[0:1, 512:1024]
        r = t.RW[0:1, 1024:1536]
        mr = t.RW[0:1, 1536:2048]
        p.dve(lambda q, o=m: q.tensor_scalar(out=o, in0=t.PR[0:1, 0:512],
              scalar1=1.0 / D, scalar2=None, op0=ALU.mult))
        p.dve(lambda q, o=v2: q.tensor_scalar(out=o, in0=t.PS[2][0:1, 0:512],
              scalar1=1.0 / D, scalar2=None, op0=ALU.mult))
        p.bar()
        p.dve(lambda q, o=r, a=m: q.scalar_tensor_tensor(out=o, in0=a, scalar=-1.0,
              in1=a, op0=ALU.mult, op1=ALU.mult))
        p.bar()
        p.dve(lambda q, o=r, a=r, b2=v2: q.tensor_tensor(out=o, in0=a, in1=b2,
              op=ALU.add))
        p.bar()
        p.act(lambda q, o=r, i=r: q.activation(out=o, in_=i, func=AF.Sqrt,
              bias=1e-5))
        p.bar()
        p.dve(lambda q, o=r, i=r: q.reciprocal(out=o, in_=i))
        p.bar()
        p.dve(lambda q, o=mr, a=m, b2=r: q.scalar_tensor_tensor(out=o, in0=a,
              scalar=-1.0, in1=b2, op0=ALU.mult, op1=ALU.mult))
        p.bar()
        rb16 = t.BG[0:1, 69048:69560]
        mrb16 = t.BG[0:1, 66000:66512]
        p.act(lambda q, o=rb16, i=r: q.copy(out=o, in_=i))
        p.act(lambda q, o=mrb16, i=mr: q.copy(out=o, in_=i))
        p.bar()
        p.pe(lambda q, o=t.PB[:, 0:512], i=rb16:
             q.matmul(o, t.ones_row, i, start=True, stop=True))
        p.pe(lambda q, o=t.PS[3][:, 0:512], i=mrb16:
             q.matmul(o, t.ones_row, i, start=True, stop=True))
        p.bar()
        for dt in range(DT):
            p.dve(lambda q, o=xh[dt], i=xh[dt]: q.tensor_tensor(out=o, in0=i,
                  in1=t.PB[:, 0:512], op=ALU.mult))
        p.bar()
        for dt in range(DT):
            p.dve(lambda q, o=xh[dt], i=xh[dt]: q.tensor_tensor(out=o, in0=i,
                  in1=t.PS[3][:, 0:512], op=ALU.add))
        p.bar()


def build_convert_in(t):
    p = t.p
    dst = t.internal['s_xfm'] if 's_xfm' in t.internal else t.inp['xfm_out']
    t.xfm_src = dst
    if t.xfm_dst is not None:
        t.xfm_dst = t.xfm_dst
    x = t.inp['xin']
    for b in range(B):
        for tt in range(NTT):
            p.ld(lambda q, o=t.FS[:, 0:2048], i=x[b, tt * 128:(tt + 1) * 128, :]:
                 q.dma_start(out=o, in_=i))
            p.bar()
            cb = t.BG[:, 67000:69048]
            p.dve(lambda q, o=cb, i=t.FS[:, 0:2048]: q.tensor_copy(out=o, in_=i))
            p.bar()
            for dt in range(DT):
                stg = t.BG[:, 69048 + (dt % 2) * 128: 69048 + (dt % 2) * 128 + 128]
                tr_copy(t, cb[:, dt * 128:(dt + 1) * 128], stg, dt % 2)
                p.bar()
                p.st(lambda q, o=dst[b, dt, :, tt * 128:(tt + 1) * 128], i=stg:
                     q.dma_start(out=o, in_=i))
            p.bar()


def build_convert_out(t):
    p = t.p
    src = t.xfm_src
    y = t.inp['y']
    for b in range(B):
        for dt in range(DT):
            cb = t.BG[:, 0:2048]
            p.ld(lambda q, o=cb, i=src[b, dt, :, :]: q.dma_start(out=o, in_=i))
            p.bar()
            for tt in range(NTT):
                fsb = t.FS[:, 0:2048]
                p.dve(lambda q, o=fsb[:, (tt % 2) * 128:(tt % 2) * 128 + 128],
                      i=cb[:, tt * 128:(tt + 1) * 128]: q.tensor_copy(out=o, in_=i))
                p.bar()
                pso = t.PS[tt % 2][0:128, 0:128]
                p.pe(lambda q, o=pso, i=fsb[:, (tt % 2) * 128:(tt % 2) * 128 + 128]:
                     q.transpose(o, i, t.IDF[:]))
                p.bar()
                stg = t.FS[:, 2048 + (tt % 2) * 128: 2048 + (tt % 2) * 128 + 128]
                p.act(lambda q, o=stg, i=pso: q.copy(out=o, in_=i))
                p.bar()
                p.st(lambda q, o=y[b, tt * 128:(tt + 1) * 128,
                                   dt * 128:(dt + 1) * 128], i=stg:
                     q.dma_start(out=o, in_=i))
            p.bar()



def build_l0f(t):
    p = t.p
    XH = 0
    HT = 32768
    WG = 49152
    WU = 51200
    SG = 53248
    CBB = 53760
    WR = 55808
    RT1 = 55936
    SQo = 57000
    wg = t.inp['L0_ffn_Wg']
    wu = t.inp['L0_ffn_Wu']
    wd = t.inp['L0_ffn_Wd']
    acc = t.internal['s_acc']

    for b in range(B):
        make_xh(t, b, XH, SQo)
        xh = []
        for dt in range(DT):
            row = []
            for tc in range(NTC):
                row.append(t.BG[:, XH + dt * 2048 + tc * 512:
                               XH + dt * 2048 + tc * 512 + 512])
            xh.append(row)
        p.ld(lambda q, o=t.FS[0:8, 0:2048], i=t.inp['L0_ffn_Wr'][:]:
             q.dma_start(out=o, in_=i))
        p.bar()
        cb8 = t.BG[0:8, 67000:69048]
        p.dve(lambda q, o=cb8, i=t.FS[0:8, 0:2048]: q.tensor_copy(out=o, in_=i))
        p.bar()
        for dt in range(16):
            tr_copy(t, cb8[:, dt * 128:(dt + 1) * 128],
                    t.BG[:, WR + dt * 8: WR + dt * 8 + 8], dt % 2)
        p.bar()
        for tc in range(NTC):
            for dt in range(DT):
                p.pe(lambda q, o=t.PS[0][0:8, 0:512],
                     w=t.BG[:, WR + dt * 8: WR + dt * 8 + 8], i=xh[dt][tc],
                     s=(dt == 0), e=(dt == 15):
                     q.matmul(o, w, i, start=s, stop=e))
            p.bar()
            lg = t.FS[0:8, 2048 + tc * 512: 2048 + tc * 512 + 512]
            p.act(lambda q, o=lg, i=t.PS[0][0:8, 0:512]: q.copy(out=o, in_=i))
            p.bar()
        for tt in range(NTT):
            tc, j = divmod(tt, 4)
            srcl = t.FS[0:8, 2048 + tc * 512 + j * 128:
                        2048 + tc * 512 + j * 128 + 128]
            pso = t.PS[1][0:128, 0:8]
            p.pe(lambda q, o=pso, i=srcl: q.transpose(o, i, t.IDF[0:8, 0:8]))
            p.bar()
            lgt = t.SM[:, 368 + tt * 8: 368 + tt * 8 + 8]
            p.act(lambda q, o=lgt, i=pso: q.copy(out=o, in_=i))
            p.bar()
        for tt in range(NTT):
            lgt = t.SM[:, 368 + tt * 8: 368 + tt * 8 + 8]
            m1 = t.SM[:, 496:497]; m2 = t.SM[:, 497:498]
            dcol = t.SM[:, 498:499]; w1 = t.SM[:, 499:500]; w2 = t.SM[:, 500:501]
            msk1 = t.SM[:, 502:510]; mskd = t.SM[:, 510:518]; msk2 = t.SM[:, 518:526]
            p.dve(lambda q, o=m1, i=lgt: q.reduce_max(out=o, in_=i, axis=AX.X))
            p.bar()
            p.dve(lambda q, o=msk1, i=lgt, m=m1: q.tensor_scalar(out=o, in0=i,
                  scalar1=m, scalar2=None, op0=ALU.is_ge))
            p.bar()
            p.dve(lambda q, o=mskd, i=msk1, lg=lgt: q.scalar_tensor_tensor(out=o,
                  in0=i, scalar=-1e30, in1=lg, op0=ALU.mult, op1=ALU.add))
            p.bar()
            p.dve(lambda q, o=m2, i=mskd: q.reduce_max(out=o, in_=i, axis=AX.X))
            p.bar()
            p.dve(lambda q, o=msk2, i=mskd, m=m2: q.tensor_scalar(out=o, in0=i,
                  scalar1=m, scalar2=None, op0=ALU.is_ge))
            p.dve(lambda q, o=dcol, a=m2, b2=m1: q.tensor_tensor(out=o, in0=a,
                  in1=b2, op=ALU.subtract))
            p.bar()
            p.act(lambda q, o=dcol, i=dcol: q.activation(out=o, in_=i, func=AF.Exp))
            p.bar()
            p.dve(lambda q, o=w1, i=dcol: q.tensor_scalar(out=o, in0=i, scalar1=1.0,
                  scalar2=None, op0=ALU.add))
            p.bar()
            p.dve(lambda q, o=w1, i=w1: q.reciprocal(out=o, in_=i))
            p.bar()
            p.dve(lambda q, o=w2, i=w1: q.tensor_scalar(out=o, in0=i, scalar1=-1.0,
                  scalar2=1.0, op0=ALU.mult, op1=ALU.add))
            p.bar()
            comb = t.SM[:, 240 + tt * 8: 240 + tt * 8 + 8]
            p.dve(lambda q, o=comb, i=msk2, w=w2: q.tensor_scalar(out=o, in0=i,
                  scalar1=w, scalar2=None, op0=ALU.mult))
            p.bar()
            p.dve(lambda q, o=comb, i=msk1, w=w1: q.scalar_tensor_tensor(out=o,
                  in0=i, scalar=w, in1=o, op0=ALU.mult, op1=ALU.add))
            p.bar()
        for e in range(9):
            if e < 8:
                for tc in range(NTC):
                    for j in range(4):
                        tt = tc * 4 + j
                        tmp = t.BG[:, 69048:69176]
                        p.dve(lambda q, o=tmp,
                              s=t.SM[:, 240 + tt * 8 + e: 240 + tt * 8 + e + 1]:
                              q.tensor_scalar(out=o, in0=t.ident_bf, scalar1=s,
                                              scalar2=None, op0=ALU.mult))
                        p.bar()
                        p.pe(lambda q, o=t.PB[:, j * 128:(j + 1) * 128], i=tmp:
                             q.matmul(o, t.ones_full, i, start=True, stop=True))
                        p.bar()
                    cbb = t.BG[:, CBB + tc * 512: CBB + (tc + 1) * 512]
                    p.act(lambda q, o=cbb, i=t.PB[:, 0:512]: q.copy(out=o, in_=i))
                    p.bar()
            for it in range(8):
                for half in range(2):
                    if e < 8:
                        srcw = (wg if half == 0 else wu)[e][:, it * 128:(it + 1) * 128]
                    else:
                        srcw = (t.inp['L0_ffn_Wg_s'] if half == 0 else
                                t.inp['L0_ffn_Wu_s'])[:, it * 128:(it + 1) * 128]
                    woff = WG if half == 0 else WU
                    for qtr in range(4):
                        sl = srcw[qtr * 512:(qtr + 1) * 512, :]
                        dst3 = t.FS[:, qtr * 512:(qtr + 1) * 512].rearrange(
                            "p (dt i) -> p dt i", dt=4)
                        p.ld(lambda q, o=dst3,
                             i=sl.rearrange("(dt p) i -> p dt i", p=128):
                             q.dma_start(out=o, in_=i))
                    p.bar()
                    p.dve(lambda q, o=t.BG[:, woff:woff + 2048], i=t.FS[:, 0:2048]:
                          q.tensor_copy(out=o, in_=i))
                    p.bar()
                for tc in range(NTC):
                    for dt in range(DT):
                        p.pe(lambda q, o=t.PS[0][:, 0:512],
                             w=t.BG[:, WG + dt * 128: WG + dt * 128 + 128],
                             i=xh[dt][tc], s=(dt == 0), e2=(dt == 15):
                             q.matmul(o, w, i, start=s, stop=e2))
                    for dt in range(DT):
                        p.pe(lambda q, o=t.PS[1][:, 0:512],
                             w=t.BG[:, WU + dt * 128: WU + dt * 128 + 128],
                             i=xh[dt][tc], s=(dt == 0), e2=(dt == 15):
                             q.matmul(o, w, i, start=s, stop=e2))
                    p.bar()
                    sg = t.BG[:, SG:SG + 512]
                    p.act(lambda q, o=sg: q.activation(out=o, in_=t.PS[0][:, 0:512],
                          func=AF.Silu))
                    p.bar()
                    ht = t.BG[:, HT + it * 2048 + tc * 512:
                              HT + it * 2048 + tc * 512 + 512]
                    p.dve(lambda q, o=ht, a=sg: q.tensor_tensor(out=o, in0=a,
                          in1=t.PS[1][:, 0:512], op=ALU.mult))
                    p.bar()
                    if e < 8:
                        cbb = t.BG[:, CBB + tc * 512: CBB + (tc + 1) * 512]
                        p.dve(lambda q, o=ht, c=cbb: q.tensor_tensor(out=o, in0=o,
                              in1=c, op=ALU.mult))
                        p.bar()
            wda = wd[e] if e < 8 else t.inp['L0_ffn_Wd_s']
            for ot in range(16):
                srcw = wda[:, ot * 128:(ot + 1) * 128]
                for qtr in range(2):
                    sl = srcw[qtr * 512:(qtr + 1) * 512, :]
                    dst3 = t.FS[:, qtr * 512:(qtr + 1) * 512].rearrange(
                        "p (it o) -> p it o", it=4)
                    p.ld(lambda q, o=dst3,
                         i=sl.rearrange("(it p) o -> p it o", p=128):
                         q.dma_start(out=o, in_=i))
                p.bar()
                p.dve(lambda q, o=t.BG[:, WG:WG + 1024], i=t.FS[:, 0:1024]:
                      q.tensor_copy(out=o, in_=i))
                p.bar()
                for tc in range(NTC):
                    for it in range(8):
                        ht = t.BG[:, HT + it * 2048 + tc * 512:
                                  HT + it * 2048 + tc * 512 + 512]
                        p.pe(lambda q, o=t.PS[2][:, 0:512],
                             w=t.BG[:, WG + it * 128: WG + it * 128 + 128], h=ht,
                             s=(it == 0), e2=(it == 7):
                             q.matmul(o, w, h, start=s, stop=e2))
                    p.bar()
                    stg = t.BG[:, RT1:RT1 + 512]
                    if e == 0:
                        p.act(lambda q, o=stg, i=t.PS[2][:, 0:512]:
                              q.copy(out=o, in_=i))
                        p.bar()
                    else:
                        p.ld(lambda q, o=stg,
                             i=acc[b, ot, :, tc * 512:(tc + 1) * 512]:
                             q.dma_start(out=o, in_=i))
                        p.bar()
                        p.dve(lambda q, o=stg: q.tensor_tensor(out=o, in0=o,
                              in1=t.PS[2][:, 0:512], op=ALU.add))
                        p.bar()
                    if e < 8:
                        p.st(lambda q, o=acc[b, ot, :, tc * 512:(tc + 1) * 512],
                             i=stg: q.dma_start(out=o, in_=i))
                        p.bar()
                    else:
                        xo = t.BG[:, RT1 + 512:RT1 + 1024]
                        p.ld(lambda q, o=xo,
                             i=t.xfm_src[b, ot, :, tc * 512:(tc + 1) * 512]:
                             q.dma_start(out=o, in_=i))
                        p.bar()
                        p.dve(lambda q, o=stg, i2=xo: q.tensor_tensor(out=o, in0=o,
                              in1=i2, op=ALU.add))
                        p.bar()
                        p.st(lambda q, o=t.xfm_dst[b, ot, :, tc * 512:(tc + 1) * 512],
                             i=stg: q.dma_start(out=o, in_=i))
                        p.bar()


def build_l0a(t):
    p = t.p
    XH = 0
    OTo = 0
    KTo = 32768
    Vo = 40960
    QSo = 49152
    QBo = 51200
    PTBo = 51712
    OVSo = 52736
    WSo = 53248
    COSo = 40960
    SINo = 45056
    qt = t.internal['s_qt']
    wq = t.inp['L0_attn_wq']; wk = t.inp['L0_attn_wk']
    wv = t.inp['L0_attn_wv']; wo = t.inp['L0_attn_wo']

    for b in range(B):
        make_xh(t, b, XH, 32768)
        xh = []
        for dt in range(DT):
            xh.append(t.BG[:, XH + dt * 2048: XH + (dt + 1) * 2048])
        p.ld(lambda q, o=t.BG[:, COSo:COSo + 4096].rearrange(
             "p (tt c) -> p tt c", tt=16),
             i=t.inp['c_cosrep'].rearrange("(tt p) c -> p tt c", p=128):
             q.dma_start(out=o, in_=i))
        p.ld(lambda q, o=t.BG[:, SINo:SINo + 4096].rearrange(
             "p (tt c) -> p tt c", tt=16),
             i=t.inp['c_sinrep'].rearrange("(tt p) c -> p tt c", p=128):
             q.dma_start(out=o, in_=i))
        p.bar()
        for wap, nocs, kind in ((wq, 4, 'q'), (wk, 1, 'k'), (wv, 1, 'v')):
            for oc in range(nocs):
                for j in range(4):
                    load_wT(t, wap[oc * 512 + j * 128: oc * 512 + (j + 1) * 128, :],
                            WSo + j * 128, stride=512)
                for tt in range(NTT):
                    for dt in range(DT):
                        p.pe(lambda q, o=t.PS[0][:, 0:512],
                             l=xh[dt][:, tt * 128:(tt + 1) * 128],
                             r=t.BG[:, WSo + dt * 512: WSo + dt * 512 + 512],
                             s=(dt == 0), e2=(dt == 15):
                             q.matmul(o, l, r, start=s, stop=e2))
                    p.bar()
                    qb = t.BG[:, QBo:QBo + 512]
                    if kind in ('q', 'k'):
                        cosn = t.BG[:, COSo + tt * 256: COSo + tt * 256 + 256]
                        sinn = t.BG[:, SINo + tt * 256: SINo + tt * 256 + 256]
                        xr = t.PS[0][:, 0:512:2]
                        xi = t.PS[0][:, 1:512:2]
                        t1 = t.FS[:, 0:256]; t2 = t.FS[:, 256:512]
                        t3 = t.FS[:, 512:768]; t4 = t.FS[:, 768:1024]
                        p.dve(lambda q, o=t1, a=cosn, b2=xr: q.tensor_tensor(
                              out=o, in0=a, in1=b2, op=ALU.mult))
                        p.dve(lambda q, o=t2, a=sinn, b2=xi: q.tensor_tensor(
                              out=o, in0=a, in1=b2, op=ALU.mult))
                        p.dve(lambda q, o=t3, a=sinn, b2=xr: q.tensor_tensor(
                              out=o, in0=a, in1=b2, op=ALU.mult))
                        p.dve(lambda q, o=t4, a=cosn, b2=xi: q.tensor_tensor(
                              out=o, in0=a, in1=b2, op=ALU.mult))
                        p.bar()
                        p.dve(lambda q, o=qb[:, 0:512:2], a=t1, b2=t2:
                              q.tensor_tensor(out=o, in0=a, in1=b2,
                                              op=ALU.subtract))
                        p.dve(lambda q, o=qb[:, 1:512:2], a=t3, b2=t4:
                              q.tensor_tensor(out=o, in0=a, in1=b2, op=ALU.add))
                        p.bar()
                        if kind == 'q':
                            for j in range(4):
                                stg = t.BG[:, 66000 + (j % 2) * 128:
                                           66000 + (j % 2) * 128 + 128]
                                tr_copy(t, qb[:, j * 128:(j + 1) * 128], stg, j % 2)
                                p.bar()
                                p.st(lambda q, o=qt[oc * 512 + j * 128:
                                                    oc * 512 + (j + 1) * 128,
                                                    tt * 128:(tt + 1) * 128],
                                     i=stg: q.dma_start(out=o, in_=i))
                            p.bar()
                        else:
                            for j in range(4):
                                dst = t.BG[:, KTo + j * 2048 + tt * 128:
                                           KTo + j * 2048 + tt * 128 + 128]
                                tr_copy(t, qb[:, j * 128:(j + 1) * 128], dst, j % 2)
                            p.bar()
                    else:
                        p.act(lambda q, o=t.BG[:, Vo + tt * 512: Vo + tt * 512 + 512],
                              i=t.PS[0][:, 0:512]: q.copy(out=o, in_=i))
                        p.bar()
        p.ld(lambda q, o=t.BG[:, WSo:WSo + 2048], i=t.inp['c_dmask'][:]:
             q.dma_start(out=o, in_=i))
        p.bar()
        for h in range(32):
            kv = h // 4
            ktile = kv // 2
            base = 64 * (kv % 2)
            qs = t.BG[base:base + 64, QSo:QSo + 2048]
            p.ld(lambda q, o=qs, i=qt[h * 64:h * 64 + 64, :]:
                 q.dma_start(out=o, in_=i))
            p.bar()
            for qc in range(NTC):
                nkt = 4 * qc + 4
                for kt in range(nkt):
                    lk = t.BG[base:base + 64,
                              KTo + ktile * 2048 + kt * 128:
                              KTo + ktile * 2048 + kt * 128 + 128]
                    rq = t.BG[base:base + 64, QSo + qc * 512: QSo + qc * 512 + 512]
                    p.pe(lambda q, o=t.PS[1][:, 0:512], l=lk, r=rq:
                         q.matmul(o, l, r, start=True, stop=True))
                    p.bar()
                    ptb = t.BG[:, PTBo + (kt % 2) * 512: PTBo + (kt % 2) * 512 + 512]
                    diag = kt - 4 * qc
                    if diag >= 0:
                        ef = t.FS[:, 1024:1536]
                        p.act(lambda q, o=ef: q.activation(
                              out=o, in_=t.PS[1][:, 0:512], func=AF.Exp,
                              scale=0.125))
                        p.bar()
                        p.dve(lambda q, o=ptb, a=ef,
                              m=t.BG[:, WSo + diag * 512: WSo + diag * 512 + 512]:
                              q.tensor_tensor(out=o, in0=a, in1=m, op=ALU.mult))
                    else:
                        p.act(lambda q, o=ptb: q.activation(
                              out=o, in_=t.PS[1][:, 0:512], func=AF.Exp,
                              scale=0.125))
                    p.bar()
                    p.pe(lambda q, o=t.PR[0:1, 0:512], i=ptb, s=(kt == 0),
                         e2=(kt == nkt - 1):
                         q.matmul(o, t.ones_col, i, start=s, stop=e2))
                    p.pe(lambda q, o=t.PS[2][0:64, 0:512],
                         l=t.BG[:, Vo + kt * 512 + kv * 64:
                                Vo + kt * 512 + kv * 64 + 64], i=ptb,
                         s=(kt == 0), e2=(kt == nkt - 1):
                         q.matmul(o, l, i, start=s, stop=e2))
                    p.bar()
                rz = t.RW[0:1, 2048:2560]
                p.dve(lambda q, o=rz: q.reciprocal(out=o, in_=t.PR[0:1, 0:512]))
                p.bar()
                rzb = t.BG[0:1, 66000:66512]
                p.act(lambda q, o=rzb, i=rz: q.copy(out=o, in_=i))
                p.bar()
                p.pe(lambda q, o=t.PB[0:64, 0:512], i=rzb:
                     q.matmul(o, t.CB[0:1, 129:193], i, start=True, stop=True))
                ovs = t.BG[0:64, OVSo:OVSo + 512]
                p.act(lambda q, o=ovs, i=t.PS[2][0:64, 0:512]: q.copy(out=o, in_=i))
                p.bar()
                obase = 64 * (h % 2)
                ht2 = h // 2
                odst = t.BG[obase:obase + 64,
                            OTo + ht2 * 2048 + qc * 512:
                            OTo + ht2 * 2048 + qc * 512 + 512]
                p.dve(lambda q, o=odst, a=ovs: q.tensor_tensor(
                      out=o, in0=a, in1=t.PB[0:64, 0:512], op=ALU.mult))
                p.bar()
        for ot in range(16):
            load_wT(t, wo[ot * 128:(ot + 1) * 128, :], WSo, stride=128)
            for tc in range(NTC):
                for hd in range(16):
                    p.pe(lambda q, o=t.PS[3][:, 0:512],
                         l=t.BG[:, WSo + hd * 128: WSo + hd * 128 + 128],
                         r=t.BG[:, OTo + hd * 2048 + tc * 512:
                                OTo + hd * 2048 + tc * 512 + 512],
                         s=(hd == 0), e2=(hd == 15):
                         q.matmul(o, l, r, start=s, stop=e2))
                p.bar()
                xo = t.BG[:, 66000:66512]
                p.ld(lambda q, o=xo, i=t.xfm_src[b, ot, :, tc * 512:(tc + 1) * 512]:
                     q.dma_start(out=o, in_=i))
                p.bar()
                stg = t.BG[:, 66512:67024]
                p.dve(lambda q, o=stg, a=xo: q.tensor_tensor(
                      out=o, in0=a, in1=t.PS[3][:, 0:512], op=ALU.add))
                p.bar()
                p.st(lambda q, o=t.xfm_dst[b, ot, :, tc * 512:(tc + 1) * 512],
                     i=stg: q.dma_start(out=o, in_=i))
                p.bar()


L1A_PH = {'p1', 'p2', 'p3', 'p4', 'p5'}
L1A_NH = 16
L1A_NC = 16
L1A_SEQ = True
L1A_GS = True
L1A_TR = True


def build_l1a(t):
    p = t.p
    qkvw = t.inp['L1_attn_qkv_w']
    gw = t.inp['L1_attn_g_w']
    ow = t.inp['L1_attn_o_w']
    sqkv = t.internal['s_qkv']
    sg = t.internal['s_g']
    sot = t.internal['s_ot']
    p.ld(lambda q, o=t.SM[0:1, 144:272],
         i=t.inp['L1_attn_out_norm_w'].rearrange("(a b) -> a b", a=1):
         q.dma_start(out=o, in_=i))
    p.bar()
    onr = t.BG[0:1, 66000:66128]
    p.act(lambda q, o=onr, i=t.SM[0:1, 144:272]: q.copy(out=o, in_=i))
    p.bar()
    p.pe(lambda q, o=t.PB[:, 0:128], i=onr:
         q.matmul(o, t.ones_row, i, start=True, stop=True))
    p.bar()
    p.act(lambda q, o=t.SM[:, 0:128], i=t.PB[:, 0:128]: q.copy(out=o, in_=i))
    p.bar()
    for ct in range(24):
        p.ld(lambda q, o=t.SM[:, 144 + ct * 4: 144 + (ct + 1) * 4],
             i=t.inp['L1_attn_conv_w'][ct * 128:(ct + 1) * 128, :]:
             q.dma_start(out=o, in_=i))
    p.bar()

    for b in range(B):
        if 'p1' not in L1A_PH:
            break
        make_xh(t, b, 0, 32768)
        xh = []
        for dt in range(DT):
            row = []
            for tc in range(NTC):
                row.append(t.BG[:, dt * 2048 + tc * 512:
                               dt * 2048 + tc * 512 + 512])
            xh.append(row)
        for wnm, outrow in (('L1_attn_b_w', 0), ('L1_attn_gk_w', 2048)):
            p.ld(lambda q, o=t.FS[0:16, 0:2048], i=t.inp[wnm][:]:
                 q.dma_start(out=o, in_=i))
            p.bar()
            cb16 = t.BG[0:16, 67000:69048]
            p.dve(lambda q, o=cb16, i=t.FS[0:16, 0:2048]: q.tensor_copy(out=o, in_=i))
            p.bar()
            for dt in range(16):
                tr_copy(t, cb16[:, dt * 128:(dt + 1) * 128],
                        t.BG[:, 33024 + dt * 16: 33024 + dt * 16 + 16], dt % 2)
            p.bar()
            for tc in range(NTC):
                for dt in range(DT):
                    p.pe(lambda q, o=t.PS[0][0:16, 0:512],
                         l=t.BG[:, 33024 + dt * 16: 33024 + dt * 16 + 16],
                         r=xh[dt][tc], s=(dt == 0), e2=(dt == 15):
                         q.matmul(o, l, r, start=s, stop=e2))
                p.bar()
                dst = t.BGL[0:16, outrow + tc * 512: outrow + (tc + 1) * 512]
                p.act(lambda q, o=dst: q.activation(
                      out=o, in_=t.PS[0][0:16, 0:512], func=AF.Sigmoid))
                p.bar()
                if outrow == 2048:
                    p.act(lambda q, o=dst, i=dst: q.activation(out=o, in_=i,
                          func=AF.Ln))
                    p.bar()
        for oc in range(4):
            for j in range(4):
                load_wT(t, gw[oc * 512 + j * 128: oc * 512 + (j + 1) * 128, :],
                        33280 + j * 128, stride=512)
            for tt in range(NTT):
                for dt in range(DT):
                    p.pe(lambda q, o=t.PS[0][:, 0:512],
                         l=xh[dt][0][:, 0:512].tensor[
                             :, dt * 2048 + tt * 128: dt * 2048 + tt * 128 + 128]
                         if False else t.BG[:, dt * 2048 + tt * 128:
                                            dt * 2048 + tt * 128 + 128],
                         r=t.BG[:, 33280 + dt * 512: 33280 + dt * 512 + 512],
                         s=(dt == 0), e2=(dt == 15):
                         q.matmul(o, l, r, start=s, stop=e2))
                p.bar()
                gb = t.BG[:, 41472:41984]
                p.act(lambda q, o=gb: q.activation(out=o, in_=t.PS[0][:, 0:512],
                      func=AF.Silu))
                p.bar()
                p.st(lambda q, o=sg[tt * 128:(tt + 1) * 128,
                                    oc * 512:(oc + 1) * 512], i=gb:
                     q.dma_start(out=o, in_=i))
                p.bar()
        for ct in range(24):
            load_wT(t, qkvw[ct * 128:(ct + 1) * 128, :], 33280, stride=128)
            for tc in range(NTC):
                for dt in range(DT):
                    p.pe(lambda q, o=t.PS[0][:, 0:512],
                         l=t.BG[:, 33280 + dt * 128: 33280 + dt * 128 + 128],
                         r=xh[dt][tc], s=(dt == 0), e2=(dt == 15):
                         q.matmul(o, l, r, start=s, stop=e2))
                p.bar()
                stg = t.BG[:, 41472:41984]
                p.act(lambda q, o=stg, i=t.PS[0][:, 0:512]: q.copy(out=o, in_=i))
                p.bar()
                p.st(lambda q, o=sqkv[b, ct, :, tc * 512:(tc + 1) * 512], i=stg:
                     q.dma_start(out=o, in_=i))
                p.bar()
        for ct in (range(24) if 'p2' in L1A_PH else []):
            kb = t.BG[:, 0:2048]
            p.ld(lambda q, o=kb, i=sqkv[b, ct, :, :]: q.dma_start(out=o, in_=i))
            p.bar()
            TA = t.FS[:, 0:2048]
            TB = t.FS[:, 2048:4096]
            w = lambda i: t.SM[:, 144 + ct * 4 + i: 144 + ct * 4 + i + 1]
            p.dve(lambda q, o=TA, i=kb, s=w(3): q.tensor_scalar(
                  out=o, in0=i, scalar1=s, scalar2=None, op0=ALU.mult))
            p.bar()
            p.dve(lambda q, o=TB[:, 1:2048], i=kb[:, 0:2047], s=w(2),
                  a=TA[:, 1:2048]: q.scalar_tensor_tensor(
                  out=o, in0=i, scalar=s, in1=a, op0=ALU.mult, op1=ALU.add))
            p.dve(lambda q, o=TB[:, 0:1], i=TA[:, 0:1]: q.tensor_copy(out=o, in_=i))
            p.bar()
            p.dve(lambda q, o=TA[:, 2:2048], i=kb[:, 0:2046], s=w(1),
                  a=TB[:, 2:2048]: q.scalar_tensor_tensor(
                  out=o, in0=i, scalar=s, in1=a, op0=ALU.mult, op1=ALU.add))
            p.dve(lambda q, o=TA[:, 0:2], i=TB[:, 0:2]: q.tensor_copy(out=o, in_=i))
            p.bar()
            p.dve(lambda q, o=TB[:, 3:2048], i=kb[:, 0:2045], s=w(0),
                  a=TA[:, 3:2048]: q.scalar_tensor_tensor(
                  out=o, in0=i, scalar=s, in1=a, op0=ALU.mult, op1=ALU.add))
            p.dve(lambda q, o=TB[:, 0:3], i=TA[:, 0:3]: q.tensor_copy(out=o, in_=i))
            p.bar()
            p.act(lambda q, o=kb, i=TB: q.activation(out=o, in_=i, func=AF.Silu))
            p.bar()
            if ct < 20:
                sqb = t.BG[:, 2048:4096]
                p.act(lambda q, o=sqb, i=kb: q.activation(out=o, in_=i,
                      func=AF.Square))
                p.bar()
                for tc in range(NTC):
                    p.pe(lambda q, o=t.PR[0:1, 0:512],
                         i=sqb[:, tc * 512:(tc + 1) * 512]:
                         q.matmul(o, t.ones_col, i, start=True, stop=True))
                    p.bar()
                    rr = t.RW[0:1, 2048:2560]
                    p.act(lambda q, o=rr: q.activation(
                          out=o, in_=t.PR[0:1, 0:512], func=AF.Sqrt, bias=1e-6))
                    p.bar()
                    p.dve(lambda q, o=rr, i=rr: q.reciprocal(out=o, in_=i))
                    p.bar()
                    if ct < 16:
                        p.dve(lambda q, o=rr, i=rr: q.tensor_scalar(
                              out=o, in0=i, scalar1=0.125, scalar2=None,
                              op0=ALU.mult))
                        p.bar()
                    rrb = t.BG[0:1, 66000:66512]
                    p.act(lambda q, o=rrb, i=rr: q.copy(out=o, in_=i))
                    p.bar()
                    p.pe(lambda q, o=t.PB[:, 0:512], i=rrb:
                         q.matmul(o, t.ones_row, i, start=True, stop=True))
                    p.bar()
                    p.dve(lambda q, o=kb[:, tc * 512:(tc + 1) * 512]:
                          q.tensor_tensor(out=o, in0=o, in1=t.PB[:, 0:512],
                                          op=ALU.mult))
                    p.bar()
            p.st(lambda q, o=sqkv[b, ct, :, :], i=kb: q.dma_start(out=o, in_=i))
            p.bar()
        for c in (range(16) if 'p3' in L1A_PH else []):
            for srow, dcol in ((2048, 0), (0, 512)):
                src = t.BGL[0:16, srow + c * 128: srow + (c + 1) * 128]
                pso = t.PS[1][0:128, 0:16]
                p.pe(lambda q, o=pso, i=src: q.transpose(o, i, t.IDF[0:16, 0:16]))
                p.bar()
                if srow == 2048:
                    gt = t.FS[:, 0:16]
                    p.act(lambda q, o=gt, i=pso: q.copy(out=o, in_=i))
                    p.bar()
                    p.pe(lambda q, o=t.PS[2][0:128, 0:16], i=gt:
                         q.matmul(o, t.UT[:], i, start=True, stop=True))
                    p.bar()
                    cc = t.CCA[:, c * 16:(c + 1) * 16]
                    p.act(lambda q, o=cc: q.activation(
                          out=o, in_=t.PS[2][0:128, 0:16], func=AF.Exp))
                    p.bar()
                    p.dve(lambda q, o=t.CCA[:, 256 + c * 16: 256 + (c + 1) * 16],
                          i=cc: q.reciprocal(out=o, in_=i))
                    p.bar()
                else:
                    p.act(lambda q, o=t.CCA[:, 512 + c * 16: 512 + (c + 1) * 16],
                          i=pso: q.copy(out=o, in_=i))
                    p.bar()
        KFM, VFM, KTOK, VTOK, QF, GSIL, CH5 = 0, 2048, 4096, 6144, 8192, 10240, 12288
        DB = 22528
        Pb = [DB, DB + 128]; PTb = [DB + 256, DB + 384]; RTb = [DB + 512, DB + 640]
        DLo, SBo, OFo = DB + 768, DB + 896, DB + 1024
        TI = DB + 1152
        KAo, VBo, STG = DB + 1536, DB + 1664, DB + 1792
        for h in (range(L1A_NH) if 'p4' in L1A_PH else []):
            kvh = h // 4
            if h % 4 == 0:
                p.ld(lambda q, o=t.BG[:, KFM:KFM + 2048],
                     i=sqkv[b, 16 + kvh, :, :]: q.dma_start(out=o, in_=i))
                p.ld(lambda q, o=t.BG[:, VFM:VFM + 2048],
                     i=sqkv[b, 20 + kvh, :, :]: q.dma_start(out=o, in_=i))
                p.bar()
                for tt in (range(16) if L1A_TR else []):
                    tr_copy(t, t.BG[:, KFM + tt * 128: KFM + tt * 128 + 128],
                            t.BG[:, KTOK + tt * 128: KTOK + tt * 128 + 128], 0)
                    tr_copy(t, t.BG[:, VFM + tt * 128: VFM + tt * 128 + 128],
                            t.BG[:, VTOK + tt * 128: VTOK + tt * 128 + 128], 1)
                p.bar()
            p.ld(lambda q, o=t.BG[:, QF:QF + 2048], i=sqkv[b, h, :, :]:
                 q.dma_start(out=o, in_=i))
            for cq in (range(4) if L1A_GS else []):
                p.ld(lambda q, o=t.BG[:, GSIL + cq * 512: GSIL + (cq + 1) * 512]
                     .rearrange("p (c v) -> p c v", c=4),
                     i=sg[cq * 512:(cq + 1) * 512, h * 128:(h + 1) * 128]
                     .rearrange("(c p) v -> p c v", p=128):
                     q.dma_start(out=o, in_=i))
            p.bar()
            S32 = [t.FS[:, 4096:4224], t.FS[:, 4224:4352]]
            sbf = t.BG[:, SBo:SBo + 128]
            p.dve(lambda q, o=S32[0]: q.memzero(o))
            p.dve(lambda q, o=sbf: q.memzero(o))
            p.bar()
            cco = lambda c: t.CCA[:, c * 16 + h: c * 16 + h + 1]
            ccio = lambda c: t.CCA[:, 256 + c * 16 + h: 256 + c * 16 + h + 1]
            bto = lambda c: t.CCA[:, 512 + c * 16 + h: 512 + c * 16 + h + 1]
            for c in range(L1A_NC):
                kf = t.BG[:, KFM + c * 128: KFM + c * 128 + 128]
                qf = t.BG[:, QF + c * 128: QF + c * 128 + 128]
                ktk = t.BG[:, KTOK + c * 128: KTOK + c * 128 + 128]
                vtk = t.BG[:, VTOK + c * 128: VTOK + c * 128 + 128]
                s1 = t.CCA[:, 768:769]
                p.dve(lambda q, o=s1, a=bto(c), b2=cco(c): q.tensor_tensor(
                      out=o, in0=a, in1=b2, op=ALU.mult))
                p.bar()
                for bi, colap in ((0, cco(c)), (1, s1), (2, ccio(c))):
                    ti = t.BG[:, TI + bi * 128: TI + (bi + 1) * 128]
                    p.dve(lambda q, o=ti, s=colap: q.tensor_scalar(
                          out=o, in0=t.ident_bf, scalar1=s, scalar2=None,
                          op0=ALU.mult))
                    p.bar()
                    p.pe(lambda q, o=t.PB[:, bi * 128:(bi + 1) * 128], i=ti:
                         q.matmul(o, t.ones_full, i, start=True, stop=True))
                    p.bar()
                ccB = t.PB[:, 0:128]; s1B = t.PB[:, 128:256]; cciB = t.PB[:, 256:384]
                p.pe(lambda q, o=t.PS[0][:, 0:128], l=kf, r=kf:
                     q.matmul(o, l, r, start=True, stop=True))
                p.bar()
                v1 = t.FS[:, 4352:4480]
                p.dve(lambda q, o=v1, s=s1: q.tensor_scalar(
                      out=o, in0=t.PS[0][:, 0:128], scalar1=s, scalar2=None,
                      op0=ALU.mult))
                v3 = t.FS[:, 4480:4608]
                p.dve(lambda q, o=v3, s=ccio(c): q.tensor_scalar(
                      out=o, in0=t.PS[0][:, 0:128], scalar1=s, scalar2=None,
                      op0=ALU.mult))
                p.bar()
                p.dve(lambda q, o=v1: q.tensor_tensor(out=o, in0=v1, in1=cciB,
                      op=ALU.mult))
                p.dve(lambda q, o=v3: q.tensor_tensor(out=o, in0=v3, in1=s1B,
                      op=ALU.mult))
                p.bar()
                pn = t.BG[:, Pb[0]:Pb[0] + 128]
                pt = t.BG[:, PTb[0]:PTb[0] + 128]
                p.dve(lambda q, o=pn, a=v1: q.tensor_tensor(out=o, in0=a,
                      in1=t.negtril, op=ALU.mult))
                p.dve(lambda q, o=pt, a=v3: q.tensor_tensor(out=o, in0=a,
                      in1=t.negtriu, op=ALU.mult))
                p.bar()
                rt = t.BG[:, RTb[0]:RTb[0] + 128]
                p.dve(lambda q, o=rt, a=pt: q.tensor_tensor(out=o, in0=a,
                      in1=t.ident_bf, op=ALU.add))
                p.bar()
                cur = 0
                for li in range(6):
                    rtc = t.BG[:, RTb[cur % 2]:RTb[cur % 2] + 128]
                    rtn = t.BG[:, RTb[(cur + 1) % 2]:RTb[(cur + 1) % 2] + 128]
                    pc = t.BG[:, Pb[cur % 2]:Pb[cur % 2] + 128]
                    ptc = t.BG[:, PTb[cur % 2]:PTb[cur % 2] + 128]
                    pnx = t.BG[:, Pb[(cur + 1) % 2]:Pb[(cur + 1) % 2] + 128]
                    ptx = t.BG[:, PTb[(cur + 1) % 2]:PTb[(cur + 1) % 2] + 128]
                    p.pe(lambda q, o=t.PS[2][:, 0:128], l=ptc, r=pc:
                         q.matmul(o, l, r, start=True, stop=True))
                    p.pe(lambda q, o=t.PS[3][:, 0:128], l=pc, r=ptc:
                         q.matmul(o, l, r, start=True, stop=True))
                    p.bar()
                    p.act(lambda q, o=pnx, i=t.PS[2][:, 0:128]: q.copy(out=o, in_=i))
                    p.act(lambda q, o=ptx, i=t.PS[3][:, 0:128]: q.copy(out=o, in_=i))
                    p.bar()
                    p.pe(lambda q, o=t.PS[1][:, 0:128], l=pnx, r=rtc:
                         q.matmul(o, l, r, start=True, stop=True))
                    p.bar()
                    p.dve(lambda q, o=rtn, a=rtc: q.tensor_tensor(
                          out=o, in0=a, in1=t.PS[1][:, 0:128], op=ALU.add))
                    p.bar()
                    cur += 1
                rtf = t.BG[:, RTb[cur % 2]:RTb[cur % 2] + 128]
                ch5 = lambda s: t.BG[:, CH5 + c * 640 + s * 128:
                                     CH5 + c * 640 + (s + 1) * 128]
                ka = t.BG[:, KAo:KAo + 128]
                p.dve(lambda q, o=ka, i=ktk, s=s1: q.tensor_scalar(
                      out=o, in0=i, scalar1=s, scalar2=None, op0=ALU.mult))
                p.bar()
                p.pe(lambda q, o=t.PS[1][:, 0:128], l=ka, r=rtf:
                     q.matmul(o, l, r, start=True, stop=True))
                p.bar()
                p.act(lambda q, o=ch5(0), i=t.PS[1][:, 0:128]: q.copy(out=o, in_=i))
                vb = t.BG[:, VBo:VBo + 128]
                p.dve(lambda q, o=vb, i=vtk, s=bto(c): q.tensor_scalar(
                      out=o, in0=i, scalar1=s, scalar2=None, op0=ALU.mult))
                p.bar()
                p.pe(lambda q, o=t.PS[2][:, 0:128], l=rtf, r=vb:
                     q.matmul(o, l, r, start=True, stop=True))
                p.pe(lambda q, o=t.PS[3][:, 0:128], l=kf, r=qf:
                     q.matmul(o, l, r, start=True, stop=True))
                p.bar()
                p.act(lambda q, o=ch5(1), i=t.PS[2][:, 0:128]: q.copy(out=o, in_=i))
                v5 = t.FS[:, 4352:4480]
                p.dve(lambda q, o=v5, s=ccio(c): q.tensor_scalar(
                      out=o, in0=t.PS[3][:, 0:128], scalar1=s, scalar2=None,
                      op0=ALU.mult))
                p.bar()
                p.dve(lambda q, o=v5: q.tensor_tensor(out=o, in0=v5, in1=ccB,
                      op=ALU.mult))
                p.bar()
                p.dve(lambda q, o=ch5(2), a=v5: q.tensor_tensor(out=o, in0=a,
                      in1=t.utri1, op=ALU.mult))
                ccr = t.CCA[:, 769:770]
                p.dve(lambda q, o=ccr, a=ccio(c): q.tensor_tensor(
                      out=o, in0=a, in1=ccB[:, 127:128], op=ALU.mult))
                p.bar()
                p.dve(lambda q, o=ch5(3), i=ktk, s=ccr: q.tensor_scalar(
                      out=o, in0=i, scalar1=s, scalar2=None, op0=ALU.mult))
                p.dve(lambda q, o=ch5(4), i=qf: q.tensor_tensor(
                      out=o, in0=i, in1=ccB, op=ALU.mult))
                p.act(lambda q, o=t.CCA[:, 784 + c: 785 + c], i=ccB[:, 127:128]:
                      q.copy(out=o, in_=i))
                p.bar()
            for c in (range(L1A_NC) if L1A_SEQ else []):
                ch5 = lambda s: t.BG[:, CH5 + c * 640 + s * 128:
                                     CH5 + c * 640 + (s + 1) * 128]
                p.pe(lambda q, o=t.PS[0][:, 0:128], l=ch5(0), r=sbf:
                     q.matmul(o, l, r, start=True, stop=True))
                p.bar()
                dl = t.BG[:, DLo:DLo + 128]
                p.dve(lambda q, o=dl, a=ch5(1): q.tensor_tensor(
                      out=o, in0=a, in1=t.PS[0][:, 0:128], op=ALU.subtract))
                p.bar()
                p.pe(lambda q, o=t.PS[1][:, 0:128], l=ch5(4), r=sbf:
                     q.matmul(o, l, r, start=True, stop=False))
                p.pe(lambda q, o=t.PS[1][:, 0:128], l=ch5(2), r=dl:
                     q.matmul(o, l, r, start=False, stop=True))
                p.pe(lambda q, o=t.PS[2][:, 0:128], l=ch5(3), r=dl:
                     q.matmul(o, l, r, start=True, stop=True))
                p.bar()
                snew = S32[(c + 1) % 2]
                p.dve(lambda q, o=snew, i=S32[c % 2], s=t.CCA[:, 784 + c: 785 + c]:
                      q.scalar_tensor_tensor(out=o, in0=i, scalar=s,
                      in1=t.PS[2][:, 0:128], op0=ALU.mult, op1=ALU.add))
                sq2 = t.FS[:, 4352:4480]
                p.act(lambda q, o=sq2: q.activation(out=o, in_=t.PS[1][:, 0:128],
                      func=AF.Square))
                p.bar()
                p.act(lambda q, o=sbf, i=snew: q.copy(out=o, in_=i))
                rs = t.CCA[:, 770:771]
                p.dve(lambda q, o=rs, i=sq2: q.reduce_sum(out=o, in_=i, axis=AX.X))
                p.bar()
                p.dve(lambda q, o=rs, i=rs: q.tensor_scalar(
                      out=o, in0=i, scalar1=1.0 / 128, scalar2=None, op0=ALU.mult))
                p.bar()
                p.act(lambda q, o=rs, i=rs: q.activation(out=o, in_=i,
                      func=AF.Sqrt, bias=1e-5))
                p.bar()
                p.dve(lambda q, o=rs, i=rs: q.reciprocal(out=o, in_=i))
                p.bar()
                o1 = t.FS[:, 4480:4608]
                p.dve(lambda q, o=o1, s=rs: q.tensor_scalar(
                      out=o, in0=t.PS[1][:, 0:128], scalar1=s, scalar2=None,
                      op0=ALU.mult))
                p.bar()
                p.dve(lambda q, o=o1: q.tensor_tensor(out=o, in0=o1,
                      in1=t.SM[:, 0:128], op=ALU.mult))
                p.bar()
                of = t.BG[:, OFo:OFo + 128]
                p.dve(lambda q, o=of, a=o1,
                      g=t.BG[:, GSIL + c * 128: GSIL + c * 128 + 128]:
                      q.tensor_tensor(out=o, in0=a, in1=g, op=ALU.mult))
                p.bar()
                stg = t.BG[:, STG:STG + 128]
                tr_copy(t, of, stg, 0)
                p.bar()
                p.st(lambda q, o=sot[h * 128:(h + 1) * 128, c * 128:(c + 1) * 128],
                     i=stg: q.dma_start(out=o, in_=i))
                p.bar()
        if 'p5' not in L1A_PH:
            continue
        for hd in range(16):
            p.ld(lambda q, o=t.BG[:, hd * 2048:(hd + 1) * 2048],
                 i=sot[hd * 128:(hd + 1) * 128, :]: q.dma_start(out=o, in_=i))
        p.bar()
        for ot in range(16):
            load_wT(t, ow[ot * 128:(ot + 1) * 128, :], 33280, stride=128)
            for tc in range(NTC):
                for hd in range(16):
                    p.pe(lambda q, o=t.PS[3][:, 0:512],
                         l=t.BG[:, 33280 + hd * 128: 33280 + hd * 128 + 128],
                         r=t.BG[:, hd * 2048 + tc * 512: hd * 2048 + tc * 512 + 512],
                         s=(hd == 0), e2=(hd == 15):
                         q.matmul(o, l, r, start=s, stop=e2))
                p.bar()
                xo = t.BG[:, 66000:66512]
                p.ld(lambda q, o=xo, i=t.xfm_src[b, ot, :, tc * 512:(tc + 1) * 512]:
                     q.dma_start(out=o, in_=i))
                p.bar()
                stg = t.BG[:, 66512:67024]
                p.dve(lambda q, o=stg, a=xo: q.tensor_tensor(
                      out=o, in0=a, in1=t.PS[3][:, 0:512], op=ALU.add))
                p.bar()
                p.st(lambda q, o=t.xfm_dst[b, ot, :, tc * 512:(tc + 1) * 512],
                     i=stg: q.dma_start(out=o, in_=i))
                p.bar()


def gather_lut(t, idx_fp_ap, out_ap, scr):
    p = t.p
    F = idx_fp_ap.shape[1]
    acc1 = t.BG[:, scr:scr + F]
    tmp0 = t.BG[:, scr + F:scr + 2 * F]
    tmp1 = t.BG[:, scr + 2 * F:scr + 3 * F]
    lut = lambda i: t.SM[:, 128 + i:128 + i + 1]
    p.dve(lambda q, o=out_ap, i=idx_fp_ap, s=lut(0): q.tensor_scalar(
          out=o, in0=i, scalar1=0.0, scalar2=s, op0=ALU.is_equal, op1=ALU.mult))
    p.pool(lambda q, o=acc1, i=idx_fp_ap, s=lut(1): q.tensor_scalar(
           out=o, in0=i, scalar1=1.0, scalar2=s, op0=ALU.is_equal, op1=ALU.mult))
    p.bar()
    for k in range(1, 8):
        p.dve(lambda q, o=tmp0, i=idx_fp_ap, v=float(2 * k), s=lut(2 * k):
              q.tensor_scalar(out=o, in0=i, scalar1=v, scalar2=s,
                              op0=ALU.is_equal, op1=ALU.mult))
        p.pool(lambda q, o=tmp1, i=idx_fp_ap, v=float(2 * k + 1), s=lut(2 * k + 1):
               q.tensor_scalar(out=o, in0=i, scalar1=v, scalar2=s,
                               op0=ALU.is_equal, op1=ALU.mult))
        p.bar()
        p.dve(lambda q, o=out_ap, a=tmp0: q.tensor_tensor(out=o, in0=o, in1=a,
              op=ALU.add))
        p.pool(lambda q, o=acc1, a=tmp1: q.tensor_tensor(out=o, in0=o, in1=a,
               op=ALU.add))
        p.bar()
    p.dve(lambda q, o=out_ap, a=acc1: q.tensor_tensor(out=o, in0=o, in1=a,
          op=ALU.add))
    p.bar()


def load_idx_fp(t, idx_rows_ap, F):
    p = t.p
    off = 0
    while off < F:
        w = min(2048, F - off)
        p.ld(lambda q, o=t.I3[:, 0:w], i=idx_rows_ap[:, off:off + w]:
             q.dma_start(out=o, in_=i))
        p.bar()
        p.dve(lambda q, o=t.FS[:, off:off + w], i=t.I3[:, 0:w]:
              q.tensor_copy(out=o, in_=i))
        p.bar()
        off += w


def build_l1f(t, final):
    p = t.p
    gi = t.inp['L1_ffn_gate_idx']
    ui = t.inp['L1_ffn_up_idx']
    di = t.inp['L1_ffn_down_idx']
    sh = t.internal['s_h']
    swg = t.internal['s_wgt']
    swu = t.internal['s_wut']
    swd = t.internal['s_wdt']
    p.ld(lambda q, o=t.SM[0:1, 144:160],
         i=t.inp['L1_ffn_lut'].rearrange("(a b) -> a b", a=1):
         q.dma_start(out=o, in_=i))
    p.bar()
    lrow = t.BG[0:1, 66000:66016]
    p.act(lambda q, o=lrow, i=t.SM[0:1, 144:160]: q.copy(out=o, in_=i))
    p.bar()
    p.pe(lambda q, o=t.PB[:, 0:16], i=lrow:
         q.matmul(o, t.ones_row, i, start=True, stop=True))
    p.bar()
    p.act(lambda q, o=t.SM[:, 128:144], i=t.PB[:, 0:16]: q.copy(out=o, in_=i))
    p.bar()
    for it in range(43):
        for idx_ap, dst in ((gi, swg), (ui, swu)):
            load_idx_fp(t, idx_ap[it * 128:(it + 1) * 128, :], 2048)
            gw = t.BG[:, 0:2048]
            gather_lut(t, t.FS[:, 0:2048], gw, 8192)
            for dt in range(16):
                tr_copy(t, gw[:, dt * 128:(dt + 1) * 128],
                        t.BG[:, 2048 + dt * 128: 2048 + dt * 128 + 128], dt % 2)
            p.bar()
            p.st(lambda q, o=dst[it], i=t.BG[:, 2048:4096]:
                 q.dma_start(out=o, in_=i))
            p.bar()
    for b in range(B):
        make_xh(t, b, 0, 32768)
        xh = []
        for dt in range(DT):
            row = []
            for tc in range(NTC):
                row.append(t.BG[:, dt * 2048 + tc * 512:
                               dt * 2048 + tc * 512 + 512])
            xh.append(row)
        for it in range(43):
            p.ld(lambda q, o=t.BG[:, 32768:34816], i=swg[it]:
                 q.dma_start(out=o, in_=i))
            p.ld(lambda q, o=t.BG[:, 34816:36864], i=swu[it]:
                 q.dma_start(out=o, in_=i))
            p.bar()
            for tc in range(NTC):
                for dt in range(DT):
                    p.pe(lambda q, o=t.PS[0][:, 0:512],
                         l=t.BG[:, 32768 + dt * 128: 32768 + dt * 128 + 128],
                         r=xh[dt][tc], s=(dt == 0), e2=(dt == 15):
                         q.matmul(o, l, r, start=s, stop=e2))
                for dt in range(DT):
                    p.pe(lambda q, o=t.PS[1][:, 0:512],
                         l=t.BG[:, 34816 + dt * 128: 34816 + dt * 128 + 128],
                         r=xh[dt][tc], s=(dt == 0), e2=(dt == 15):
                         q.matmul(o, l, r, start=s, stop=e2))
                p.bar()
                sg = t.BG[:, 36864:37376]
                p.act(lambda q, o=sg: q.activation(out=o, in_=t.PS[0][:, 0:512],
                      func=AF.Silu))
                p.bar()
                htile = t.BG[:, 37376:37888]
                p.dve(lambda q, o=htile, a=sg: q.tensor_tensor(out=o, in0=a,
                      in1=t.PS[1][:, 0:512], op=ALU.mult))
                p.bar()
                p.st(lambda q, o=sh[it, :, b * 2048 + tc * 512:
                                    b * 2048 + (tc + 1) * 512], i=htile:
                     q.dma_start(out=o, in_=i))
                p.bar()
    for ot in range(16):
        load_idx_fp(t, di[ot * 128:(ot + 1) * 128, :], 5504)
        dw = t.BG[:, 0:5504]
        gather_lut(t, t.FS[:, 0:5504], dw, 16384)
        for it in range(43):
            tr_copy(t, dw[:, it * 128:(it + 1) * 128],
                    t.BG[:, 5504 + it * 128: 5504 + it * 128 + 128], it % 2)
        p.bar()
        p.st(lambda q, o=swd[ot], i=t.BG[:, 5504:11008]:
             q.dma_start(out=o, in_=i))
        p.bar()
    y = t.inp['y'] if final else None
    for b in range(B):
        for tc in range(NTC):
            for it in range(43):
                p.ld(lambda q, o=t.BG[:, 16384 + it * 512: 16384 + it * 512 + 512],
                     i=sh[it, :, b * 2048 + tc * 512: b * 2048 + (tc + 1) * 512]:
                     q.dma_start(out=o, in_=i))
            p.bar()
            for ot in range(16):
                p.ld(lambda q, o=t.BG[:, 5504:11008], i=swd[ot]:
                     q.dma_start(out=o, in_=i))
                p.bar()
                for it in range(43):
                    p.pe(lambda q, o=t.PS[2][:, 0:512],
                         l=t.BG[:, 5504 + it * 128: 5504 + it * 128 + 128],
                         r=t.BG[:, 16384 + it * 512: 16384 + it * 512 + 512],
                         s=(it == 0), e2=(it == 42):
                         q.matmul(o, l, r, start=s, stop=e2))
                p.bar()
                xo = t.BG[:, 66016:66528]
                p.ld(lambda q, o=xo,
                     i=t.xfm_src[b, ot, :, tc * 512:(tc + 1) * 512]:
                     q.dma_start(out=o, in_=i))
                p.bar()
                if final:
                    ofm = t.FS[:, 0:512]
                    p.dve(lambda q, o=ofm, a=xo: q.tensor_tensor(out=o, in0=a,
                          in1=t.PS[2][:, 0:512], op=ALU.add))
                    p.bar()
                    for j in range(4):
                        pso = t.PS[3][0:128, 0:128]
                        p.pe(lambda q, o=pso, i=ofm[:, j * 128:(j + 1) * 128]:
                             q.transpose(o, i, t.IDF[:]))
                        p.bar()
                        stg = t.FS[:, 512 + (j % 2) * 128: 512 + (j % 2) * 128 + 128]
                        p.act(lambda q, o=stg, i=pso: q.copy(out=o, in_=i))
                        p.bar()
                        p.st(lambda q, o=y[b, tc * 512 + j * 128:
                                           tc * 512 + (j + 1) * 128,
                                           ot * 128:(ot + 1) * 128], i=stg:
                             q.dma_start(out=o, in_=i))
                        p.bar()
                else:
                    stg = t.BG[:, 66528:67040]
                    p.dve(lambda q, o=stg, a=xo: q.tensor_tensor(out=o, in0=a,
                          in1=t.PS[2][:, 0:512], op=ALU.add))
                    p.bar()
                    p.st(lambda q, o=t.xfm_dst[b, ot, :, tc * 512:(tc + 1) * 512],
                         i=stg: q.dma_start(out=o, in_=i))
                    p.bar()



LAUNCHES = [
    (['l0a', 'l0f', 'l1a', 'l1f'], True, True),
]


def kernel(**inputs):
    ch = host_consts()
    state = None
    out = None
    for parts, first, last in LAUNCHES:
        nc, stack, t = build_program(parts, first, last)
        in_maps = []
        for c in range(8):
            m = {}
            for nm in t.inp.keys():
                if nm in ('y', 'xfm_out'):
                    continue
                if nm == 'xin':
                    m[nm] = np.ascontiguousarray(inputs['L0_attn_x'][2 * c:2 * c + 2])
                elif nm == 'xfm_in':
                    m[nm] = state[c]
                elif nm in ch:
                    m[nm] = ch[nm]
                else:
                    m[nm] = np.ascontiguousarray(inputs[nm])
            in_maps.append(m)
        res = run_bass_kernel_spmd(nc, in_maps, core_ids=list(range(8)))
        if last:
            out = np.concatenate([res.results[c]['y'] for c in range(8)], axis=0)
        else:
            state = [res.results[c]['xfm_out'] for c in range(8)]
        stack.close()
    return out


# version 03
# speedup vs baseline: 1.0840x; 1.0808x over previous
import numpy as np
import ml_dtypes
import concourse.bass as bass
import concourse.mybir as mybir
from concourse.bass_utils import run_bass_kernel_spmd
from contextlib import ExitStack

F32 = mybir.dt.float32
BF16 = mybir.dt.bfloat16
I32 = mybir.dt.int32
AF = mybir.ActivationFunctionType
ALU = mybir.AluOpType
AX = mybir.AxisListType
BF = ml_dtypes.bfloat16

B = 2
S = 2048
D = 2048
DT = 16
NTC = 4
NTT = 16

ENG = ['pe', 'act', 'dve', 'pool', 'sp']


class Prog:
    def __init__(self, nc):
        self.nc = nc
        self.ops = {e: [] for e in ENG}
        self.cnt = {e: 0 for e in ENG}
        self.dcnt = {e: 0 for e in ENG}
        self.pend = {e: {} for e in ENG}

    def _add(self, eng, fn, is_dma):
        w = self.pend[eng]
        self.pend[eng] = {}
        self.ops[eng].append((w, fn, is_dma))
        if is_dma:
            self.dcnt[eng] += 16
        else:
            self.cnt[eng] += 1

    def pe(self, fn): self._add('pe', fn, False)
    def act(self, fn): self._add('act', fn, False)
    def dve(self, fn): self._add('dve', fn, False)
    def pool(self, fn): self._add('pool', fn, False)
    def ld(self, fn): self._add('sp', fn, True)
    def st(self, fn): self._add('pool', fn, True)

    def bar(self):
        snap = {}
        for e in ENG:
            if self.cnt[e]:
                snap[('c', e)] = self.cnt[e]
            if self.dcnt[e]:
                snap[('d', e)] = self.dcnt[e]
        for e in ENG:
            d = self.pend[e]
            for k, v in snap.items():
                if d.get(k, 0) < v:
                    d[k] = v

    def emit(self, stack):
        nc = self.nc
        self.bar()
        csem = {e: stack.enter_context(nc.semaphore("c_" + e)) for e in ENG}
        dsem = {e: stack.enter_context(nc.semaphore("d_" + e))
                for e in ENG if self.dcnt[e]}

        def runner(e):
            def go(q):
                last = {}
                for w, fn, is_dma in self.ops[e]:
                    for k in sorted(w.keys()):
                        v = w[k]
                        if last.get(k, 0) >= v:
                            continue
                        q.wait_ge(csem[k[1]] if k[0] == 'c' else dsem[k[1]], v)
                        last[k] = v
                    instr = fn(q)
                    if is_dma:
                        instr.then_inc(dsem[e], 16)
                    else:
                        instr.then_inc(csem[e], 1)
                w = self.pend[e]
                for k in sorted(w.keys()):
                    v = w[k]
                    if last.get(k, 0) >= v:
                        continue
                    q.wait_ge(csem[k[1]] if k[0] == 'c' else dsem[k[1]], v)
                    last[k] = v
            return go

        blk = stack.enter_context(nc.Block())
        blk.tensor(runner('pe'))
        blk.scalar(runner('act'))
        blk.vector(runner('dve'))
        blk.gpsimd(runner('pool'))
        blk.sync(runner('sp'))


def host_consts():
    c = {}
    ident = np.eye(128, dtype=np.float32)
    c['c_identf'] = ident
    cb = np.zeros((128, 1024), dtype=np.float32)
    cb[:, 0:128] = ident
    cb[:, 128] = 1.0
    cb[:, 129:257] = 1.0
    p = np.arange(128)[:, None]
    f = np.arange(128)[None, :]
    cb[:, 257:385] = (p <= f).astype(np.float32)
    cb[:, 385:513] = -(f < p).astype(np.float32)
    cb[:, 513:641] = -(f > p).astype(np.float32)
    c['c_constb'] = cb.astype(BF)
    c['c_ut16'] = ((p <= f).astype(np.float32) / 16.0)
    dm = np.zeros((128, 4 * 512), dtype=np.float32)
    qf = np.arange(512)[None, :]
    for j in range(4):
        dm[:, j * 512:(j + 1) * 512] = (qf >= (j * 128 + p)).astype(np.float32)
    c['c_dmask'] = dm.astype(BF)
    hd = 64
    inv = 1.0 / 10000.0 ** (np.arange(0, hd, 2) / hd)
    fr = np.outer(np.arange(S), inv)
    c['c_cosrep'] = np.tile(np.cos(fr), (1, 8)).astype(np.float32).astype(BF)
    c['c_sinrep'] = np.tile(np.sin(fr), (1, 8)).astype(np.float32).astype(BF)
    c['c_scal'] = np.tile(np.array([[1e-5, 1e-6, 0.125, 1.0, 0.0, 0.0, 0.0,
                           0.0]], dtype=np.float32), (128, 1))
    return c


class Ctx:
    pass


def build_program(parts, first, last):
    nc = bass.Bass("TRN2", target_bir_lowering=False)
    stack = ExitStack()
    t = Ctx()
    t.nc = nc

    def dram(name, shape, dtype, kind):
        return nc.dram_tensor(name, list(shape), dtype, kind=kind).ap()

    t.inp = {}
    if first:
        t.inp['xin'] = dram('xin', (B, S, D), F32, 'ExternalInput')
    else:
        t.inp['xfm_in'] = dram('xfm_in', (B, DT, 128, S), BF16, 'ExternalInput')
    if last:
        t.inp['y'] = dram('y', (B, S, D), F32, 'ExternalOutput')
    else:
        t.inp['xfm_out'] = dram('xfm_out', (B, DT, 128, S), BF16, 'ExternalOutput')

    wnames = {
        'l0a': [('L0_attn_wq', (2048, 2048), F32), ('L0_attn_wk', (512, 2048), F32),
                ('L0_attn_wv', (512, 2048), F32), ('L0_attn_wo', (2048, 2048), F32)],
        'l0f': [('L0_ffn_Wr', (8, 2048), F32), ('L0_ffn_Wg', (8, 2048, 1024), F32),
                ('L0_ffn_Wu', (8, 2048, 1024), F32), ('L0_ffn_Wd', (8, 1024, 2048), F32),
                ('L0_ffn_Wg_s', (2048, 1024), F32), ('L0_ffn_Wu_s', (2048, 1024), F32),
                ('L0_ffn_Wd_s', (1024, 2048), F32)],
        'l1a': [('L1_attn_qkv_w', (3072, 2048), F32), ('L1_attn_g_w', (2048, 2048), F32),
                ('L1_attn_b_w', (16, 2048), F32), ('L1_attn_gk_w', (16, 2048), F32),
                ('L1_attn_conv_w', (3072, 4), F32), ('L1_attn_out_norm_w', (128,), F32),
                ('L1_attn_o_w', (2048, 2048), F32)],
        'l1f': [('L1_ffn_lut', (16,), F32), ('L1_ffn_gate_idx', (5504, 2048), I32),
                ('L1_ffn_up_idx', (5504, 2048), I32),
                ('L1_ffn_down_idx', (2048, 5504), I32)],
    }
    for pt in parts:
        for nm, shp, dt in wnames[pt]:
            t.inp[nm] = dram(nm, shp, dt, 'ExternalInput')

    t.chost = host_consts()
    for nm, arr in t.chost.items():
        t.inp[nm] = dram(nm, arr.shape, BF16 if arr.dtype == BF else F32,
                         'ExternalInput')

    internal = {}
    if first and last:
        internal['s_xfm'] = dram('s_xfm', (B, DT, 128, S), BF16, 'Internal')
    if 'l0a' in parts:
        internal['s_qt'] = dram('s_qt', (2048, 2048), BF16, 'Internal')
    if 'l1a' in parts:
        internal['s_g'] = dram('s_g', (2048, 2048), BF16, 'Internal')
        internal['s_ot'] = dram('s_ot', (2048, 2048), BF16, 'Internal')
        internal['s_qkv'] = dram('s_qkv', (B, 24, 128, S), BF16, 'Internal')
    if 'l0f' in parts:
        internal['s_acc'] = dram('s_acc', (B, DT, 128, S), BF16, 'Internal')
    if 'l1f' in parts:
        internal['s_h'] = dram('s_h', (43, 128, B * S), BF16, 'Internal')
        internal['s_wdt'] = dram('s_wdt', (16, 128, 5504), BF16, 'Internal')
        internal['s_wgt'] = dram('s_wgt', (43, 128, 2048), BF16, 'Internal')
        internal['s_wut'] = dram('s_wut', (43, 128, 2048), BF16, 'Internal')
    t.internal = internal

    if first and last:
        t.xfm_src = internal['s_xfm']
        t.xfm_dst = internal['s_xfm']
    elif first:
        t.xfm_src = t.inp['xfm_out']
        t.xfm_dst = t.inp['xfm_out']
    elif last:
        t.xfm_src = t.inp['xfm_in']
        t.xfm_dst = None
    else:
        t.xfm_src = t.inp['xfm_in']
        t.xfm_dst = t.inp['xfm_out']

    sb = lambda nm, shp, dt: stack.enter_context(nc.sbuf_tensor(nm, list(shp), dt))
    t.BG = sb('BG', (128, 69600), BF16)
    t.FS = sb('FS', (128, 5504), F32)
    t.CCA = sb('CCA', (128, 800), F32)
    t.BGL = sb('BGL', (16, 4096), F32)
    t.RW = sb('RW', (1, 2560), F32)
    t.SM = sb('SM', (128, 1024), F32)
    t.CB = sb('CB', (128, 1024), BF16)
    t.UT = sb('UT', (128, 128), F32)
    t.IDF = sb('IDF', (128, 128), F32)
    t.I3 = sb('I3', (128, 2048), I32)
    t.CSC = sb('CSC', (128, 8), F32)
    for i, v in enumerate([1e-5, 1e-6, 0.125, 1.0]):
        nc.const_aps.register(F32, v, t.CSC[:, i:i + 1])
    ps = lambda nm, shp, dt: stack.enter_context(nc.psum_tensor(nm, list(shp), dt))
    t.PS = [ps('PS%d' % i, (128, 512), F32) for i in range(4)]
    t.PR = ps('PR', (128, 512), F32)
    t.PB = ps('PB', (128, 512), F32)
    t.PT2 = [ps('PTa', (128, 512), BF16), ps('PTb', (128, 512), BF16)]

    p = Prog(nc)
    t.p = p

    t.ident_bf = t.CB[:, 0:128]
    t.ones_col = t.CB[:, 128:129]
    t.ones_row = t.CB[0:1, 129:257]
    t.ones_full = t.CB[:, 129:257]
    t.utri1 = t.CB[:, 257:385]
    t.negtril = t.CB[:, 385:513]
    t.negtriu = t.CB[:, 513:641]

    def _c(q, a, b):
        return q.dma_start(out=a, in_=b)
    p.ld(lambda q, a=t.CB[:], b=t.inp['c_constb'][:]: _c(q, a, b))
    p.ld(lambda q, a=t.UT[:], b=t.inp['c_ut16'][:]: _c(q, a, b))
    p.ld(lambda q, a=t.IDF[:], b=t.inp['c_identf'][:]: _c(q, a, b))
    p.ld(lambda q, a=t.CSC[:], b=t.inp['c_scal'][:]: _c(q, a, b))
    p.bar()

    if first:
        build_convert_in(t)
    for pt in parts:
        {'l0a': build_l0a, 'l0f': build_l0f, 'l1a': build_l1a,
         'l1f': lambda tt: build_l1f(tt, final=last)}[pt](t)
    if last and 'l1f' not in parts:
        build_convert_out(t)

    p.emit(stack)
    return nc, stack, t



def tr_copy(t, src_ap, dst_ap, pb_idx):
    p = t.p
    pp, ff = src_ap.shape[0], src_ap.shape[1]
    pso = t.PT2[pb_idx][0:ff, 0:pp]
    p.pe(lambda q, o=pso, i=src_ap, idn=t.ident_bf[0:pp, 0:pp]:
         q.transpose(o, i, idn))
    p.bar()
    p.act(lambda q, o=dst_ap, i=pso: q.copy(out=o, in_=i))


def load_wT(t, w_rows_ap, wt_off, stride=128):
    p = t.p
    p.ld(lambda q, o=t.FS[:, 0:2048], i=w_rows_ap: q.dma_start(out=o, in_=i))
    p.bar()
    cb = t.BG[:, 67000:69048]
    p.dve(lambda q, o=cb, i=t.FS[:, 0:2048]: q.tensor_copy(out=o, in_=i))
    p.bar()
    for dt in range(16):
        tr_copy(t, cb[:, dt * 128:(dt + 1) * 128],
                t.BG[:, wt_off + dt * stride: wt_off + dt * stride + 128], dt % 2)
    p.bar()


def make_xh(t, b, xh_off, sq_off):
    p = t.p
    src = t.xfm_src
    for tc in range(NTC):
        xh = []
        for dt in range(DT):
            xh.append(t.BG[:, xh_off + dt * 2048 + tc * 512:
                           xh_off + dt * 2048 + tc * 512 + 512])
        for dt in range(DT):
            p.ld(lambda q, o=xh[dt], i=src[b, dt, :, tc * 512:(tc + 1) * 512]:
                 q.dma_start(out=o, in_=i))
        p.bar()
        for dt in range(DT):
            sq = t.BG[:, sq_off + dt * 512: sq_off + (dt + 1) * 512]
            p.act(lambda q, o=sq, i=xh[dt]:
                  q.activation(out=o, in_=i, func=AF.Square))
        p.bar()
        for dt in range(DT):
            p.pe(lambda q, o=t.PR[0:1, 0:512], i=xh[dt], s=(dt == 0), e=(dt == 15):
                 q.matmul(o, t.ones_col, i, start=s, stop=e))
        for dt in range(DT):
            sq = t.BG[:, sq_off + dt * 512: sq_off + (dt + 1) * 512]
            p.pe(lambda q, o=t.PS[2][0:1, 0:512], i=sq, s=(dt == 0), e=(dt == 15):
                 q.matmul(o, t.ones_col, i, start=s, stop=e))
        p.bar()
        m = t.RW[0:1, 0:512]
        v2 = t.RW[0:1, 512:1024]
        r = t.RW[0:1, 1024:1536]
        mr = t.RW[0:1, 1536:2048]
        p.dve(lambda q, o=m: q.tensor_scalar(out=o, in0=t.PR[0:1, 0:512],
              scalar1=1.0 / D, scalar2=None, op0=ALU.mult))
        p.dve(lambda q, o=v2: q.tensor_scalar(out=o, in0=t.PS[2][0:1, 0:512],
              scalar1=1.0 / D, scalar2=None, op0=ALU.mult))
        p.bar()
        p.dve(lambda q, o=r, a=m: q.scalar_tensor_tensor(out=o, in0=a, scalar=-1.0,
              in1=a, op0=ALU.mult, op1=ALU.mult))
        p.bar()
        p.dve(lambda q, o=r, a=r, b2=v2: q.tensor_tensor(out=o, in0=a, in1=b2,
              op=ALU.add))
        p.bar()
        p.act(lambda q, o=r, i=r: q.activation(out=o, in_=i, func=AF.Sqrt,
              bias=1e-5))
        p.bar()
        p.dve(lambda q, o=r, i=r: q.reciprocal(out=o, in_=i))
        p.bar()
        p.dve(lambda q, o=mr, a=m, b2=r: q.scalar_tensor_tensor(out=o, in0=a,
              scalar=-1.0, in1=b2, op0=ALU.mult, op1=ALU.mult))
        p.bar()
        rb16 = t.BG[0:1, 69048:69560]
        mrb16 = t.BG[0:1, 66000:66512]
        p.act(lambda q, o=rb16, i=r: q.copy(out=o, in_=i))
        p.act(lambda q, o=mrb16, i=mr: q.copy(out=o, in_=i))
        p.bar()
        p.pe(lambda q, o=t.PB[:, 0:512], i=rb16:
             q.matmul(o, t.ones_row, i, start=True, stop=True))
        p.pe(lambda q, o=t.PS[3][:, 0:512], i=mrb16:
             q.matmul(o, t.ones_row, i, start=True, stop=True))
        p.bar()
        for dt in range(DT):
            p.dve(lambda q, o=xh[dt], i=xh[dt]: q.tensor_tensor(out=o, in0=i,
                  in1=t.PB[:, 0:512], op=ALU.mult))
        p.bar()
        for dt in range(DT):
            p.dve(lambda q, o=xh[dt], i=xh[dt]: q.tensor_tensor(out=o, in0=i,
                  in1=t.PS[3][:, 0:512], op=ALU.add))
        p.bar()


def build_convert_in(t):
    p = t.p
    dst = t.internal['s_xfm'] if 's_xfm' in t.internal else t.inp['xfm_out']
    t.xfm_src = dst
    if t.xfm_dst is not None:
        t.xfm_dst = t.xfm_dst
    x = t.inp['xin']
    for b in range(B):
        for tt in range(NTT):
            p.ld(lambda q, o=t.FS[:, 0:2048], i=x[b, tt * 128:(tt + 1) * 128, :]:
                 q.dma_start(out=o, in_=i))
            p.bar()
            cb = t.BG[:, 67000:69048]
            p.dve(lambda q, o=cb, i=t.FS[:, 0:2048]: q.tensor_copy(out=o, in_=i))
            p.bar()
            for dt in range(DT):
                stg = t.BG[:, 69048 + (dt % 2) * 128: 69048 + (dt % 2) * 128 + 128]
                tr_copy(t, cb[:, dt * 128:(dt + 1) * 128], stg, dt % 2)
                p.bar()
                p.st(lambda q, o=dst[b, dt, :, tt * 128:(tt + 1) * 128], i=stg:
                     q.dma_start(out=o, in_=i))
            p.bar()


def build_convert_out(t):
    p = t.p
    src = t.xfm_src
    y = t.inp['y']
    for b in range(B):
        for dt in range(DT):
            cb = t.BG[:, 0:2048]
            p.ld(lambda q, o=cb, i=src[b, dt, :, :]: q.dma_start(out=o, in_=i))
            p.bar()
            for tt in range(NTT):
                fsb = t.FS[:, 0:2048]
                p.dve(lambda q, o=fsb[:, (tt % 2) * 128:(tt % 2) * 128 + 128],
                      i=cb[:, tt * 128:(tt + 1) * 128]: q.tensor_copy(out=o, in_=i))
                p.bar()
                pso = t.PS[tt % 2][0:128, 0:128]
                p.pe(lambda q, o=pso, i=fsb[:, (tt % 2) * 128:(tt % 2) * 128 + 128]:
                     q.transpose(o, i, t.IDF[:]))
                p.bar()
                stg = t.FS[:, 2048 + (tt % 2) * 128: 2048 + (tt % 2) * 128 + 128]
                p.act(lambda q, o=stg, i=pso: q.copy(out=o, in_=i))
                p.bar()
                p.st(lambda q, o=y[b, tt * 128:(tt + 1) * 128,
                                   dt * 128:(dt + 1) * 128], i=stg:
                     q.dma_start(out=o, in_=i))
            p.bar()



def build_l0f(t):
    p = t.p
    XH = 0
    HT = 32768
    WG = 49152
    WU = 51200
    SG = 53248
    CBB = 53760
    WR = 55808
    RT1 = 55936
    SQo = 57000
    wg = t.inp['L0_ffn_Wg']
    wu = t.inp['L0_ffn_Wu']
    wd = t.inp['L0_ffn_Wd']
    acc = t.internal['s_acc']

    for b in range(B):
        make_xh(t, b, XH, SQo)
        xh = []
        for dt in range(DT):
            row = []
            for tc in range(NTC):
                row.append(t.BG[:, XH + dt * 2048 + tc * 512:
                               XH + dt * 2048 + tc * 512 + 512])
            xh.append(row)
        p.ld(lambda q, o=t.FS[0:8, 0:2048], i=t.inp['L0_ffn_Wr'][:]:
             q.dma_start(out=o, in_=i))
        p.bar()
        cb8 = t.BG[0:8, 67000:69048]
        p.dve(lambda q, o=cb8, i=t.FS[0:8, 0:2048]: q.tensor_copy(out=o, in_=i))
        p.bar()
        for dt in range(16):
            tr_copy(t, cb8[:, dt * 128:(dt + 1) * 128],
                    t.BG[:, WR + dt * 8: WR + dt * 8 + 8], dt % 2)
        p.bar()
        for tc in range(NTC):
            for dt in range(DT):
                p.pe(lambda q, o=t.PS[0][0:8, 0:512],
                     w=t.BG[:, WR + dt * 8: WR + dt * 8 + 8], i=xh[dt][tc],
                     s=(dt == 0), e=(dt == 15):
                     q.matmul(o, w, i, start=s, stop=e))
            p.bar()
            lg = t.FS[0:8, 2048 + tc * 512: 2048 + tc * 512 + 512]
            p.act(lambda q, o=lg, i=t.PS[0][0:8, 0:512]: q.copy(out=o, in_=i))
            p.bar()
        for tt in range(NTT):
            tc, j = divmod(tt, 4)
            srcl = t.FS[0:8, 2048 + tc * 512 + j * 128:
                        2048 + tc * 512 + j * 128 + 128]
            pso = t.PS[1][0:128, 0:8]
            p.pe(lambda q, o=pso, i=srcl: q.transpose(o, i, t.IDF[0:8, 0:8]))
            p.bar()
            lgt = t.SM[:, 368 + tt * 8: 368 + tt * 8 + 8]
            p.act(lambda q, o=lgt, i=pso: q.copy(out=o, in_=i))
            p.bar()
        for tt in range(NTT):
            lgt = t.SM[:, 368 + tt * 8: 368 + tt * 8 + 8]
            m1 = t.SM[:, 496:497]; m2 = t.SM[:, 497:498]
            dcol = t.SM[:, 498:499]; w1 = t.SM[:, 499:500]; w2 = t.SM[:, 500:501]
            msk1 = t.SM[:, 502:510]; mskd = t.SM[:, 510:518]; msk2 = t.SM[:, 518:526]
            p.dve(lambda q, o=m1, i=lgt: q.reduce_max(out=o, in_=i, axis=AX.X))
            p.bar()
            p.dve(lambda q, o=msk1, i=lgt, m=m1: q.tensor_scalar(out=o, in0=i,
                  scalar1=m, scalar2=None, op0=ALU.is_ge))
            p.bar()
            p.dve(lambda q, o=mskd, i=msk1, lg=lgt: q.scalar_tensor_tensor(out=o,
                  in0=i, scalar=-1e30, in1=lg, op0=ALU.mult, op1=ALU.add))
            p.bar()
            p.dve(lambda q, o=m2, i=mskd: q.reduce_max(out=o, in_=i, axis=AX.X))
            p.bar()
            p.dve(lambda q, o=msk2, i=mskd, m=m2: q.tensor_scalar(out=o, in0=i,
                  scalar1=m, scalar2=None, op0=ALU.is_ge))
            p.dve(lambda q, o=dcol, a=m2, b2=m1: q.tensor_tensor(out=o, in0=a,
                  in1=b2, op=ALU.subtract))
            p.bar()
            p.act(lambda q, o=dcol, i=dcol: q.activation(out=o, in_=i, func=AF.Exp))
            p.bar()
            p.dve(lambda q, o=w1, i=dcol: q.tensor_scalar(out=o, in0=i, scalar1=1.0,
                  scalar2=None, op0=ALU.add))
            p.bar()
            p.dve(lambda q, o=w1, i=w1: q.reciprocal(out=o, in_=i))
            p.bar()
            p.dve(lambda q, o=w2, i=w1: q.tensor_scalar(out=o, in0=i, scalar1=-1.0,
                  scalar2=1.0, op0=ALU.mult, op1=ALU.add))
            p.bar()
            comb = t.SM[:, 240 + tt * 8: 240 + tt * 8 + 8]
            p.dve(lambda q, o=comb, i=msk2, w=w2: q.tensor_scalar(out=o, in0=i,
                  scalar1=w, scalar2=None, op0=ALU.mult))
            p.bar()
            p.dve(lambda q, o=comb, i=msk1, w=w1: q.scalar_tensor_tensor(out=o,
                  in0=i, scalar=w, in1=o, op0=ALU.mult, op1=ALU.add))
            p.bar()
        WGB = [WG, 60000]
        WUB = [WU, 62048]
        WDB = [49152, 50176]
        SGS = [SG, 64096]
        T8 = 64608

        def gu_loads(e2, it2):
            for half in range(2):
                if e2 < 8:
                    srcw = (wg if half == 0 else wu)[e2][:, it2 * 128:(it2 + 1) * 128]
                else:
                    srcw = (t.inp['L0_ffn_Wg_s'] if half == 0 else
                            t.inp['L0_ffn_Wu_s'])[:, it2 * 128:(it2 + 1) * 128]
                for qtr in range(4):
                    sl = srcw[qtr * 512:(qtr + 1) * 512, :]
                    dst3 = t.FS[:, half * 2048 + qtr * 512:
                                half * 2048 + (qtr + 1) * 512].rearrange(
                        "p (dt i) -> p dt i", dt=4)
                    p.ld(lambda q, o=dst3,
                         i=sl.rearrange("(dt p) i -> p dt i", p=128):
                         q.dma_start(out=o, in_=i))

        def gu_convert(par):
            p.dve(lambda q, o=t.BG[:, WGB[par]:WGB[par] + 2048],
                  i=t.FS[:, 0:2048]: q.tensor_copy(out=o, in_=i))
            p.dve(lambda q, o=t.BG[:, WUB[par]:WUB[par] + 2048],
                  i=t.FS[:, 2048:4096]: q.tensor_copy(out=o, in_=i))

        def dn_loads(e2, ot2):
            wda2 = wd[e2] if e2 < 8 else t.inp['L0_ffn_Wd_s']
            srcw = wda2[:, ot2 * 128:(ot2 + 1) * 128]
            for qtr in range(2):
                sl = srcw[qtr * 512:(qtr + 1) * 512, :]
                dst3 = t.FS[:, 4096 + qtr * 512: 4096 + (qtr + 1) * 512].rearrange(
                    "p (it o) -> p it o", it=4)
                p.ld(lambda q, o=dst3,
                     i=sl.rearrange("(it p) o -> p it o", p=128):
                     q.dma_start(out=o, in_=i))

        def dn_convert(par):
            p.dve(lambda q, o=t.BG[:, WDB[par]:WDB[par] + 1024],
                  i=t.FS[:, 4096:5120]: q.tensor_copy(out=o, in_=i))

        for e in range(9):
            if e < 8:
                for tp in range(2):
                    for jj in range(8):
                        tt = tp * 8 + jj
                        tmp = t.BG[:, T8 + jj * 128: T8 + (jj + 1) * 128]
                        p.dve(lambda q, o=tmp,
                              s=t.SM[:, 240 + tt * 8 + e: 240 + tt * 8 + e + 1]:
                              q.tensor_scalar(out=o, in0=t.ident_bf, scalar1=s,
                                              scalar2=None, op0=ALU.mult))
                    p.bar()
                    for jj in range(8):
                        bank = t.PB if jj < 4 else t.PR
                        p.pe(lambda q, o=bank[:, (jj % 4) * 128:(jj % 4 + 1) * 128],
                             i=t.BG[:, T8 + jj * 128: T8 + (jj + 1) * 128]:
                             q.matmul(o, t.ones_full, i, start=True, stop=True))
                    p.bar()
                    p.act(lambda q, o=t.BG[:, CBB + tp * 1024: CBB + tp * 1024 + 512],
                          i=t.PB[:, 0:512]: q.copy(out=o, in_=i))
                    p.act(lambda q, o=t.BG[:, CBB + tp * 1024 + 512:
                                         CBB + tp * 1024 + 1024],
                          i=t.PR[:, 0:512]: q.copy(out=o, in_=i))
                    p.bar()
            if e == 0:
                gu_loads(0, 0)
                p.bar()
                gu_convert(0)
                p.bar()
            for it in range(8):
                par = it % 2
                if it < 7:
                    gu_loads(e, it + 1)
                else:
                    dn_loads(e, 0)
                for tp in range(2):
                    for tc2 in (0, 1):
                        tc = tp * 2 + tc2
                        for dt in range(DT):
                            p.pe(lambda q, o=t.PS[tc2 * 2][:, 0:512],
                                 w=t.BG[:, WGB[par] + dt * 128:
                                        WGB[par] + dt * 128 + 128],
                                 i=xh[dt][tc], s=(dt == 0), e2=(dt == 15):
                                 q.matmul(o, w, i, start=s, stop=e2))
                        for dt in range(DT):
                            p.pe(lambda q, o=t.PS[tc2 * 2 + 1][:, 0:512],
                                 w=t.BG[:, WUB[par] + dt * 128:
                                        WUB[par] + dt * 128 + 128],
                                 i=xh[dt][tc], s=(dt == 0), e2=(dt == 15):
                                 q.matmul(o, w, i, start=s, stop=e2))
                    p.bar()
                    for tc2 in (0, 1):
                        p.act(lambda q, o=t.BG[:, SGS[tc2]:SGS[tc2] + 512], v=tc2 * 2:
                              q.activation(out=o, in_=t.PS[v][:, 0:512],
                                           func=AF.Silu))
                    p.bar()
                    for tc2 in (0, 1):
                        tc = tp * 2 + tc2
                        ht = t.BG[:, HT + it * 2048 + tc * 512:
                                  HT + it * 2048 + tc * 512 + 512]
                        p.dve(lambda q, o=ht, a=t.BG[:, SGS[tc2]:SGS[tc2] + 512],
                              v=tc2 * 2 + 1: q.tensor_tensor(
                              out=o, in0=a, in1=t.PS[v][:, 0:512], op=ALU.mult))
                    if tp == 1 and it < 7:
                        gu_convert((it + 1) % 2)
                    elif tp == 1:
                        dn_convert(0)
                    p.bar()
                    if e < 8:
                        for tc2 in (0, 1):
                            tc = tp * 2 + tc2
                            ht = t.BG[:, HT + it * 2048 + tc * 512:
                                      HT + it * 2048 + tc * 512 + 512]
                            cbb = t.BG[:, CBB + tc * 512: CBB + (tc + 1) * 512]
                            p.dve(lambda q, o=ht, c=cbb: q.tensor_tensor(
                                  out=o, in0=o, in1=c, op=ALU.mult))
                        p.bar()
            for ot in range(16):
                par = ot % 2
                if ot < 15:
                    dn_loads(e, ot + 1)
                elif e < 8:
                    gu_loads(e + 1, 0)
                for tp in range(2):
                    if e > 0:
                        for tc2 in (0, 1):
                            tc = tp * 2 + tc2
                            stg = t.BG[:, RT1 + tc * 512: RT1 + (tc + 1) * 512]
                            p.ld(lambda q, o=stg,
                                 i=acc[b, ot, :, tc * 512:(tc + 1) * 512]:
                                 q.dma_start(out=o, in_=i))
                    for tc2 in (0, 1):
                        tc = tp * 2 + tc2
                        for it in range(8):
                            ht = t.BG[:, HT + it * 2048 + tc * 512:
                                      HT + it * 2048 + tc * 512 + 512]
                            p.pe(lambda q, o=t.PS[2 + tc2][:, 0:512],
                                 w=t.BG[:, WDB[par] + it * 128:
                                        WDB[par] + it * 128 + 128], h=ht,
                                 s=(it == 0), e2=(it == 7):
                                 q.matmul(o, w, h, start=s, stop=e2))
                    p.bar()
                    for tc2 in (0, 1):
                        tc = tp * 2 + tc2
                        stg = t.BG[:, RT1 + tc * 512: RT1 + (tc + 1) * 512]
                        if e == 0:
                            p.act(lambda q, o=stg, i=t.PS[2 + tc2][:, 0:512]:
                                  q.copy(out=o, in_=i))
                        else:
                            p.dve(lambda q, o=stg, v=2 + tc2: q.tensor_tensor(
                                  out=o, in0=o, in1=t.PS[v][:, 0:512], op=ALU.add))
                    if tp == 1:
                        if ot < 15:
                            dn_convert((ot + 1) % 2)
                        elif e < 8:
                            gu_convert(0)
                    p.bar()
                    if e < 8:
                        for tc2 in (0, 1):
                            tc = tp * 2 + tc2
                            stg = t.BG[:, RT1 + tc * 512: RT1 + (tc + 1) * 512]
                            p.st(lambda q, o=acc[b, ot, :, tc * 512:(tc + 1) * 512],
                                 i=stg: q.dma_start(out=o, in_=i))
                        p.bar()
                    else:
                        for tc2 in (0, 1):
                            tc = tp * 2 + tc2
                            xo = t.BG[:, T8 + tc * 512: T8 + (tc + 1) * 512]
                            p.ld(lambda q, o=xo,
                                 i=t.xfm_src[b, ot, :, tc * 512:(tc + 1) * 512]:
                                 q.dma_start(out=o, in_=i))
                        p.bar()
                        for tc2 in (0, 1):
                            tc = tp * 2 + tc2
                            stg = t.BG[:, RT1 + tc * 512: RT1 + (tc + 1) * 512]
                            xo = t.BG[:, T8 + tc * 512: T8 + (tc + 1) * 512]
                            p.dve(lambda q, o=stg, i2=xo: q.tensor_tensor(
                                  out=o, in0=o, in1=i2, op=ALU.add))
                        p.bar()
                        for tc2 in (0, 1):
                            tc = tp * 2 + tc2
                            stg = t.BG[:, RT1 + tc * 512: RT1 + (tc + 1) * 512]
                            p.st(lambda q, o=t.xfm_dst[b, ot, :,
                                                       tc * 512:(tc + 1) * 512],
                                 i=stg: q.dma_start(out=o, in_=i))
                        p.bar()


def build_l0a(t):
    p = t.p
    XH = 0
    OTo = 0
    KTo = 32768
    Vo = 40960
    QSo = 49152
    QBo = 51200
    PTBo = 51712
    OVSo = 52736
    WSo = 53248
    COSo = 40960
    SINo = 45056
    qt = t.internal['s_qt']
    wq = t.inp['L0_attn_wq']; wk = t.inp['L0_attn_wk']
    wv = t.inp['L0_attn_wv']; wo = t.inp['L0_attn_wo']

    for b in range(B):
        make_xh(t, b, XH, 32768)
        xh = []
        for dt in range(DT):
            xh.append(t.BG[:, XH + dt * 2048: XH + (dt + 1) * 2048])
        p.ld(lambda q, o=t.BG[:, COSo:COSo + 4096].rearrange(
             "p (tt c) -> p tt c", tt=16),
             i=t.inp['c_cosrep'].rearrange("(tt p) c -> p tt c", p=128):
             q.dma_start(out=o, in_=i))
        p.ld(lambda q, o=t.BG[:, SINo:SINo + 4096].rearrange(
             "p (tt c) -> p tt c", tt=16),
             i=t.inp['c_sinrep'].rearrange("(tt p) c -> p tt c", p=128):
             q.dma_start(out=o, in_=i))
        p.bar()
        for wap, nocs, kind in ((wq, 4, 'q'), (wk, 1, 'k'), (wv, 1, 'v')):
            for oc in range(nocs):
                for j in range(4):
                    load_wT(t, wap[oc * 512 + j * 128: oc * 512 + (j + 1) * 128, :],
                            WSo + j * 128, stride=512)
                for tt in range(NTT):
                    for dt in range(DT):
                        p.pe(lambda q, o=t.PS[0][:, 0:512],
                             l=xh[dt][:, tt * 128:(tt + 1) * 128],
                             r=t.BG[:, WSo + dt * 512: WSo + dt * 512 + 512],
                             s=(dt == 0), e2=(dt == 15):
                             q.matmul(o, l, r, start=s, stop=e2))
                    p.bar()
                    qb = t.BG[:, QBo:QBo + 512]
                    if kind in ('q', 'k'):
                        cosn = t.BG[:, COSo + tt * 256: COSo + tt * 256 + 256]
                        sinn = t.BG[:, SINo + tt * 256: SINo + tt * 256 + 256]
                        xr = t.PS[0][:, 0:512:2]
                        xi = t.PS[0][:, 1:512:2]
                        t1 = t.FS[:, 0:256]; t2 = t.FS[:, 256:512]
                        t3 = t.FS[:, 512:768]; t4 = t.FS[:, 768:1024]
                        p.dve(lambda q, o=t1, a=cosn, b2=xr: q.tensor_tensor(
                              out=o, in0=a, in1=b2, op=ALU.mult))
                        p.dve(lambda q, o=t2, a=sinn, b2=xi: q.tensor_tensor(
                              out=o, in0=a, in1=b2, op=ALU.mult))
                        p.dve(lambda q, o=t3, a=sinn, b2=xr: q.tensor_tensor(
                              out=o, in0=a, in1=b2, op=ALU.mult))
                        p.dve(lambda q, o=t4, a=cosn, b2=xi: q.tensor_tensor(
                              out=o, in0=a, in1=b2, op=ALU.mult))
                        p.bar()
                        p.dve(lambda q, o=qb[:, 0:512:2], a=t1, b2=t2:
                              q.tensor_tensor(out=o, in0=a, in1=b2,
                                              op=ALU.subtract))
                        p.dve(lambda q, o=qb[:, 1:512:2], a=t3, b2=t4:
                              q.tensor_tensor(out=o, in0=a, in1=b2, op=ALU.add))
                        p.bar()
                        if kind == 'q':
                            for j in range(4):
                                stg = t.BG[:, 66000 + (j % 2) * 128:
                                           66000 + (j % 2) * 128 + 128]
                                tr_copy(t, qb[:, j * 128:(j + 1) * 128], stg, j % 2)
                                p.bar()
                                p.st(lambda q, o=qt[oc * 512 + j * 128:
                                                    oc * 512 + (j + 1) * 128,
                                                    tt * 128:(tt + 1) * 128],
                                     i=stg: q.dma_start(out=o, in_=i))
                            p.bar()
                        else:
                            for j in range(4):
                                dst = t.BG[:, KTo + j * 2048 + tt * 128:
                                           KTo + j * 2048 + tt * 128 + 128]
                                tr_copy(t, qb[:, j * 128:(j + 1) * 128], dst, j % 2)
                            p.bar()
                    else:
                        p.act(lambda q, o=t.BG[:, Vo + tt * 512: Vo + tt * 512 + 512],
                              i=t.PS[0][:, 0:512]: q.copy(out=o, in_=i))
                        p.bar()
        p.ld(lambda q, o=t.BG[:, WSo:WSo + 2048], i=t.inp['c_dmask'][:]:
             q.dma_start(out=o, in_=i))
        p.bar()
        for h in range(32):
            kv = h // 4
            ktile = kv // 2
            base = 64 * (kv % 2)
            qs = t.BG[base:base + 64, QSo:QSo + 2048]
            p.ld(lambda q, o=qs, i=qt[h * 64:h * 64 + 64, :]:
                 q.dma_start(out=o, in_=i))
            p.bar()
            for qc in range(NTC):
                nkt = 4 * qc + 4
                for kt in range(nkt):
                    lk = t.BG[base:base + 64,
                              KTo + ktile * 2048 + kt * 128:
                              KTo + ktile * 2048 + kt * 128 + 128]
                    rq = t.BG[base:base + 64, QSo + qc * 512: QSo + qc * 512 + 512]
                    p.pe(lambda q, o=t.PS[1][:, 0:512], l=lk, r=rq:
                         q.matmul(o, l, r, start=True, stop=True))
                    p.bar()
                    ptb = t.BG[:, PTBo + (kt % 2) * 512: PTBo + (kt % 2) * 512 + 512]
                    diag = kt - 4 * qc
                    if diag >= 0:
                        ef = t.FS[:, 1024:1536]
                        p.act(lambda q, o=ef: q.activation(
                              out=o, in_=t.PS[1][:, 0:512], func=AF.Exp,
                              scale=0.125))
                        p.bar()
                        p.dve(lambda q, o=ptb, a=ef,
                              m=t.BG[:, WSo + diag * 512: WSo + diag * 512 + 512]:
                              q.tensor_tensor(out=o, in0=a, in1=m, op=ALU.mult))
                    else:
                        p.act(lambda q, o=ptb: q.activation(
                              out=o, in_=t.PS[1][:, 0:512], func=AF.Exp,
                              scale=0.125))
                    p.bar()
                    p.pe(lambda q, o=t.PR[0:1, 0:512], i=ptb, s=(kt == 0),
                         e2=(kt == nkt - 1):
                         q.matmul(o, t.ones_col, i, start=s, stop=e2))
                    p.pe(lambda q, o=t.PS[2][0:64, 0:512],
                         l=t.BG[:, Vo + kt * 512 + kv * 64:
                                Vo + kt * 512 + kv * 64 + 64], i=ptb,
                         s=(kt == 0), e2=(kt == nkt - 1):
                         q.matmul(o, l, i, start=s, stop=e2))
                    p.bar()
                rz = t.RW[0:1, 2048:2560]
                p.dve(lambda q, o=rz: q.reciprocal(out=o, in_=t.PR[0:1, 0:512]))
                p.bar()
                rzb = t.BG[0:1, 66000:66512]
                p.act(lambda q, o=rzb, i=rz: q.copy(out=o, in_=i))
                p.bar()
                p.pe(lambda q, o=t.PB[0:64, 0:512], i=rzb:
                     q.matmul(o, t.CB[0:1, 129:193], i, start=True, stop=True))
                ovs = t.BG[0:64, OVSo:OVSo + 512]
                p.act(lambda q, o=ovs, i=t.PS[2][0:64, 0:512]: q.copy(out=o, in_=i))
                p.bar()
                obase = 64 * (h % 2)
                ht2 = h // 2
                odst = t.BG[obase:obase + 64,
                            OTo + ht2 * 2048 + qc * 512:
                            OTo + ht2 * 2048 + qc * 512 + 512]
                p.dve(lambda q, o=odst, a=ovs: q.tensor_tensor(
                      out=o, in0=a, in1=t.PB[0:64, 0:512], op=ALU.mult))
                p.bar()
        for ot in range(16):
            load_wT(t, wo[ot * 128:(ot + 1) * 128, :], WSo, stride=128)
            for tc in range(NTC):
                for hd in range(16):
                    p.pe(lambda q, o=t.PS[3][:, 0:512],
                         l=t.BG[:, WSo + hd * 128: WSo + hd * 128 + 128],
                         r=t.BG[:, OTo + hd * 2048 + tc * 512:
                                OTo + hd * 2048 + tc * 512 + 512],
                         s=(hd == 0), e2=(hd == 15):
                         q.matmul(o, l, r, start=s, stop=e2))
                p.bar()
                xo = t.BG[:, 66000:66512]
                p.ld(lambda q, o=xo, i=t.xfm_src[b, ot, :, tc * 512:(tc + 1) * 512]:
                     q.dma_start(out=o, in_=i))
                p.bar()
                stg = t.BG[:, 66512:67024]
                p.dve(lambda q, o=stg, a=xo: q.tensor_tensor(
                      out=o, in0=a, in1=t.PS[3][:, 0:512], op=ALU.add))
                p.bar()
                p.st(lambda q, o=t.xfm_dst[b, ot, :, tc * 512:(tc + 1) * 512],
                     i=stg: q.dma_start(out=o, in_=i))
                p.bar()


L1A_PH = {'p1', 'p2', 'p3', 'p4', 'p5'}
L1A_NH = 16
L1A_NC = 16
L1A_SEQ = True
L1A_GS = True
L1A_TR = True


def build_l1a(t):
    p = t.p
    qkvw = t.inp['L1_attn_qkv_w']
    gw = t.inp['L1_attn_g_w']
    ow = t.inp['L1_attn_o_w']
    sqkv = t.internal['s_qkv']
    sg = t.internal['s_g']
    sot = t.internal['s_ot']
    p.ld(lambda q, o=t.SM[0:1, 144:272],
         i=t.inp['L1_attn_out_norm_w'].rearrange("(a b) -> a b", a=1):
         q.dma_start(out=o, in_=i))
    p.bar()
    onr = t.BG[0:1, 66000:66128]
    p.act(lambda q, o=onr, i=t.SM[0:1, 144:272]: q.copy(out=o, in_=i))
    p.bar()
    p.pe(lambda q, o=t.PB[:, 0:128], i=onr:
         q.matmul(o, t.ones_row, i, start=True, stop=True))
    p.bar()
    p.act(lambda q, o=t.SM[:, 0:128], i=t.PB[:, 0:128]: q.copy(out=o, in_=i))
    p.bar()
    for ct in range(24):
        p.ld(lambda q, o=t.SM[:, 144 + ct * 4: 144 + (ct + 1) * 4],
             i=t.inp['L1_attn_conv_w'][ct * 128:(ct + 1) * 128, :]:
             q.dma_start(out=o, in_=i))
    p.bar()

    for b in range(B):
        if 'p1' not in L1A_PH:
            break
        make_xh(t, b, 0, 32768)
        xh = []
        for dt in range(DT):
            row = []
            for tc in range(NTC):
                row.append(t.BG[:, dt * 2048 + tc * 512:
                               dt * 2048 + tc * 512 + 512])
            xh.append(row)
        for wnm, outrow in (('L1_attn_b_w', 0), ('L1_attn_gk_w', 2048)):
            p.ld(lambda q, o=t.FS[0:16, 0:2048], i=t.inp[wnm][:]:
                 q.dma_start(out=o, in_=i))
            p.bar()
            cb16 = t.BG[0:16, 67000:69048]
            p.dve(lambda q, o=cb16, i=t.FS[0:16, 0:2048]: q.tensor_copy(out=o, in_=i))
            p.bar()
            for dt in range(16):
                tr_copy(t, cb16[:, dt * 128:(dt + 1) * 128],
                        t.BG[:, 33024 + dt * 16: 33024 + dt * 16 + 16], dt % 2)
            p.bar()
            for tc in range(NTC):
                for dt in range(DT):
                    p.pe(lambda q, o=t.PS[0][0:16, 0:512],
                         l=t.BG[:, 33024 + dt * 16: 33024 + dt * 16 + 16],
                         r=xh[dt][tc], s=(dt == 0), e2=(dt == 15):
                         q.matmul(o, l, r, start=s, stop=e2))
                p.bar()
                dst = t.BGL[0:16, outrow + tc * 512: outrow + (tc + 1) * 512]
                p.act(lambda q, o=dst: q.activation(
                      out=o, in_=t.PS[0][0:16, 0:512], func=AF.Sigmoid))
                p.bar()
                if outrow == 2048:
                    p.act(lambda q, o=dst, i=dst: q.activation(out=o, in_=i,
                          func=AF.Ln))
                    p.bar()
        for oc in range(4):
            for j in range(4):
                load_wT(t, gw[oc * 512 + j * 128: oc * 512 + (j + 1) * 128, :],
                        33280 + j * 128, stride=512)
            for tt in range(NTT):
                for dt in range(DT):
                    p.pe(lambda q, o=t.PS[0][:, 0:512],
                         l=xh[dt][0][:, 0:512].tensor[
                             :, dt * 2048 + tt * 128: dt * 2048 + tt * 128 + 128]
                         if False else t.BG[:, dt * 2048 + tt * 128:
                                            dt * 2048 + tt * 128 + 128],
                         r=t.BG[:, 33280 + dt * 512: 33280 + dt * 512 + 512],
                         s=(dt == 0), e2=(dt == 15):
                         q.matmul(o, l, r, start=s, stop=e2))
                p.bar()
                gb = t.BG[:, 41472:41984]
                p.act(lambda q, o=gb: q.activation(out=o, in_=t.PS[0][:, 0:512],
                      func=AF.Silu))
                p.bar()
                p.st(lambda q, o=sg[tt * 128:(tt + 1) * 128,
                                    oc * 512:(oc + 1) * 512], i=gb:
                     q.dma_start(out=o, in_=i))
                p.bar()
        for ct in range(24):
            load_wT(t, qkvw[ct * 128:(ct + 1) * 128, :], 33280, stride=128)
            for tc in range(NTC):
                for dt in range(DT):
                    p.pe(lambda q, o=t.PS[0][:, 0:512],
                         l=t.BG[:, 33280 + dt * 128: 33280 + dt * 128 + 128],
                         r=xh[dt][tc], s=(dt == 0), e2=(dt == 15):
                         q.matmul(o, l, r, start=s, stop=e2))
                p.bar()
                stg = t.BG[:, 41472:41984]
                p.act(lambda q, o=stg, i=t.PS[0][:, 0:512]: q.copy(out=o, in_=i))
                p.bar()
                p.st(lambda q, o=sqkv[b, ct, :, tc * 512:(tc + 1) * 512], i=stg:
                     q.dma_start(out=o, in_=i))
                p.bar()
        for ct in (range(24) if 'p2' in L1A_PH else []):
            kb = t.BG[:, 0:2048]
            p.ld(lambda q, o=kb, i=sqkv[b, ct, :, :]: q.dma_start(out=o, in_=i))
            p.bar()
            TA = t.FS[:, 0:2048]
            TB = t.FS[:, 2048:4096]
            w = lambda i: t.SM[:, 144 + ct * 4 + i: 144 + ct * 4 + i + 1]
            p.dve(lambda q, o=TA, i=kb, s=w(3): q.tensor_scalar(
                  out=o, in0=i, scalar1=s, scalar2=None, op0=ALU.mult))
            p.bar()
            p.dve(lambda q, o=TB[:, 1:2048], i=kb[:, 0:2047], s=w(2),
                  a=TA[:, 1:2048]: q.scalar_tensor_tensor(
                  out=o, in0=i, scalar=s, in1=a, op0=ALU.mult, op1=ALU.add))
            p.dve(lambda q, o=TB[:, 0:1], i=TA[:, 0:1]: q.tensor_copy(out=o, in_=i))
            p.bar()
            p.dve(lambda q, o=TA[:, 2:2048], i=kb[:, 0:2046], s=w(1),
                  a=TB[:, 2:2048]: q.scalar_tensor_tensor(
                  out=o, in0=i, scalar=s, in1=a, op0=ALU.mult, op1=ALU.add))
            p.dve(lambda q, o=TA[:, 0:2], i=TB[:, 0:2]: q.tensor_copy(out=o, in_=i))
            p.bar()
            p.dve(lambda q, o=TB[:, 3:2048], i=kb[:, 0:2045], s=w(0),
                  a=TA[:, 3:2048]: q.scalar_tensor_tensor(
                  out=o, in0=i, scalar=s, in1=a, op0=ALU.mult, op1=ALU.add))
            p.dve(lambda q, o=TB[:, 0:3], i=TA[:, 0:3]: q.tensor_copy(out=o, in_=i))
            p.bar()
            p.act(lambda q, o=kb, i=TB: q.activation(out=o, in_=i, func=AF.Silu))
            p.bar()
            if ct < 20:
                sqb = t.BG[:, 2048:4096]
                p.act(lambda q, o=sqb, i=kb: q.activation(out=o, in_=i,
                      func=AF.Square))
                p.bar()
                for tc in range(NTC):
                    p.pe(lambda q, o=t.PR[0:1, 0:512],
                         i=sqb[:, tc * 512:(tc + 1) * 512]:
                         q.matmul(o, t.ones_col, i, start=True, stop=True))
                    p.bar()
                    rr = t.RW[0:1, 2048:2560]
                    p.act(lambda q, o=rr: q.activation(
                          out=o, in_=t.PR[0:1, 0:512], func=AF.Sqrt, bias=1e-6))
                    p.bar()
                    p.dve(lambda q, o=rr, i=rr: q.reciprocal(out=o, in_=i))
                    p.bar()
                    if ct < 16:
                        p.dve(lambda q, o=rr, i=rr: q.tensor_scalar(
                              out=o, in0=i, scalar1=0.125, scalar2=None,
                              op0=ALU.mult))
                        p.bar()
                    rrb = t.BG[0:1, 66000:66512]
                    p.act(lambda q, o=rrb, i=rr: q.copy(out=o, in_=i))
                    p.bar()
                    p.pe(lambda q, o=t.PB[:, 0:512], i=rrb:
                         q.matmul(o, t.ones_row, i, start=True, stop=True))
                    p.bar()
                    p.dve(lambda q, o=kb[:, tc * 512:(tc + 1) * 512]:
                          q.tensor_tensor(out=o, in0=o, in1=t.PB[:, 0:512],
                                          op=ALU.mult))
                    p.bar()
            p.st(lambda q, o=sqkv[b, ct, :, :], i=kb: q.dma_start(out=o, in_=i))
            p.bar()
        for c in (range(16) if 'p3' in L1A_PH else []):
            for srow, dcol in ((2048, 0), (0, 512)):
                src = t.BGL[0:16, srow + c * 128: srow + (c + 1) * 128]
                pso = t.PS[1][0:128, 0:16]
                p.pe(lambda q, o=pso, i=src: q.transpose(o, i, t.IDF[0:16, 0:16]))
                p.bar()
                if srow == 2048:
                    gt = t.FS[:, 0:16]
                    p.act(lambda q, o=gt, i=pso: q.copy(out=o, in_=i))
                    p.bar()
                    p.pe(lambda q, o=t.PS[2][0:128, 0:16], i=gt:
                         q.matmul(o, t.UT[:], i, start=True, stop=True))
                    p.bar()
                    cc = t.CCA[:, c * 16:(c + 1) * 16]
                    p.act(lambda q, o=cc: q.activation(
                          out=o, in_=t.PS[2][0:128, 0:16], func=AF.Exp))
                    p.bar()
                    p.dve(lambda q, o=t.CCA[:, 256 + c * 16: 256 + (c + 1) * 16],
                          i=cc: q.reciprocal(out=o, in_=i))
                    p.bar()
                else:
                    p.act(lambda q, o=t.CCA[:, 512 + c * 16: 512 + (c + 1) * 16],
                          i=pso: q.copy(out=o, in_=i))
                    p.bar()
        KFM, VFM, KTOK, VTOK, QF, GSIL, CH5 = 0, 2048, 4096, 6144, 8192, 10240, 12288
        DB = 22528
        Pb = [DB, DB + 128]; PTb = [DB + 256, DB + 384]; RTb = [DB + 512, DB + 640]
        DLo, SBo, OFo = DB + 768, DB + 896, DB + 1024
        TI = DB + 1152
        KAo, VBo, STG = DB + 1536, DB + 1664, DB + 1792
        for h in (range(L1A_NH) if 'p4' in L1A_PH else []):
            kvh = h // 4
            if h % 4 == 0:
                p.ld(lambda q, o=t.BG[:, KFM:KFM + 2048],
                     i=sqkv[b, 16 + kvh, :, :]: q.dma_start(out=o, in_=i))
                p.ld(lambda q, o=t.BG[:, VFM:VFM + 2048],
                     i=sqkv[b, 20 + kvh, :, :]: q.dma_start(out=o, in_=i))
                p.bar()
                for tt in (range(16) if L1A_TR else []):
                    tr_copy(t, t.BG[:, KFM + tt * 128: KFM + tt * 128 + 128],
                            t.BG[:, KTOK + tt * 128: KTOK + tt * 128 + 128], 0)
                    tr_copy(t, t.BG[:, VFM + tt * 128: VFM + tt * 128 + 128],
                            t.BG[:, VTOK + tt * 128: VTOK + tt * 128 + 128], 1)
                p.bar()
            p.ld(lambda q, o=t.BG[:, QF:QF + 2048], i=sqkv[b, h, :, :]:
                 q.dma_start(out=o, in_=i))
            for cq in (range(4) if L1A_GS else []):
                p.ld(lambda q, o=t.BG[:, GSIL + cq * 512: GSIL + (cq + 1) * 512]
                     .rearrange("p (c v) -> p c v", c=4),
                     i=sg[cq * 512:(cq + 1) * 512, h * 128:(h + 1) * 128]
                     .rearrange("(c p) v -> p c v", p=128):
                     q.dma_start(out=o, in_=i))
            p.bar()
            S32 = [t.FS[:, 4096:4224], t.FS[:, 4224:4352]]
            sbf = t.BG[:, SBo:SBo + 128]
            p.dve(lambda q, o=S32[0]: q.memzero(o))
            p.dve(lambda q, o=sbf: q.memzero(o))
            p.bar()
            cco = lambda c: t.CCA[:, c * 16 + h: c * 16 + h + 1]
            ccio = lambda c: t.CCA[:, 256 + c * 16 + h: 256 + c * 16 + h + 1]
            bto = lambda c: t.CCA[:, 512 + c * 16 + h: 512 + c * 16 + h + 1]
            for c in range(L1A_NC):
                kf = t.BG[:, KFM + c * 128: KFM + c * 128 + 128]
                qf = t.BG[:, QF + c * 128: QF + c * 128 + 128]
                ktk = t.BG[:, KTOK + c * 128: KTOK + c * 128 + 128]
                vtk = t.BG[:, VTOK + c * 128: VTOK + c * 128 + 128]
                s1 = t.CCA[:, 768:769]
                p.dve(lambda q, o=s1, a=bto(c), b2=cco(c): q.tensor_tensor(
                      out=o, in0=a, in1=b2, op=ALU.mult))
                p.bar()
                for bi, colap in ((0, cco(c)), (1, s1), (2, ccio(c))):
                    ti = t.BG[:, TI + bi * 128: TI + (bi + 1) * 128]
                    p.dve(lambda q, o=ti, s=colap: q.tensor_scalar(
                          out=o, in0=t.ident_bf, scalar1=s, scalar2=None,
                          op0=ALU.mult))
                    p.bar()
                    p.pe(lambda q, o=t.PB[:, bi * 128:(bi + 1) * 128], i=ti:
                         q.matmul(o, t.ones_full, i, start=True, stop=True))
                    p.bar()
                ccB = t.PB[:, 0:128]; s1B = t.PB[:, 128:256]; cciB = t.PB[:, 256:384]
                p.pe(lambda q, o=t.PS[0][:, 0:128], l=kf, r=kf:
                     q.matmul(o, l, r, start=True, stop=True))
                p.bar()
                v1 = t.FS[:, 4352:4480]
                p.dve(lambda q, o=v1, s=s1: q.tensor_scalar(
                      out=o, in0=t.PS[0][:, 0:128], scalar1=s, scalar2=None,
                      op0=ALU.mult))
                v3 = t.FS[:, 4480:4608]
                p.dve(lambda q, o=v3, s=ccio(c): q.tensor_scalar(
                      out=o, in0=t.PS[0][:, 0:128], scalar1=s, scalar2=None,
                      op0=ALU.mult))
                p.bar()
                p.dve(lambda q, o=v1: q.tensor_tensor(out=o, in0=v1, in1=cciB,
                      op=ALU.mult))
                p.dve(lambda q, o=v3: q.tensor_tensor(out=o, in0=v3, in1=s1B,
                      op=ALU.mult))
                p.bar()
                pn = t.BG[:, Pb[0]:Pb[0] + 128]
                pt = t.BG[:, PTb[0]:PTb[0] + 128]
                p.dve(lambda q, o=pn, a=v1: q.tensor_tensor(out=o, in0=a,
                      in1=t.negtril, op=ALU.mult))
                p.dve(lambda q, o=pt, a=v3: q.tensor_tensor(out=o, in0=a,
                      in1=t.negtriu, op=ALU.mult))
                p.bar()
                rt = t.BG[:, RTb[0]:RTb[0] + 128]
                p.dve(lambda q, o=rt, a=pt: q.tensor_tensor(out=o, in0=a,
                      in1=t.ident_bf, op=ALU.add))
                p.bar()
                cur = 0
                for li in range(6):
                    rtc = t.BG[:, RTb[cur % 2]:RTb[cur % 2] + 128]
                    rtn = t.BG[:, RTb[(cur + 1) % 2]:RTb[(cur + 1) % 2] + 128]
                    pc = t.BG[:, Pb[cur % 2]:Pb[cur % 2] + 128]
                    ptc = t.BG[:, PTb[cur % 2]:PTb[cur % 2] + 128]
                    pnx = t.BG[:, Pb[(cur + 1) % 2]:Pb[(cur + 1) % 2] + 128]
                    ptx = t.BG[:, PTb[(cur + 1) % 2]:PTb[(cur + 1) % 2] + 128]
                    p.pe(lambda q, o=t.PS[2][:, 0:128], l=ptc, r=pc:
                         q.matmul(o, l, r, start=True, stop=True))
                    p.pe(lambda q, o=t.PS[3][:, 0:128], l=pc, r=ptc:
                         q.matmul(o, l, r, start=True, stop=True))
                    p.bar()
                    p.act(lambda q, o=pnx, i=t.PS[2][:, 0:128]: q.copy(out=o, in_=i))
                    p.act(lambda q, o=ptx, i=t.PS[3][:, 0:128]: q.copy(out=o, in_=i))
                    p.bar()
                    p.pe(lambda q, o=t.PS[1][:, 0:128], l=pnx, r=rtc:
                         q.matmul(o, l, r, start=True, stop=True))
                    p.bar()
                    p.dve(lambda q, o=rtn, a=rtc: q.tensor_tensor(
                          out=o, in0=a, in1=t.PS[1][:, 0:128], op=ALU.add))
                    p.bar()
                    cur += 1
                rtf = t.BG[:, RTb[cur % 2]:RTb[cur % 2] + 128]
                ch5 = lambda s: t.BG[:, CH5 + c * 640 + s * 128:
                                     CH5 + c * 640 + (s + 1) * 128]
                ka = t.BG[:, KAo:KAo + 128]
                p.dve(lambda q, o=ka, i=ktk, s=s1: q.tensor_scalar(
                      out=o, in0=i, scalar1=s, scalar2=None, op0=ALU.mult))
                p.bar()
                p.pe(lambda q, o=t.PS[1][:, 0:128], l=ka, r=rtf:
                     q.matmul(o, l, r, start=True, stop=True))
                p.bar()
                p.act(lambda q, o=ch5(0), i=t.PS[1][:, 0:128]: q.copy(out=o, in_=i))
                vb = t.BG[:, VBo:VBo + 128]
                p.dve(lambda q, o=vb, i=vtk, s=bto(c): q.tensor_scalar(
                      out=o, in0=i, scalar1=s, scalar2=None, op0=ALU.mult))
                p.bar()
                p.pe(lambda q, o=t.PS[2][:, 0:128], l=rtf, r=vb:
                     q.matmul(o, l, r, start=True, stop=True))
                p.pe(lambda q, o=t.PS[3][:, 0:128], l=kf, r=qf:
                     q.matmul(o, l, r, start=True, stop=True))
                p.bar()
                p.act(lambda q, o=ch5(1), i=t.PS[2][:, 0:128]: q.copy(out=o, in_=i))
                v5 = t.FS[:, 4352:4480]
                p.dve(lambda q, o=v5, s=ccio(c): q.tensor_scalar(
                      out=o, in0=t.PS[3][:, 0:128], scalar1=s, scalar2=None,
                      op0=ALU.mult))
                p.bar()
                p.dve(lambda q, o=v5: q.tensor_tensor(out=o, in0=v5, in1=ccB,
                      op=ALU.mult))
                p.bar()
                p.dve(lambda q, o=ch5(2), a=v5: q.tensor_tensor(out=o, in0=a,
                      in1=t.utri1, op=ALU.mult))
                ccr = t.CCA[:, 769:770]
                p.dve(lambda q, o=ccr, a=ccio(c): q.tensor_tensor(
                      out=o, in0=a, in1=ccB[:, 127:128], op=ALU.mult))
                p.bar()
                p.dve(lambda q, o=ch5(3), i=ktk, s=ccr: q.tensor_scalar(
                      out=o, in0=i, scalar1=s, scalar2=None, op0=ALU.mult))
                p.dve(lambda q, o=ch5(4), i=qf: q.tensor_tensor(
                      out=o, in0=i, in1=ccB, op=ALU.mult))
                p.act(lambda q, o=t.CCA[:, 784 + c: 785 + c], i=ccB[:, 127:128]:
                      q.copy(out=o, in_=i))
                p.bar()
            for c in (range(L1A_NC) if L1A_SEQ else []):
                ch5 = lambda s: t.BG[:, CH5 + c * 640 + s * 128:
                                     CH5 + c * 640 + (s + 1) * 128]
                p.pe(lambda q, o=t.PS[0][:, 0:128], l=ch5(0), r=sbf:
                     q.matmul(o, l, r, start=True, stop=True))
                p.bar()
                dl = t.BG[:, DLo:DLo + 128]
                p.dve(lambda q, o=dl, a=ch5(1): q.tensor_tensor(
                      out=o, in0=a, in1=t.PS[0][:, 0:128], op=ALU.subtract))
                p.bar()
                p.pe(lambda q, o=t.PS[1][:, 0:128], l=ch5(4), r=sbf:
                     q.matmul(o, l, r, start=True, stop=False))
                p.pe(lambda q, o=t.PS[1][:, 0:128], l=ch5(2), r=dl:
                     q.matmul(o, l, r, start=False, stop=True))
                p.pe(lambda q, o=t.PS[2][:, 0:128], l=ch5(3), r=dl:
                     q.matmul(o, l, r, start=True, stop=True))
                p.bar()
                snew = S32[(c + 1) % 2]
                p.dve(lambda q, o=snew, i=S32[c % 2], s=t.CCA[:, 784 + c: 785 + c]:
                      q.scalar_tensor_tensor(out=o, in0=i, scalar=s,
                      in1=t.PS[2][:, 0:128], op0=ALU.mult, op1=ALU.add))
                sq2 = t.FS[:, 4352:4480]
                p.act(lambda q, o=sq2: q.activation(out=o, in_=t.PS[1][:, 0:128],
                      func=AF.Square))
                p.bar()
                p.act(lambda q, o=sbf, i=snew: q.copy(out=o, in_=i))
                rs = t.CCA[:, 770:771]
                p.dve(lambda q, o=rs, i=sq2: q.reduce_sum(out=o, in_=i, axis=AX.X))
                p.bar()
                p.dve(lambda q, o=rs, i=rs: q.tensor_scalar(
                      out=o, in0=i, scalar1=1.0 / 128, scalar2=None, op0=ALU.mult))
                p.bar()
                p.act(lambda q, o=rs, i=rs: q.activation(out=o, in_=i,
                      func=AF.Sqrt, bias=1e-5))
                p.bar()
                p.dve(lambda q, o=rs, i=rs: q.reciprocal(out=o, in_=i))
                p.bar()
                o1 = t.FS[:, 4480:4608]
                p.dve(lambda q, o=o1, s=rs: q.tensor_scalar(
                      out=o, in0=t.PS[1][:, 0:128], scalar1=s, scalar2=None,
                      op0=ALU.mult))
                p.bar()
                p.dve(lambda q, o=o1: q.tensor_tensor(out=o, in0=o1,
                      in1=t.SM[:, 0:128], op=ALU.mult))
                p.bar()
                of = t.BG[:, OFo:OFo + 128]
                p.dve(lambda q, o=of, a=o1,
                      g=t.BG[:, GSIL + c * 128: GSIL + c * 128 + 128]:
                      q.tensor_tensor(out=o, in0=a, in1=g, op=ALU.mult))
                p.bar()
                stg = t.BG[:, STG:STG + 128]
                tr_copy(t, of, stg, 0)
                p.bar()
                p.st(lambda q, o=sot[h * 128:(h + 1) * 128, c * 128:(c + 1) * 128],
                     i=stg: q.dma_start(out=o, in_=i))
                p.bar()
        if 'p5' not in L1A_PH:
            continue
        for hd in range(16):
            p.ld(lambda q, o=t.BG[:, hd * 2048:(hd + 1) * 2048],
                 i=sot[hd * 128:(hd + 1) * 128, :]: q.dma_start(out=o, in_=i))
        p.bar()
        for ot in range(16):
            load_wT(t, ow[ot * 128:(ot + 1) * 128, :], 33280, stride=128)
            for tc in range(NTC):
                for hd in range(16):
                    p.pe(lambda q, o=t.PS[3][:, 0:512],
                         l=t.BG[:, 33280 + hd * 128: 33280 + hd * 128 + 128],
                         r=t.BG[:, hd * 2048 + tc * 512: hd * 2048 + tc * 512 + 512],
                         s=(hd == 0), e2=(hd == 15):
                         q.matmul(o, l, r, start=s, stop=e2))
                p.bar()
                xo = t.BG[:, 66000:66512]
                p.ld(lambda q, o=xo, i=t.xfm_src[b, ot, :, tc * 512:(tc + 1) * 512]:
                     q.dma_start(out=o, in_=i))
                p.bar()
                stg = t.BG[:, 66512:67024]
                p.dve(lambda q, o=stg, a=xo: q.tensor_tensor(
                      out=o, in0=a, in1=t.PS[3][:, 0:512], op=ALU.add))
                p.bar()
                p.st(lambda q, o=t.xfm_dst[b, ot, :, tc * 512:(tc + 1) * 512],
                     i=stg: q.dma_start(out=o, in_=i))
                p.bar()


def gather_lut(t, idx_fp_ap, out_ap, scr):
    p = t.p
    F = idx_fp_ap.shape[1]
    acc1 = t.BG[:, scr:scr + F]
    tmp0 = t.BG[:, scr + F:scr + 2 * F]
    tmp1 = t.BG[:, scr + 2 * F:scr + 3 * F]
    lut = lambda i: t.SM[:, 128 + i:128 + i + 1]
    p.dve(lambda q, o=out_ap, i=idx_fp_ap, s=lut(0): q.tensor_scalar(
          out=o, in0=i, scalar1=0.0, scalar2=s, op0=ALU.is_equal, op1=ALU.mult))
    p.pool(lambda q, o=acc1, i=idx_fp_ap, s=lut(1): q.tensor_scalar(
           out=o, in0=i, scalar1=1.0, scalar2=s, op0=ALU.is_equal, op1=ALU.mult))
    p.bar()
    for k in range(1, 8):
        p.dve(lambda q, o=tmp0, i=idx_fp_ap, v=float(2 * k), s=lut(2 * k):
              q.tensor_scalar(out=o, in0=i, scalar1=v, scalar2=s,
                              op0=ALU.is_equal, op1=ALU.mult))
        p.pool(lambda q, o=tmp1, i=idx_fp_ap, v=float(2 * k + 1), s=lut(2 * k + 1):
               q.tensor_scalar(out=o, in0=i, scalar1=v, scalar2=s,
                               op0=ALU.is_equal, op1=ALU.mult))
        p.bar()
        p.dve(lambda q, o=out_ap, a=tmp0: q.tensor_tensor(out=o, in0=o, in1=a,
              op=ALU.add))
        p.pool(lambda q, o=acc1, a=tmp1: q.tensor_tensor(out=o, in0=o, in1=a,
               op=ALU.add))
        p.bar()
    p.dve(lambda q, o=out_ap, a=acc1: q.tensor_tensor(out=o, in0=o, in1=a,
          op=ALU.add))
    p.bar()


def load_idx_fp(t, idx_rows_ap, F):
    p = t.p
    off = 0
    while off < F:
        w = min(2048, F - off)
        p.ld(lambda q, o=t.I3[:, 0:w], i=idx_rows_ap[:, off:off + w]:
             q.dma_start(out=o, in_=i))
        p.bar()
        p.dve(lambda q, o=t.FS[:, off:off + w], i=t.I3[:, 0:w]:
              q.tensor_copy(out=o, in_=i))
        p.bar()
        off += w


def build_l1f(t, final):
    p = t.p
    gi = t.inp['L1_ffn_gate_idx']
    ui = t.inp['L1_ffn_up_idx']
    di = t.inp['L1_ffn_down_idx']
    sh = t.internal['s_h']
    swg = t.internal['s_wgt']
    swu = t.internal['s_wut']
    swd = t.internal['s_wdt']
    p.ld(lambda q, o=t.SM[0:1, 144:160],
         i=t.inp['L1_ffn_lut'].rearrange("(a b) -> a b", a=1):
         q.dma_start(out=o, in_=i))
    p.bar()
    lrow = t.BG[0:1, 66000:66016]
    p.act(lambda q, o=lrow, i=t.SM[0:1, 144:160]: q.copy(out=o, in_=i))
    p.bar()
    p.pe(lambda q, o=t.PB[:, 0:16], i=lrow:
         q.matmul(o, t.ones_row, i, start=True, stop=True))
    p.bar()
    p.act(lambda q, o=t.SM[:, 128:144], i=t.PB[:, 0:16]: q.copy(out=o, in_=i))
    p.bar()
    for it in range(43):
        for idx_ap, dst in ((gi, swg), (ui, swu)):
            load_idx_fp(t, idx_ap[it * 128:(it + 1) * 128, :], 2048)
            gw = t.BG[:, 0:2048]
            gather_lut(t, t.FS[:, 0:2048], gw, 8192)
            for dt in range(16):
                tr_copy(t, gw[:, dt * 128:(dt + 1) * 128],
                        t.BG[:, 2048 + dt * 128: 2048 + dt * 128 + 128], dt % 2)
            p.bar()
            p.st(lambda q, o=dst[it], i=t.BG[:, 2048:4096]:
                 q.dma_start(out=o, in_=i))
            p.bar()
    for b in range(B):
        make_xh(t, b, 0, 32768)
        xh = []
        for dt in range(DT):
            row = []
            for tc in range(NTC):
                row.append(t.BG[:, dt * 2048 + tc * 512:
                               dt * 2048 + tc * 512 + 512])
            xh.append(row)
        for it in range(43):
            p.ld(lambda q, o=t.BG[:, 32768:34816], i=swg[it]:
                 q.dma_start(out=o, in_=i))
            p.ld(lambda q, o=t.BG[:, 34816:36864], i=swu[it]:
                 q.dma_start(out=o, in_=i))
            p.bar()
            for tc in range(NTC):
                for dt in range(DT):
                    p.pe(lambda q, o=t.PS[0][:, 0:512],
                         l=t.BG[:, 32768 + dt * 128: 32768 + dt * 128 + 128],
                         r=xh[dt][tc], s=(dt == 0), e2=(dt == 15):
                         q.matmul(o, l, r, start=s, stop=e2))
                for dt in range(DT):
                    p.pe(lambda q, o=t.PS[1][:, 0:512],
                         l=t.BG[:, 34816 + dt * 128: 34816 + dt * 128 + 128],
                         r=xh[dt][tc], s=(dt == 0), e2=(dt == 15):
                         q.matmul(o, l, r, start=s, stop=e2))
                p.bar()
                sg = t.BG[:, 36864:37376]
                p.act(lambda q, o=sg: q.activation(out=o, in_=t.PS[0][:, 0:512],
                      func=AF.Silu))
                p.bar()
                htile = t.BG[:, 37376:37888]
                p.dve(lambda q, o=htile, a=sg: q.tensor_tensor(out=o, in0=a,
                      in1=t.PS[1][:, 0:512], op=ALU.mult))
                p.bar()
                p.st(lambda q, o=sh[it, :, b * 2048 + tc * 512:
                                    b * 2048 + (tc + 1) * 512], i=htile:
                     q.dma_start(out=o, in_=i))
                p.bar()
    for ot in range(16):
        load_idx_fp(t, di[ot * 128:(ot + 1) * 128, :], 5504)
        dw = t.BG[:, 0:5504]
        gather_lut(t, t.FS[:, 0:5504], dw, 16384)
        for it in range(43):
            tr_copy(t, dw[:, it * 128:(it + 1) * 128],
                    t.BG[:, 5504 + it * 128: 5504 + it * 128 + 128], it % 2)
        p.bar()
        p.st(lambda q, o=swd[ot], i=t.BG[:, 5504:11008]:
             q.dma_start(out=o, in_=i))
        p.bar()
    y = t.inp['y'] if final else None
    for b in range(B):
        for tc in range(NTC):
            for it in range(43):
                p.ld(lambda q, o=t.BG[:, 16384 + it * 512: 16384 + it * 512 + 512],
                     i=sh[it, :, b * 2048 + tc * 512: b * 2048 + (tc + 1) * 512]:
                     q.dma_start(out=o, in_=i))
            p.bar()
            for ot in range(16):
                p.ld(lambda q, o=t.BG[:, 5504:11008], i=swd[ot]:
                     q.dma_start(out=o, in_=i))
                p.bar()
                for it in range(43):
                    p.pe(lambda q, o=t.PS[2][:, 0:512],
                         l=t.BG[:, 5504 + it * 128: 5504 + it * 128 + 128],
                         r=t.BG[:, 16384 + it * 512: 16384 + it * 512 + 512],
                         s=(it == 0), e2=(it == 42):
                         q.matmul(o, l, r, start=s, stop=e2))
                p.bar()
                xo = t.BG[:, 66016:66528]
                p.ld(lambda q, o=xo,
                     i=t.xfm_src[b, ot, :, tc * 512:(tc + 1) * 512]:
                     q.dma_start(out=o, in_=i))
                p.bar()
                if final:
                    ofm = t.FS[:, 0:512]
                    p.dve(lambda q, o=ofm, a=xo: q.tensor_tensor(out=o, in0=a,
                          in1=t.PS[2][:, 0:512], op=ALU.add))
                    p.bar()
                    for j in range(4):
                        pso = t.PS[3][0:128, 0:128]
                        p.pe(lambda q, o=pso, i=ofm[:, j * 128:(j + 1) * 128]:
                             q.transpose(o, i, t.IDF[:]))
                        p.bar()
                        stg = t.FS[:, 512 + (j % 2) * 128: 512 + (j % 2) * 128 + 128]
                        p.act(lambda q, o=stg, i=pso: q.copy(out=o, in_=i))
                        p.bar()
                        p.st(lambda q, o=y[b, tc * 512 + j * 128:
                                           tc * 512 + (j + 1) * 128,
                                           ot * 128:(ot + 1) * 128], i=stg:
                             q.dma_start(out=o, in_=i))
                        p.bar()
                else:
                    stg = t.BG[:, 66528:67040]
                    p.dve(lambda q, o=stg, a=xo: q.tensor_tensor(out=o, in0=a,
                          in1=t.PS[2][:, 0:512], op=ALU.add))
                    p.bar()
                    p.st(lambda q, o=t.xfm_dst[b, ot, :, tc * 512:(tc + 1) * 512],
                         i=stg: q.dma_start(out=o, in_=i))
                    p.bar()



LAUNCHES = [
    (['l0a', 'l0f', 'l1a', 'l1f'], True, True),
]


def kernel(**inputs):
    ch = host_consts()
    state = None
    out = None
    for parts, first, last in LAUNCHES:
        nc, stack, t = build_program(parts, first, last)
        in_maps = []
        for c in range(8):
            m = {}
            for nm in t.inp.keys():
                if nm in ('y', 'xfm_out'):
                    continue
                if nm == 'xin':
                    m[nm] = np.ascontiguousarray(inputs['L0_attn_x'][2 * c:2 * c + 2])
                elif nm == 'xfm_in':
                    m[nm] = state[c]
                elif nm in ch:
                    m[nm] = ch[nm]
                else:
                    m[nm] = np.ascontiguousarray(inputs[nm])
            in_maps.append(m)
        res = run_bass_kernel_spmd(nc, in_maps, core_ids=list(range(8)))
        if last:
            out = np.concatenate([res.results[c]['y'] for c in range(8)], axis=0)
        else:
            state = [res.results[c]['xfm_out'] for c in range(8)]
        stack.close()
    return out
